# version 10
# baseline (speedup 1.0000x reference)
import math
import os
import numpy as np
import ml_dtypes
import concourse.bass as bass
import concourse.mybir as mybir
from concourse.bass_utils import run_bass_kernel_spmd

F32 = mybir.dt.float32
BF16 = mybir.dt.bfloat16
FP8 = mybir.dt.float8e4
AF = mybir.ActivationFunctionType
ALU = mybir.AluOpType
AX = mybir.AxisListType

NCORES = 8
T, TOWN, E, M, KV, FF = 8192, 1024, 2048, 2048, 512, 8192
HEADS, GROUPS, D = 16, 4, 128
NT, NE, NJ = TOWN // 128, E // 128, FF // 128
SCALE_QK = 1.0 / math.sqrt(D)
EXP_BIAS = -30.0
LN_EPS = 1e-5

N_PHASES = int(os.environ.get("KPHASES", "99"))
DEBUG = os.environ.get("KDEBUG", "0") == "1"


class Sem:
    def __init__(self, nc, name):
        self.h = nc.alloc_semaphore(name)
        self.n = 0

    def inc(self, inst, k=1):
        inst.then_inc(self.h, k)
        self.n += k
        return inst


def _mask_host(core):
    out = np.zeros((NT, 128, 9 * 128), dtype=ml_dtypes.bfloat16)
    kr = np.arange(128)[:, None]
    qc = np.arange(128)[None, :]
    for m in range(NT):
        gm = NT * core + m
        for kt in range(9):
            jt = gm - 8 + kt
            if jt < 0:
                continue
            delta = (8 - kt) * 128
            allowed = (kr <= qc + delta) & (kr >= qc + delta - 1023)
            out[m, :, kt * 128:(kt + 1) * 128] = allowed.astype(ml_dtypes.bfloat16)
    return out


def build_nc():
    nc = bass.Bass("TRN2", target_bir_lowering=False, num_devices=NCORES)

    x_own = nc.dram_tensor("x_own", [TOWN, E], F32, kind="ExternalInput").ap()
    x_halo = nc.dram_tensor("x_halo", [TOWN, E], F32, kind="ExternalInput").ap()
    Wq_d = nc.dram_tensor("Wq", [E, M], F32, kind="ExternalInput").ap()
    Wk_d = nc.dram_tensor("Wk", [E, KV], F32, kind="ExternalInput").ap()
    Wv_d = nc.dram_tensor("Wv", [E, KV], F32, kind="ExternalInput").ap()
    Wo_d = nc.dram_tensor("Wo", [M, M], F32, kind="ExternalInput").ap()
    bo_d = nc.dram_tensor("bo", [1, M], F32, kind="ExternalInput").ap()
    wfc_half = nc.dram_tensor("wfc_half", [FF // 2, E], F32, kind="ExternalInput").ap()
    wproj_half = nc.dram_tensor("wproj_half", [E, FF // 2], F32, kind="ExternalInput").ap()
    wfc_8 = nc.dram_tensor("wfc_8", [FF // 8, E], F32, kind="ExternalInput").ap()
    wproj_8 = nc.dram_tensor("wproj_8", [E, FF // 8], F32, kind="ExternalInput").ap()
    masks_d = nc.dram_tensor("masks", [NT, 128, 9 * 128], BF16, kind="ExternalInput").ap()
    y = nc.dram_tensor("y", [TOWN, E], F32, kind="ExternalOutput").ap()

    h_spill = nc.dram_tensor("h_spill", [NJ, 128, TOWN], F32, kind="Internal").ap()
    wfcT_q = nc.dram_tensor("wfcT_q", [NE, 128, FF], BF16, kind="Internal",
                            addr_space="Shared").ap()
    wprojT_q = nc.dram_tensor("wprojT_q", [NE, 128, 64, 128], BF16, kind="Internal",
                              addr_space="Shared").ap()
    wfcT_loc = nc.dram_tensor("wfcT_loc", [NE, 128, 32, 128], BF16, kind="Internal").ap()
    wprojT_loc = nc.dram_tensor("wprojT_loc", [NE, 128, 32, 128], BF16, kind="Internal").ap()
    cc_w = nc.dram_tensor("cc_w", [1, 16], F32, kind="Internal").ap()
    cc_x = nc.dram_tensor("cc_x", [1, 16], F32, kind="Internal").ap()
    cc_h = nc.dram_tensor("cc_h", [1, 16], F32, kind="Internal").ap()
    rg = [list(range(NCORES))]

    from contextlib import ExitStack
    es = ExitStack()
    esA = ExitStack()
    esB = ExitStack()
    esC = ExitStack()
    esD = ExitStack()
    esE = ExitStack()
    esF = ExitStack()
    SB = lambda stk, name, shape, dt: stk.enter_context(nc.sbuf_tensor(name, shape, dt))

    xres = SB(es, "xres", [128, NT * E], F32)
    ident = SB(es, "ident", [128, 128], BF16)
    identf = SB(es, "identf", [128, 128], F32)
    ones = SB(es, "ones", [128, 128], BF16)
    scl = SB(es, "scl", [128, 8], F32)
    smax = SB(es, "smax", [128, 4], F32)
    srow = SB(es, "srow", [1, 128], F32)
    cns = SB(es, "cns", [128, 2], F32)
    lncol = SB(es, "lncol", [128, 8], F32)
    ccb = SB(es, "ccb", [1, 16], F32)

    psA = es.enter_context(nc.psum_tensor("psA", [128, 2048], F32))
    psB = es.enter_context(nc.psum_tensor("psB", [128, 2048], F32))

    s_ld = Sem(nc, "s_ld")
    s_const = Sem(nc, "s_const")
    s_x0 = Sem(nc, "s_x0")
    s_x1 = Sem(nc, "s_x1")
    s_wkv = Sem(nc, "s_wkv")
    s_wq0 = Sem(nc, "s_wq0")
    s_wq1 = Sem(nc, "s_wq1")
    s_pe = Sem(nc, "s_pe")
    s_act = Sem(nc, "s_act")
    s_dve = Sem(nc, "s_dve")
    s_pl = Sem(nc, "s_pl")
    s_st = Sem(nc, "s_st")
    s_wp = Sem(nc, "s_wp")
    s_wd = Sem(nc, "s_wd")
    s_wt = Sem(nc, "s_wt")
    s_cc = Sem(nc, "s_cc")
    s_mk = Sem(nc, "s_mk")
    s_sp = Sem(nc, "s_sp")
    s_rb = Sem(nc, "s_rb")
    s_wd1 = Sem(nc, "s_wd1")
    s_txs = [[Sem(nc, "s_tx%d_%d" % (p, k)) for k in range(4)] for p in range(2)]
    s_wos = [Sem(nc, "s_wo%d" % k) for k in range(4)]
    s_ccd = Sem(nc, "s_ccd")
    s_rb1 = Sem(nc, "s_rb1")
    s_sp1 = Sem(nc, "s_sp1")
    s_sp2 = Sem(nc, "s_sp2")
    s_sp3 = Sem(nc, "s_sp3")
    s_h0 = Sem(nc, "s_h0")
    s_h1 = Sem(nc, "s_h1")
    shw = [s_h0, s_h1]
    sx = [s_x0, s_x1]
    swq = [s_wq0, s_wq1]
    swd = [s_wd, s_wd1]
    ssp = [s_sp, s_sp1, s_sp2, s_sp3]
    srb = [s_rb, s_rb1]

    identt = nc.inline_tensor(np.eye(128, dtype=ml_dtypes.bfloat16).view(np.uint16), "idbf")
    identtf = nc.inline_tensor(np.eye(128, dtype=np.float32), "idf32")

    dbg = {}
    if DEBUG:
        for nm, shape in [("dbg_a", [128, HEADS * TOWN]), ("dbg_b", [128, GROUPS * 2048]),
                          ("dbg_c", [128, 16 * KV])]:
            dbg[nm] = nc.dram_tensor(nm, shape, BF16, kind="ExternalOutput").ap()
        dbg["dbg_f"] = nc.dram_tensor("dbg_f", [128, NT * E], F32, kind="ExternalOutput").ap()

    with es, nc.Block() as block:
        Qs = SB(esA, "Qs", [128, HEADS * TOWN], BF16)
        Ks = SB(esA, "Ks", [128, GROUPS * 2 * TOWN], BF16)
        Vs = SB(esA, "Vs", [128, 16 * KV], BF16)
        wb = SB(esA, "wb", [128, 2 * 2048], BF16)
        WKV = SB(esB, "WKV", [128, NE * 512], BF16)
        xstg = SB(esB, "xstg", [128, 2 * 1024], BF16)
        xT = SB(esB, "xT", [128, NE * TOWN], BF16)

        WKV3 = WKV[:].rearrange("p (ec c) -> p ec c", c=512)
        xT3 = xT[:].rearrange("p (ec t) -> p ec t", t=TOWN)
        Qs3 = Qs[:].rearrange("p (h t) -> p h t", t=TOWN)
        Ks3 = Ks[:].rearrange("p (g t) -> p g t", t=2 * TOWN)
        Vs3 = Vs[:].rearrange("p (l c) -> p l c", c=KV)
        xres3 = xres[:].rearrange("p (t e) -> p t e", e=E)

        @block.sync
        def _(sync):
            s_ld.inc(sync.dma_start(
                out=xres3, in_=x_own.rearrange("(t p) e -> p t e", p=128)), 16)

        @block.gpsimd
        def _(g):
            s_const.inc(g.dma_start(out=ident[:], in_=identt.ap().bitcast(BF16)), 16)
            s_const.inc(g.dma_start(out=identf[:], in_=identtf.ap()), 16)

        @block.vector
        def _(v):
            s_dve.inc(v.memset(ones[:], 1.0), 1)
            s_dve.inc(v.memset(smax[:], 0.0), 1)
            s_dve.inc(v.memset(cns[:, 0:1], EXP_BIAS), 1)
            s_dve.inc(v.memset(cns[:, 1:2], LN_EPS), 1)
            s_dve.inc(v.memset(srow[:], 0.0), 1)
            s_dve.inc(v.memset(scl[:], 0.0), 1)
            s_dve.inc(v.memset(lncol[:], 0.0), 1)

        def phase_proj(src_dram, half):
            import os
            SUB = int(os.environ.get("KP1SUB", "9"))
            pe0, dve0 = s_pe.n, s_dve.n
            x_tot0 = [s_x0.n, s_x1.n]

            @block.gpsimd
            def _(g):
                g.wait_ge(s_pe.h, pe0)
                s_wkv.inc(g.dma_start(
                    out=WKV3, in_=Wk_d.rearrange("(ec e) k -> e ec k", e=128)), 16)
                for u in range(2 * NT):
                    tt, gi = divmod(u, 2)
                    slot = u % 2
                    if u >= 2:
                        g.wait_ge(s_pe.h, pe0 + u - 1)
                    sx[slot].inc(g.dma_start(
                        out=xstg[:, slot * 1024:(slot + 1) * 1024],
                        in_=src_dram.rearrange("(t p) e -> p t e", p=128)[
                            :, tt, gi * 1024:(gi + 1) * 1024]), 16)
                if SUB >= 3:
                    g.wait_ge(s_pe.h, pe0 + 2 * NT + 8)
                    s_wkv.inc(g.dma_start(
                        out=WKV3, in_=Wv_d.rearrange("(ec e) k -> e ec k", e=128)), 16)

            @block.tensor
            def _(t):
                t.wait_ge(s_const.h, 32)
                t.wait_ge(s_dve.h, dve0)
                for u in range(2 * NT):
                    tt, gi = divmod(u, 2)
                    slot = u % 2
                    t.wait_ge(sx[slot].h, x_tot0[slot] + (u // 2 + 1) * 16)
                    ps = (psA if u % 2 == 0 else psB).bitcast(BF16)
                    if u >= 2:
                        t.wait_ge(s_dve.h, dve0 + u - 1)
                    last = None
                    for j in range(8):
                        last = t.transpose(
                            ps[:, j * 128:(j + 1) * 128],
                            xstg[:, slot * 1024 + j * 128: slot * 1024 + j * 128 + 128],
                            ident[:])
                    s_pe.inc(last, 1)

            @block.vector
            def _(v):
                for u in range(2 * NT):
                    tt, gi = divmod(u, 2)
                    ps = (psA if u % 2 == 0 else psB).bitcast(BF16)
                    v.wait_ge(s_pe.h, pe0 + u + 1)
                    s_dve.inc(v.tensor_copy(
                        xT3[:, gi * 8:(gi + 1) * 8, tt * 128:(tt + 1) * 128],
                        ps[:, 0:1024].rearrange("p (j c) -> p j c", c=128)), 1)

            pe1, dve1 = s_pe.n, s_dve.n

            wk_tot = s_wkv.n - (16 if SUB >= 3 else 0)
            wv_tot = s_wkv.n
            wq_tot0 = [s_wq0.n, s_wq1.n]
            if half == 1:
                @block.gpsimd
                def _(g):
                    for mc in range(HEADS):
                        slot = mc % 2
                        if mc >= 2:
                            g.wait_ge(s_pe.h, pe1 + 16 + mc - 1)
                        swq[slot].inc(g.dma_start(
                            out=wb[:, slot * 2048: slot * 2048 + 2048].rearrange(
                                "p (ec c) -> p ec c", c=128),
                            in_=Wq_d.rearrange("(ec e) m -> e ec m", e=128)[
                                :, :, mc * 128:(mc + 1) * 128]), 16)

            @block.tensor
            def _(t):
                t.wait_ge(s_dve.h, dve1)
                t.wait_ge(s_wkv.h, wk_tot)
                for gidx in range(8 if SUB >= 2 else 0):
                    kd, kq = divmod(gidx, 2)
                    ps = (psA if gidx % 2 == 0 else psB)[:, 0:512]
                    if gidx >= 2:
                        t.wait_ge(s_dve.h, dve1 + gidx - 1)
                    last = None
                    for ec in range(NE):
                        last = t.matmul(ps, WKV3[:, ec, kd * 128:(kd + 1) * 128],
                                        xT3[:, ec, kq * 512:(kq + 1) * 512],
                                        start=(ec == 0), stop=(ec == NE - 1))
                    s_pe.inc(last, 1)
                if SUB >= 3:
                    t.wait_ge(s_wkv.h, wv_tot)
                for tt in range(NT if SUB >= 3 else 0):
                    gidx = (8 if SUB >= 2 else 0) + tt
                    ps = (psA if gidx % 2 == 0 else psB)[:, 0:512]
                    t.wait_ge(s_dve.h, dve1 + gidx - 1)
                    last = None
                    for ec in range(NE):
                        last = t.matmul(ps, xT3[:, ec, tt * 128:(tt + 1) * 128],
                                        WKV3[:, ec, 0:512],
                                        start=(ec == 0), stop=(ec == NE - 1))
                    s_pe.inc(last, 1)
                if half == 1:
                    t.wait_ge(s_dve.h, dve1 + 16)
                    for mc in range(HEADS):
                        slot = mc % 2
                        psq = (psA if mc % 2 == 0 else psB)
                        t.wait_ge(swq[mc % 2].h, wq_tot0[mc % 2] + (mc // 2 + 1) * 16)
                        if mc >= 2:
                            t.wait_ge(s_dve.h, dve1 + 16 + (mc - 2) + 1)
                        wv = wb[:, slot * 2048: slot * 2048 + 2048].rearrange(
                            "p (ec c) -> p ec c", c=128)
                        last = None
                        for ec in range(NE):
                            for nq in range(2):
                                last = t.matmul(
                                    psq[:, nq * 512:(nq + 1) * 512],
                                    wv[:, ec, :],
                                    xT3[:, ec, nq * 512:(nq + 1) * 512],
                                    start=(ec == 0), stop=(ec == NE - 1))
                        s_pe.inc(last, 1)

            @block.vector
            def _(v):
                nk = 8 if SUB >= 2 else 0
                for gidx in range(nk):
                    kd, kq = divmod(gidx, 2)
                    ps = (psA if gidx % 2 == 0 else psB)[:, 0:512]
                    v.wait_ge(s_pe.h, pe1 + gidx + 1)
                    s_dve.inc(v.tensor_copy(
                        Ks3[:, kd, half * TOWN + kq * 512: half * TOWN + (kq + 1) * 512], ps), 1)
                for tt in range(NT if SUB >= 3 else 0):
                    gidx = nk + tt
                    ps = (psA if gidx % 2 == 0 else psB)[:, 0:512]
                    v.wait_ge(s_pe.h, pe1 + gidx + 1)
                    s_dve.inc(v.tensor_copy(Vs3[:, half * 8 + tt, :], ps), 1)
                if half == 1:
                    for mc in range(HEADS):
                        psq = (psA if mc % 2 == 0 else psB)
                        v.wait_ge(s_pe.h, pe1 + 16 + mc + 1)
                        s_dve.inc(v.tensor_copy(Qs3[:, mc, :], psq[:, 0:1024]), 1)

        if N_PHASES >= 1:
            phase_proj(x_halo, 0)
        if N_PHASES >= 2:
            phase_proj(x_own, 1)

        attO = None
        P3T = {}
        if N_PHASES >= 3:
            esB.close()
            attO = SB(esC, "attO", [128, HEADS * TOWN], BF16)
            wpq = SB(esC, "wpq", [128, 8 * 512], BF16)
            wpin = Ks[:].bitcast(F32)[:, 0:3 * 512]
            wpt32 = Qs[:].bitcast(F32)[:, 0:4 * 512]
            wpb8 = Qs[:].bitcast(FP8)[:, 8192:8192 + 4 * 512]
            from contextlib import ExitStack as _ES
            esC2 = _ES()
            mskb = SB(esC2, "mskb", [128, 2 * 1152], BF16)
            PTb = SB(esC2, "PTb", [128, 2 * 1152], BF16)
            Onrm = SB(esC2, "Onrm", [128, 2 * 128], BF16)
            rcp = SB(esC2, "rcp", [128, 2], F32)
            wmx = wpq[:].bitcast(F32)
            attO3 = attO[:].rearrange("p (h t) -> p h t", t=TOWN)
            s_mk0, s_mk1 = Sem(nc, "s_mk0"), Sem(nc, "s_mk1")
            smk = [s_mk0, s_mk1]
            smx = [s_txs[0][0], s_txs[0][1]]
            mx_t0 = [s.n for s in smx]
            pe3, act3, dve3 = s_pe.n, s_act.n, s_dve.n
            ccd0 = s_ccd.n
            PREP3 = N_PHASES >= 5
            KT1, KV1, KT2, KV2, KCH, KBC, KEV = 64, 65, 66, 67, 81, 82, 83

            def peoff(k):
                if not PREP3:
                    return 0
                return (1 if k > KT1 else 0) + (1 if k > KT2 else 0) + \
                       (1 if k > KBC else 0)

            PET1 = pe3 + 3 * (KT1 + 1) + 1
            PET2 = pe3 + 3 * (KT2 + 1) + 2
            PEBC = pe3 + 3 * (KBC + 1) + 3

            def psS(par):
                return (psA if par == 0 else psB)[:, 0:1152]

            def psO(par):
                return (psA if par == 0 else psB)[:, 1536:1664]

            def psSum(par):
                return (psA if par == 0 else psB)[:, 1152:1153]

            def psOT(par):
                return (psA if par == 0 else psB).bitcast(BF16)[:, 3456:3584]

            dve_mask = {}
            dve_recip = {}
            dve_pair = {}

            @block.vector
            def _(v):
                def dr():
                    v.wait_ge(s_dve.h, s_dve.n)
                for m in range(NT):
                    for h in range(HEADS):
                        k = m * HEADS + h
                        par = k % 2
                        if h == 0:
                            v.wait_ge(smk[m % 2].h, (m // 2 + 1) * 16)
                        v.wait_ge(s_act.h, act3 + 2 * k + 1)
                        s_dve.inc(v.tensor_tensor(
                            PTb[:, par * 1152: par * 1152 + 1152],
                            PTb[:, par * 1152: par * 1152 + 1152],
                            mskb[:, (m % 2) * 1152:(m % 2) * 1152 + 1152],
                            ALU.mult), 1)
                        dve_mask[k] = s_dve.n
                        v.wait_ge(s_pe.h, pe3 + 3 * k + peoff(k) + 2)
                        s_dve.inc(v.reciprocal(rcp[:, par:par + 1], psSum(par)), 1)
                        dve_recip[k] = s_dve.n
                        v.wait_ge(s_pe.h, pe3 + 3 * k + peoff(k) + 3)
                        s_dve.inc(v.tensor_copy(
                            attO3[:, h, m * 128:(m + 1) * 128], psOT(par)), 1)
                        if not PREP3:
                            continue
                        if k < 64:
                            i = k
                            v.wait_ge(smx[i % 2].h, mx_t0[i % 2] + (i // 2 + 1) * 16)
                            dr()
                            s_dve.inc(v.reduce_max(
                                lncol[:, 0:1],
                                wmx[:, (i % 2) * 512:(i % 2) * 512 + 512],
                                AX.X, apply_absolute_value=True), 1)
                            dr()
                            s_dve.inc(v.tensor_max(
                                smax[:, 2 + i // 32:3 + i // 32],
                                smax[:, 2 + i // 32:3 + i // 32],
                                lncol[:, 0:1]), 1)
                            dve_pair[i] = s_dve.n
                        elif k == KV1:
                            v.wait_ge(s_pe.h, PET1)
                            dr()
                            s_dve.inc(v.reduce_max(lncol[0:2, 0:1],
                                                   psA[0:2, 1280:1408], AX.X), 1)
                            P3T["v1"] = s_dve.n
                        elif k == KV2:
                            v.wait_ge(s_pe.h, PET2 + 2)
                            dr()
                            s_dve.inc(v.tensor_copy(srow[0:1, 4:6],
                                                    psB[0:1, 1280:1282]), 1)
                            P3T["v2"] = s_dve.n
                        elif k == KCH:
                            v.wait_ge(s_ccd.h, ccd0 + 32)
                            for i2 in range(2):
                                dr()
                                s_dve.inc(v.tensor_scalar(
                                    srow[0:1, 32 + 2 * i2:33 + 2 * i2],
                                    srow[0:1, 20 + i2:21 + i2],
                                    1.0 / 448.0, 1e-6, ALU.mult, ALU.max), 1)
                                dr()
                                s_dve.inc(v.reciprocal(
                                    srow[0:1, 32 + 2 * i2:33 + 2 * i2],
                                    srow[0:1, 32 + 2 * i2:33 + 2 * i2]), 1)
                                dr()
                                s_dve.inc(v.tensor_scalar(
                                    srow[0:1, 33 + 2 * i2:34 + 2 * i2],
                                    srow[0:1, 32 + 2 * i2:33 + 2 * i2],
                                    0.5, None, ALU.mult), 1)
                            P3T["chain"] = s_dve.n
                        elif k == KEV:
                            v.wait_ge(s_pe.h, PEBC + 2)
                            dr()
                            s_dve.inc(v.tensor_copy(scl[:, 4:8],
                                                    psB[:, 1280:1284]), 1)
                            P3T["scl"] = s_dve.n

            if PREP3:
                @block.gpsimd
                def _(g):
                    g.wait_ge(s_pe.h, pe3)
                    for i in range(64):
                        wi, ch = divmod(i, 32)
                        w8 = wfc_8 if wi == 0 else wproj_8
                        cn = w8.shape[1] // 512
                        a, c2 = divmod(ch, cn)
                        if i >= 2:
                            g.wait_ge(s_dve.h, dve_pair[i - 2])
                        smx[i % 2].inc(g.dma_start(
                            out=wmx[:, (i % 2) * 512:(i % 2) * 512 + 512],
                            in_=w8.rearrange("(a p) (c q) -> p a c q",
                                             p=128, q=512)[:, a, c2, :]), 16)
                    g.wait_ge(s_dve.h, P3T["v2"])
                    s_ccd.inc(g.dma_start(out=cc_w, in_=srow[0:1, 0:16]), 16)
                    g.wait_ge(s_ccd.h, ccd0 + 16)
                    s_cc.inc(g.collective_compute(
                        "AllReduce", ALU.max, rg, [cc_w], [cc_w],
                        cc_dim="Partition", unique_tensors="No"), 1)
                    g.wait_ge(s_cc.h, s_cc.n)
                    s_ccd.inc(g.dma_start(out=srow[0:1, 16:32], in_=cc_w), 16)

            @block.sync
            def _(sync):
                sync.wait_ge(s_pe.h, pe3)
                for m in range(NT):
                    if m >= 2:
                        klast = (m - 2) * HEADS + (HEADS - 1)
                        sync.wait_ge(s_dve.h, dve_mask[klast])
                    smk[m % 2].inc(sync.dma_start(
                        out=mskb[:, (m % 2) * 1152:(m % 2) * 1152 + 1152],
                        in_=masks_d[m]), 16)

            @block.tensor
            def _(t):
                t.wait_ge(s_dve.h, dve3)
                for m in range(NT):
                    for h in range(HEADS):
                        k = m * HEADS + h
                        par = k % 2
                        g = h // 4
                        if k >= 2:
                            t.wait_ge(s_act.h, act3 + 2 * (k - 2) + 1)
                            t.wait_ge(s_dve.h, dve_recip[k - 2])
                        if PREP3 and k == KT1 + 2:
                            t.wait_ge(s_dve.h, P3T["v1"])
                        elif PREP3 and k == KV2 + 2:
                            t.wait_ge(s_dve.h, P3T["v2"])
                        elif PREP3 and k == KEV + 2:
                            t.wait_ge(s_dve.h, P3T["scl"])
                        last = None
                        for kt in range(9):
                            last = t.matmul(
                                psS(par)[:, kt * 128:(kt + 1) * 128],
                                Ks3[:, g, (m + kt) * 128:(m + kt + 1) * 128],
                                Qs3[:, h, m * 128:(m + 1) * 128],
                                start=True, stop=True)
                        s_pe.inc(last, 1)
                        t.wait_ge(s_dve.h, dve_mask[k])
                        for kt in range(9):
                            pt = PTb[:, par * 1152 + kt * 128: par * 1152 + (kt + 1) * 128]
                            t.matmul(psO(par), pt,
                                     Vs3[:, m + kt, g * 128:(g + 1) * 128],
                                     start=(kt == 0), stop=(kt == 8))
                            last = t.matmul(psSum(par), pt, ones[:, 0:1],
                                            start=(kt == 0), stop=(kt == 8))
                        s_pe.inc(last, 1)
                        t.wait_ge(s_act.h, act3 + 2 * k + 2)
                        s_pe.inc(t.transpose(
                            psOT(par), Onrm[:, par * 128:(par + 1) * 128], ident[:]), 1)
                        if PREP3:
                            if k == KT1:
                                t.wait_ge(s_dve.h, dve_pair[63])
                                t.wait_ge(s_dve.h, dve_recip[KT1])
                                s_pe.inc(t.transpose(psA[0:2, 1280:1408],
                                                     smax[:, 2:4], identf[:]), 1)
                            elif k == KT2:
                                t.wait_ge(s_dve.h, P3T["v1"])
                                t.wait_ge(s_dve.h, dve_recip[KT2 - 1])
                                s_pe.inc(t.transpose(psB[0:1, 1280:1282],
                                                     lncol[0:2, 0:1],
                                                     identf[0:2, 0:2]), 1)
                            elif k == KBC:
                                t.wait_ge(s_dve.h, P3T["chain"])
                                s_pe.inc(t.matmul(
                                    psB[:, 1280:1284],
                                    identf[0:1, 0:1].broadcast_to((1, 128)),
                                    srow[0:1, 32:36],
                                    start=True, stop=True), 1)

            @block.scalar
            def _(sc):
                for m in range(NT):
                    for h in range(HEADS):
                        k = m * HEADS + h
                        par = k % 2
                        sc.wait_ge(s_pe.h, pe3 + 3 * k + peoff(k) + 1)
                        s_act.inc(sc.activation(
                            PTb[:, par * 1152: par * 1152 + 1152],
                            psS(par), AF.Exp, bias=cns[:, 0:1], scale=SCALE_QK), 1)
                        sc.wait_ge(s_dve.h, dve_recip[k])
                        s_act.inc(sc.activation(
                            Onrm[:, par * 128:(par + 1) * 128], psO(par),
                            AF.Copy, scale=rcp[:, par:par + 1]), 1)

            esC2.close()

        if N_PHASES >= 4:
            o_sb = SB(esD, "o_sb", [128, 2 * 1024], F32)
            wstg = Vs[:, 0:2 * 1024]
            wpu8 = Vs[:].bitcast(FP8)[:, 8192:8192 + 2 * 512]
            wpcl = Vs[:].bitcast(FP8)[:, 9216:9216 + 2 * 512]
            wpd = Vs[:, 5120:5120 + 512]
            PREP4 = N_PHASES >= 5
            SL = 8
            pe4, act4, dve4 = s_pe.n, s_act.n, s_dve.n
            wo_t0 = [s_wq0.n, s_wq1.n]
            sldw = s_txs[1]
            sld4_0 = [s.n for s in sldw]
            spub4 = [s_wos[0], s_wos[2]]
            pub4_0 = [s.n for s in spub4]

            peW = {}; peT = {}; peSB = {}
            c = pe4 + 4
            for mo in range(16):
                c += 1; peW[mo] = c
                for b in range(2):
                    c += 1; peT[(mo, b)] = c
                if PREP4 and mo >= 1:
                    for b in range(4):
                        c += 1; peSB[4 * (mo - 1) + b] = c
            if PREP4:
                for b in range(4):
                    c += 1; peSB[60 + b] = c
            dveE = {}; dveA = {}; dveEV = {}; dveU = {}; dveQ = {}
            c = dve4 + 32
            for mo in range(16):
                c += 1; dveE[mo] = c
                for b in range(2):
                    c += 1; dveA[(mo, b)] = c
                if PREP4:
                    if mo >= 1:
                        for b in range(4):
                            c += 1; dveEV[4 * (mo - 1) + b] = c
                    for i in range(SL):
                        s = mo * SL + i
                        c += 1; dveU[s] = c
                        c += 3; dveQ[s] = c
            if PREP4:
                for b in range(4):
                    c += 1; dveEV[60 + b] = c
            actT = {}; actB = {}
            c = act4
            if PREP4:
                for mo in range(16):
                    for i in range(SL):
                        s = mo * SL + i
                        c += 1; actT[s] = c
                        c += 1; actB[s] = c

            def pubSB(g):
                return pub4_0[g % 2] + 32 * (g // 2)

            PSLB = [(psB, 3072), (psA, 3072)]

            def PSL(sl):
                pt, off = PSLB[sl]
                return pt.bitcast(BF16)[:, off:off + 1024]

            @block.sync
            def _(sync):
                sync.wait_ge(s_pe.h, pe4)
                sync.wait_ge(s_dve.h, dve4)
                sync.wait_ge(s_act.h, act4)
                s_ld.inc(sync.dma_start(out=o_sb[0:1, 0:E], in_=bo_d), 16)
                if PREP4:
                    wfc_r = wfc_half.rearrange("(a p) (c q) -> p a c q",
                                               p=128, q=512)
                    par32 = nc.snap(nc.partition_id([mybir.EngineType.SP]) % 2 * 32,
                                    engines=[mybir.EngineType.SP],
                                    min_val=0, max_val=32)

                    def wfc_copy(a4):
                        sync.wait_ge(spub4[0].h, spub4[0].n)
                        sync.wait_ge(spub4[1].h, spub4[1].n)
                        sync.wait_ge(s_wt.h, s_wt.n)
                        s_wt.inc(sync.dma_start(
                            out=wfcT_q.rearrange("a p (t j) -> a p t j", j=128)[
                                a4 * 4:(a4 + 1) * 4, :, bass.ds(par32, 32), :],
                            in_=wfcT_loc[a4 * 4:(a4 + 1) * 4]), 16)

                    def pubs(mo_sl):
                        for b in range(4):
                            g = 4 * mo_sl + b
                            st = 2 * mo_sl + b // 2
                            cq, r0 = divmod(st * 4, 32)
                            sync.wait_ge(s_dve.h, dveEV[g])
                            for t2 in range(2):
                                dc = cq * 4 + 2 * (b % 2) + t2
                                spub4[g % 2].inc(sync.dma_start(
                                    out=wfcT_loc[dc][:, r0:r0 + 4, :],
                                    in_=wstg[:, (g % 2) * 1024 + t2 * 512:
                                             (g % 2) * 1024 + (t2 + 1) * 512].rearrange(
                                        "p (i j) -> p i j", j=128)), 16)

                    for mo in range(16):
                        if mo >= 1:
                            pubs(mo - 1)
                            if mo % 4 == 0:
                                wfc_copy(mo // 4 - 1)
                        for i in range(SL):
                            s = mo * SL + i
                            cq, rb = divmod(s, 32)
                            if s >= 3:
                                sync.wait_ge(s_act.h, actT[s - 3])
                            sldw[s % 3].inc(sync.dma_start(
                                out=wpin[:, (s % 3) * 512:(s % 3 + 1) * 512],
                                in_=wfc_r[:, rb, cq, :]), 16)
                    pubs(15)
                    wfc_copy(3)

            @block.tensor
            def _(t):
                t.wait_ge(s_ld.h, s_ld.n)
                t.wait_ge(s_dve.h, dve4)
                for j in range(4):
                    if j >= 1:
                        t.wait_ge(s_dve.h, dve4 + 8 * j)
                    s_pe.inc(t.matmul(psA[:, 1024:1536],
                                      identf[0:1, 0:1].broadcast_to((1, 128)),
                                      o_sb[0:1, j * 512:(j + 1) * 512],
                                      start=True, stop=True), 1)

            @block.vector
            def _(v):
                for j in range(4):
                    v.wait_ge(s_pe.h, pe4 + j + 1)
                    for tt in range(NT):
                        s_dve.inc(v.tensor_tensor(
                            xres3[:, tt, j * 512:(j + 1) * 512],
                            xres3[:, tt, j * 512:(j + 1) * 512],
                            psA[:, 1024:1536], ALU.add), 1)

            @block.gpsimd
            def _(g):
                g.wait_ge(s_pe.h, pe4)
                for mo in range(HEADS):
                    slot = mo % 2
                    if mo >= 2:
                        g.wait_ge(s_pe.h, peW[mo - 2])
                    swq[slot].inc(g.dma_start(
                        out=wb[:, slot * 2048: slot * 2048 + 2048].rearrange(
                            "p (mi c) -> p mi c", c=128),
                        in_=Wo_d.rearrange("(mi e) m -> e mi m", e=128)[
                            :, :, mo * 128:(mo + 1) * 128]), 16)

            def psT(slot):
                return (psA if slot == 0 else psB)[:, 1024:1536]

            def emit_sb(t, mo_sl):
                for b in range(4):
                    g = 4 * mo_sl + b
                    st = 2 * mo_sl + b // 2
                    s0 = st * 4
                    t.wait_ge(s_dve.h, dveQ[s0 + 3])
                    if g >= 2:
                        t.wait_ge(s_dve.h, dveEV[g - 2])
                    sl = PSL(g % 2)
                    last = None
                    for t2 in range(2):
                        tq = 2 * (b % 2) + t2
                        for i in range(4):
                            s = s0 + i
                            last = t.transpose(
                                sl[:, (t2 * 4 + i) * 128:(t2 * 4 + i + 1) * 128],
                                wpq[:, (s % 8) * 512 + tq * 128:
                                    (s % 8) * 512 + (tq + 1) * 128],
                                ident[:])
                    s_pe.inc(last, 1)

            @block.tensor
            def _(t):
                for mo in range(HEADS):
                    par = mo % 2
                    psW = (psA if par == 0 else psB)[:, 0:1024]
                    t.wait_ge(swq[par].h, wo_t0[par] + (mo // 2 + 1) * 16)
                    if mo >= 2:
                        t.wait_ge(s_dve.h, dveE[mo - 2])
                    last = None
                    for mi in range(HEADS):
                        for nq in range(2):
                            last = t.matmul(
                                psW[:, nq * 512:(nq + 1) * 512],
                                wb[:, par * 2048 + mi * 128: par * 2048 + (mi + 1) * 128],
                                attO3[:, mi, nq * 512:(nq + 1) * 512],
                                start=(mi == 0), stop=(mi == HEADS - 1))
                    s_pe.inc(last, 1)
                    t.wait_ge(s_dve.h, dveE[mo])
                    for b in range(2):
                        if mo >= 1:
                            t.wait_ge(s_dve.h, dveA[(mo - 1, b)])
                        last = None
                        for i in range(4):
                            tt = b * 4 + i
                            last = t.transpose(
                                psT(b)[:, i * 128:(i + 1) * 128],
                                o_sb[:, par * 1024 + tt * 128: par * 1024 + tt * 128 + 128],
                                identf[:])
                        s_pe.inc(last, 1)
                    if PREP4 and mo >= 1:
                        emit_sb(t, mo - 1)
                if PREP4:
                    emit_sb(t, 15)

            @block.vector
            def _(v):
                v.wait_ge(s_dve.h, dve4 + 32)
                for mo in range(HEADS):
                    par = mo % 2
                    psW = (psA if par == 0 else psB)[:, 0:1024]
                    v.wait_ge(s_pe.h, peW[mo])
                    s_dve.inc(v.tensor_copy(o_sb[:, par * 1024:(par + 1) * 1024], psW), 1)
                    for b in range(2):
                        v.wait_ge(s_pe.h, peT[(mo, b)])
                        s_dve.inc(v.tensor_tensor(
                            xres3[:, b * 4:(b + 1) * 4, mo * 128:(mo + 1) * 128],
                            xres3[:, b * 4:(b + 1) * 4, mo * 128:(mo + 1) * 128],
                            psT(b).rearrange("p (t e) -> p t e", e=128), ALU.add), 1)
                    if PREP4:
                        if mo >= 1:
                            for b in range(4):
                                g = 4 * (mo - 1) + b
                                v.wait_ge(s_pe.h, peSB[g])
                                if g >= 2:
                                    v.wait_ge(spub4[g % 2].h, pubSB(g))
                                v.wait_ge(s_dve.h, s_dve.n)
                                s_dve.inc(v.tensor_copy(
                                    wstg[:, (g % 2) * 1024:(g % 2 + 1) * 1024],
                                    PSL(g % 2)), 1)
                        for i in range(SL):
                            s = mo * SL + i
                            v.wait_ge(s_act.h, actB[s])
                            if s >= 8:
                                v.wait_ge(s_pe.h, peSB[2 * ((s - 8) // 4) + 1])
                            s_dve.inc(v.tensor_scalar(
                                wpu8[:, (s % 2) * 512:(s % 2 + 1) * 512],
                                wpt32[:, (s % 4) * 512:(s % 4 + 1) * 512],
                                240.0, -240.0, ALU.min, ALU.max), 1)
                            s_dve.inc(v.tensor_scalar(
                                wpcl[:, (s % 2) * 512:(s % 2 + 1) * 512],
                                wpb8[:, (s % 4) * 512:(s % 4 + 1) * 512],
                                120.0, -120.0, ALU.min, ALU.max), 1)
                            v.wait_ge(s_dve.h, s_dve.n)
                            s_dve.inc(v.tensor_tensor(
                                wpd[:, 0:512], wpb8[:, (s % 4) * 512:(s % 4 + 1) * 512],
                                wpcl[:, (s % 2) * 512:(s % 2 + 1) * 512], ALU.subtract), 1)
                            v.wait_ge(s_dve.h, s_dve.n)
                            s_dve.inc(v.scalar_tensor_tensor(
                                wpq[:, (s % 8) * 512:(s % 8 + 1) * 512],
                                wpd[:, 0:512], 2.0,
                                wpu8[:, (s % 2) * 512:(s % 2 + 1) * 512],
                                ALU.mult, ALU.add), 1)
                if PREP4:
                    for b in range(4):
                        g = 60 + b
                        v.wait_ge(s_pe.h, peSB[g])
                        v.wait_ge(spub4[g % 2].h, pubSB(g))
                        v.wait_ge(s_dve.h, s_dve.n)
                        s_dve.inc(v.tensor_copy(
                            wstg[:, (g % 2) * 1024:(g % 2 + 1) * 1024],
                            PSL(g % 2)), 1)

            if PREP4:
                @block.scalar
                def _(sc):
                    for mo in range(16):
                        for i in range(SL):
                            s = mo * SL + i
                            sc.wait_ge(sldw[s % 3].h,
                                       sld4_0[s % 3] + (s // 3 + 1) * 16)
                            if s == 0:
                                sc.wait_ge(s_dve.h, P3T["scl"])
                            if s >= 4:
                                sc.wait_ge(s_dve.h, dveU[s - 4])
                            s_act.inc(sc.activation(
                                wpt32[:, (s % 4) * 512:(s % 4 + 1) * 512],
                                wpin[:, (s % 3) * 512:(s % 3 + 1) * 512],
                                AF.Identity, bias=0.0, scale=scl[:, 4:5]), 1)
                            if s >= 4:
                                sc.wait_ge(s_dve.h, dveU[s - 4] + 2)
                            sc.wait_ge(s_act.h, s_act.n)
                            s_act.inc(sc.activation(
                                wpb8[:, (s % 4) * 512:(s % 4 + 1) * 512],
                                wpt32[:, (s % 4) * 512:(s % 4 + 1) * 512],
                                AF.Copy, scale=0.5), 1)

            actL0, dveL0 = s_act.n, s_dve.n
            INV = 1.0 / E

            @block.scalar
            def _(sc):
                sc.wait_ge(s_dve.h, dveA[(15, 1)])
                for tt in range(NT):
                    s_act.inc(sc.activation(o_sb[:, 0:E], xres3[:, tt, :], AF.Square,
                                            accum_out=lncol[:, 0:1]), 1)
                    sc.wait_ge(s_dve.h, dveL0 + 9 * tt + 4)
                    s_act.inc(sc.activation(lncol[:, 5:6], lncol[:, 4:5],
                                            AF.Sqrt, bias=cns[:, 1:2]), 1)

            @block.vector
            def _(v):
                def dr():
                    v.wait_ge(s_dve.h, s_dve.n)
                for tt in range(NT):
                    v.wait_ge(s_act.h, actL0 + 2 * tt + 1)
                    dr()
                    s_dve.inc(v.reduce_sum(lncol[:, 1:2], xres3[:, tt, :], AX.X), 1)
                    dr()
                    s_dve.inc(v.tensor_scalar(lncol[:, 2:3], lncol[:, 1:2], INV,
                                              None, ALU.mult), 1)
                    dr()
                    s_dve.inc(v.tensor_tensor(lncol[:, 3:4], lncol[:, 2:3],
                                              lncol[:, 2:3], ALU.mult), 1)
                    dr()
                    s_dve.inc(v.scalar_tensor_tensor(lncol[:, 4:5], lncol[:, 0:1], INV,
                                                     lncol[:, 3:4], ALU.mult,
                                                     ALU.subtract), 1)
                    v.wait_ge(s_act.h, actL0 + 2 * tt + 2)
                    s_dve.inc(v.reciprocal(lncol[:, 6:7], lncol[:, 5:6]), 1)
                    dr()
                    s_dve.inc(v.tensor_scalar(lncol[:, 7:8], lncol[:, 2:3],
                                              lncol[:, 6:7], -1.0, ALU.mult,
                                              ALU.mult), 1)
                    dr()
                    s_dve.inc(v.tensor_scalar(xres3[:, tt, :], xres3[:, tt, :],
                                              lncol[:, 6:7], lncol[:, 7:8],
                                              ALU.mult, ALU.add), 1)
                    dr()
                    s_dve.inc(v.reduce_max(lncol[:, 3:4], xres3[:, tt, :], AX.X,
                                           apply_absolute_value=True), 1)
                    dr()
                    s_dve.inc(v.tensor_max(smax[:, 0:1], smax[:, 0:1],
                                           lncol[:, 3:4]), 1)


        if N_PHASES >= 5:
            def row_from_col(cols_lo, cols_n, dst_lo):
                @block.tensor
                def _(t):
                    t.wait_ge(s_dve.h, s_dve.n)
                    s_pe.inc(t.transpose(psA[0:cols_n, 0:128],
                                         smax[:, cols_lo:cols_lo + cols_n], identf[:]), 1)
                @block.vector
                def _(v):
                    v.wait_ge(s_pe.h, s_pe.n)
                    s_dve.inc(v.reduce_max(lncol[0:cols_n, 0:1],
                                           psA[0:cols_n, 0:128], AX.X), 1)
                @block.tensor
                def _(t):
                    t.wait_ge(s_dve.h, s_dve.n)
                    s_pe.inc(t.transpose(psB[0:1, 0:cols_n],
                                         lncol[0:cols_n, 0:1],
                                         identf[0:cols_n, 0:cols_n]), 1)
                @block.vector
                def _(v):
                    v.wait_ge(s_pe.h, s_pe.n)
                    v.wait_ge(s_dve.h, s_dve.n)
                    s_dve.inc(v.tensor_copy(srow[0:1, dst_lo:dst_lo + cols_n],
                                            psB[0:1, 0:cols_n]), 1)

            def scales_from_row(src_lo, n, scl_lo, cc, pre=None):
                @block.gpsimd
                def _(g):
                    if pre is not None:
                        pre(g)
                    g.wait_ge(s_dve.h, s_dve.n)
                    s_ccd.inc(g.dma_start(out=cc, in_=srow[0:1, 0:16]), 16)
                    g.wait_ge(s_ccd.h, s_ccd.n)
                    s_cc.inc(g.collective_compute(
                        "AllReduce", ALU.max, rg, [cc], [cc],
                        cc_dim="Partition", unique_tensors="No"), 1)
                    g.wait_ge(s_cc.h, s_cc.n)
                    s_ccd.inc(g.dma_start(out=srow[0:1, 16:32], in_=cc), 16)
                @block.vector
                def _(v):
                    v.wait_ge(s_ccd.h, s_ccd.n)
                    for i in range(n):
                        v.wait_ge(s_dve.h, s_dve.n)
                        s_dve.inc(v.tensor_scalar(srow[0:1, 32 + 2 * i:33 + 2 * i],
                                                  srow[0:1, 16 + src_lo + i:17 + src_lo + i],
                                                  1.0 / 448.0, 1e-6, ALU.mult, ALU.max), 1)
                        v.wait_ge(s_dve.h, s_dve.n)
                        s_dve.inc(v.reciprocal(srow[0:1, 32 + 2 * i:33 + 2 * i],
                                               srow[0:1, 32 + 2 * i:33 + 2 * i]), 1)
                        v.wait_ge(s_dve.h, s_dve.n)
                        s_dve.inc(v.tensor_scalar(srow[0:1, 33 + 2 * i:34 + 2 * i],
                                                  srow[0:1, 32 + 2 * i:33 + 2 * i],
                                                  0.5, None, ALU.mult), 1)
                @block.tensor
                def _(t):
                    t.wait_ge(s_dve.h, s_dve.n)
                    s_pe.inc(t.matmul(psB[:, 256:256 + 2 * n],
                                      identf[0:1, 0:1].broadcast_to((1, 128)),
                                      srow[0:1, 32:32 + 2 * n],
                                      start=True, stop=True), 1)
                @block.vector
                def _(v):
                    v.wait_ge(s_pe.h, s_pe.n)
                    s_dve.inc(v.tensor_copy(scl[:, scl_lo:scl_lo + 2 * n],
                                            psB[:, 256:256 + 2 * n]), 1)

        if N_PHASES >= 5:
            esD.close()
            esC.close()
            esA.close()
            x8tok = SB(esE, "x8tok", [128, NT * E], BF16)
            x8f = SB(esE, "x8f", [128, NE * TOWN], BF16)
            xq32 = SB(esE, "xq32", [128, E], F32)
            q8u = SB(esE, "q8u", [128, E], FP8)
            q8p = SB(esE, "q8p", [128, E], FP8)
            q8n = SB(esE, "q8n", [128, E], FP8)
            s1f = SB(esE, "s1f", [128, E], F32)
            wfl = SB(esE, "wfl", [128, 2 * 16 * 512], BF16)
            rl = SB(esE, "rl", [128, 2 * 1024], F32)
            h_sb = SB(esE, "h_sb", [128, 4 * 1024], F32)
            x8t3 = x8tok[:].rearrange("p (t e) -> p t e", e=E)
            x8f3 = x8f[:].rearrange("p (ec t) -> p ec t", t=TOWN)
            wfl3 = wfl[:].rearrange("p (s a j) -> p s a j", s=2, a=NE)

            row_from_col(0, 1, 0)
            scales_from_row(0, 1, 0, cc_x,
                            pre=lambda g: g.wait_ge(s_wt.h, s_wt.n))

            @block.vector
            def _(v):
                v.wait_ge(s_dve.h, s_dve.n)
                s_dve.inc(v.tensor_tensor(scl[:, 5:6], scl[:, 0:1], scl[:, 4:5],
                                          ALU.mult), 1)
                v.wait_ge(s_dve.h, s_dve.n)
                s_dve.inc(v.reciprocal(scl[:, 5:6], scl[:, 5:6]), 1)

            act6, dve6 = s_act.n, s_dve.n

            @block.scalar
            def _(sc):
                sc.wait_ge(s_dve.h, dve6)
                for tt in range(NT):
                    if tt >= 1:
                        sc.wait_ge(s_dve.h, dve6 + 5 * tt)
                    sc.wait_ge(s_act.h, s_act.n)
                    s_act.inc(sc.activation(xq32[:], xres3[:, tt, :], AF.Identity,
                                            bias=0.0, scale=scl[:, 0:1]), 1)

            @block.vector
            def _(v):
                v.wait_ge(s_dve.h, dve6)
                for tt in range(NT):
                    v.wait_ge(s_act.h, act6 + tt + 1)
                    s_dve.inc(v.tensor_scalar(q8u[:], xq32[:], 240.0, -240.0,
                                              ALU.min, ALU.max), 1)
                    s_dve.inc(v.tensor_scalar(q8p[:], xq32[:], 0.5, 120.0,
                                              ALU.mult, ALU.max), 1)
                    s_dve.inc(v.tensor_scalar(q8n[:], xq32[:], 0.5, -120.0,
                                              ALU.mult, ALU.min), 1)
                    v.wait_ge(s_dve.h, s_dve.n)
                    s_dve.inc(v.tensor_tensor(s1f[:], q8p[:], q8n[:], ALU.add), 1)
                    v.wait_ge(s_dve.h, s_dve.n)
                    s_dve.inc(v.scalar_tensor_tensor(x8t3[:, tt, :], s1f[:], 2.0,
                                                     q8u[:], ALU.mult, ALU.add), 1)

            pe6, dve6b = s_pe.n, s_dve.n

            @block.tensor
            def _(t):
                t.wait_ge(s_dve.h, dve6b)
                for u in range(2 * NT):
                    tt, gi = divmod(u, 2)
                    ps = (psA if u % 2 == 0 else psB).bitcast(BF16)
                    if u >= 2:
                        t.wait_ge(s_dve.h, dve6b + u - 1)
                    last = None
                    for j in range(8):
                        ec = gi * 8 + j
                        last = t.transpose(ps[:, j * 128:(j + 1) * 128],
                                           x8t3[:, tt, ec * 128:(ec + 1) * 128],
                                           ident[:])
                    s_pe.inc(last, 1)

            @block.vector
            def _(v):
                for u in range(2 * NT):
                    tt, gi = divmod(u, 2)
                    ps = (psA if u % 2 == 0 else psB).bitcast(BF16)
                    v.wait_ge(s_pe.h, pe6 + u + 1)
                    s_dve.inc(v.tensor_copy(
                        x8f3[:, gi * 8:(gi + 1) * 8, tt * 128:(tt + 1) * 128],
                        ps[:, 0:1024].rearrange("p (j c) -> p j c", c=128)), 1)

        h_spill3 = h_spill.rearrange("a p t -> a p t")

        MG = {}
        MSLB = [(0, 3072), (1, 3072), (0, 3584), (1, 3584)]

        def MSL(sl):
            pt, off = MSLB[sl]
            return (psA if pt == 0 else psB).bitcast(BF16)[:, off:off + 512]

        def PSH(sl):
            pt, off = [(0, 0), (1, 0), (0, 512), (1, 512)][sl]
            return (psA if pt == 0 else psB)[:, off:off + 512]

        def phase_mm1(hf):
            pe7, act7, dve7 = s_pe.n, s_act.n, s_dve.n
            wq_t0 = [s_h0.n, s_h1.n]
            sp_t0 = [s.n for s in ssp]
            t0 = hf * 512
            sldm = [s_txs[0][2], s_txs[0][3], smk[0], smk[1]]
            if hf == 0:
                MG["ld0"] = [s.n for s in sldm]
                MG["pub0"] = [s_wos[1].n, s_wos[3].n]
                MG["t"] = {}; MG["b"] = {}; MG["u"] = {}; MG["q"] = {}
                MG["tr"] = {}; MG["ev"] = {}
            mt, mb, mu, mq = MG["t"], MG["b"], MG["u"], MG["q"]
            mtr, mev = MG["tr"], MG["ev"]
            wpq6 = x8tok[:, 512:512 + 8 * 512]
            wd6 = x8tok[:, 0:512]
            mstg = x8tok[:, 4608:4608 + 2 * 512]

            pe_mm = {}; act_rl = {}
            c = pe7
            for jc in range(NJ):
                c += 1; pe_mm[jc] = c
                s_t = hf * 64 + jc - 4
                if s_t >= 0:
                    c += 1; mtr[s_t] = c
            if hf == 1:
                for s_t in range(124, 128):
                    c += 1; mtr[s_t] = c
            dve_rm = {}
            c = dve7
            for jc in range(NJ):
                s = hf * 64 + jc
                c += 1; dve_rm[jc] = c
                c += 1
                s_t = hf * 64 + jc - 4
                if s_t >= 0:
                    c += 1; mev[s_t] = c
                c += 1; mu[s] = c
                c += 3; mq[s] = c
            if hf == 1:
                for s_t in range(124, 128):
                    c += 1; mev[s_t] = c
            act_sq = {}
            c = act7
            for jc in range(NJ):
                s = hf * 64 + jc
                c += 1; act_rl[jc] = c
                c += 1; act_sq[jc] = c
                c += 1; mt[s] = c
                c += 1; mb[s] = c

            spubM = [s_wos[1], s_wos[3]]

            def mpub(s_t):
                return MG["pub0"][s_t % 2] + 16 * (s_t // 2)

            @block.sync
            def _(sync):
                sync.wait_ge(s_cc.h, s_cc.n)
                wpr = wproj_half.rearrange("(a p) (c q) -> p a c q", p=128, q=512)
                par32b = nc.snap(nc.partition_id([mybir.EngineType.SP]) % 2 * 32,
                                 engines=[mybir.EngineType.SP],
                                 min_val=0, max_val=32)

                def wproj_copy(a4):
                    sync.wait_ge(spubM[0].h, spubM[0].n)
                    sync.wait_ge(spubM[1].h, spubM[1].n)
                    sync.wait_ge(s_wt.h, s_wt.n)
                    s_wt.inc(sync.dma_start(
                        out=wprojT_q[a4 * 4:(a4 + 1) * 4, :,
                                     bass.ds(par32b, 32), :],
                        in_=wprojT_loc[a4 * 4:(a4 + 1) * 4]), 16)

                def pub(s_t):
                    ebp, jqp = divmod(s_t, 8)
                    sync.wait_ge(s_dve.h, mev[s_t])
                    spubM[s_t % 2].inc(sync.dma_start(
                        out=wprojT_loc[ebp][:, jqp * 4:(jqp + 1) * 4, :],
                        in_=mstg[:, (s_t % 2) * 512:(s_t % 2 + 1) * 512].rearrange(
                            "p (i j) -> p i j", j=128)), 16)

                for jc in range(NJ):
                    s = hf * 64 + jc
                    if jc % 4 == 0:
                        jq4 = jc // 4
                        slot = jq4 % 2
                        if jq4 >= 2:
                            sync.wait_ge(s_pe.h, pe_mm[4 * jq4 - 5])
                        else:
                            sync.wait_ge(s_pe.h, pe7)
                        shw[slot].inc(sync.dma_start(
                            out=wfl3[:, slot, :, :],
                            in_=wfcT_q[:, :, jq4 * 512:(jq4 + 1) * 512].rearrange(
                                "a p j -> p a j")), 16)
                    if s >= 4:
                        sync.wait_ge(s_act.h, mt[s - 4])
                    eb, jq = divmod(s, 8)
                    sldm[s % 4].inc(sync.dma_start(
                        out=s1f[:, (s % 4) * 512:(s % 4 + 1) * 512],
                        in_=wpr[:, eb, jq, :]), 16)
                    s_t = hf * 64 + jc - 4
                    if s_t >= 0:
                        pub(s_t)
                        if s_t % 32 == 31:
                            wproj_copy(s_t // 32)
                if hf == 1:
                    for s_t in range(124, 128):
                        pub(s_t)
                    wproj_copy(3)

            @block.tensor
            def _(t):
                t.wait_ge(s_dve.h, dve7)

                def tr(s_t):
                    t.wait_ge(s_dve.h, mq[s_t])
                    if s_t >= 2:
                        t.wait_ge(s_dve.h, mev[s_t - 2])
                    sl = MSL(s_t % 4)
                    last = None
                    for tq in range(4):
                        last = t.transpose(
                            sl[:, tq * 128:(tq + 1) * 128],
                            wpq6[:, (s_t % 8) * 512 + tq * 128:
                                 (s_t % 8) * 512 + (tq + 1) * 128],
                            ident[:])
                    s_pe.inc(last, 1)

                for jc in range(NJ):
                    slot = (jc // 4) % 2
                    psH = PSH(jc % 4)
                    t.wait_ge(shw[slot].h, wq_t0[slot] + (jc // 8 + 1) * 16)
                    if jc >= 4:
                        t.wait_ge(s_act.h, act_rl[jc - 4])
                    last = None
                    for dc in range(NE):
                        last = t.matmul(psH,
                                        wfl3[:, slot, dc,
                                             (jc % 4) * 128:(jc % 4) * 128 + 128],
                                        x8f3[:, dc, t0:t0 + 512],
                                        start=(dc == 0), stop=(dc == NE - 1))
                    s_pe.inc(last, 1)
                    s_t = hf * 64 + jc - 4
                    if s_t >= 0:
                        tr(s_t)
                if hf == 1:
                    for s_t in range(124, 128):
                        tr(s_t)

            @block.scalar
            def _(sc):
                for jc in range(NJ):
                    s = hf * 64 + jc
                    sl4 = jc % 4
                    sc.wait_ge(s_pe.h, pe_mm[jc])
                    s_act.inc(sc.activation(rl[:, (jc % 4) * 512:(jc % 4) * 512 + 512],
                                            PSH(jc % 4),
                                            AF.Relu, scale=scl[:, 5:6]), 1)
                    sc.wait_ge(ssp[sl4].h, sp_t0[sl4] + (jc // 4) * 16)
                    if jc >= 4:
                        sc.wait_ge(s_dve.h, dve_rm[jc - 4])
                    sc.wait_ge(s_act.h, s_act.n)
                    s_act.inc(sc.activation(h_sb[:, sl4 * 1024: sl4 * 1024 + 512],
                                            rl[:, (jc % 4) * 512:(jc % 4) * 512 + 512],
                                            AF.Square), 1)
                    sc.wait_ge(sldm[s % 4].h, MG["ld0"][s % 4] + (s // 4 + 1) * 16)
                    if s >= 4:
                        sc.wait_ge(s_dve.h, mu[s - 4])
                    s_act.inc(sc.activation(
                        xq32[:, (s % 4) * 512:(s % 4 + 1) * 512],
                        s1f[:, (s % 4) * 512:(s % 4 + 1) * 512],
                        AF.Identity, bias=0.0, scale=scl[:, 6:7]), 1)
                    if s >= 4:
                        sc.wait_ge(s_dve.h, mu[s - 4] + 2)
                    sc.wait_ge(s_act.h, s_act.n)
                    s_act.inc(sc.activation(
                        q8p[:, (s % 4) * 512:(s % 4 + 1) * 512],
                        xq32[:, (s % 4) * 512:(s % 4 + 1) * 512],
                        AF.Copy, scale=0.5), 1)

            @block.vector
            def _(v):
                for jc in range(NJ):
                    s = hf * 64 + jc
                    sl4 = jc % 4
                    v.wait_ge(s_act.h, act_sq[jc])
                    v.wait_ge(s_dve.h, s_dve.n)
                    s_dve.inc(v.reduce_max(lncol[:, 0:1],
                                           h_sb[:, sl4 * 1024: sl4 * 1024 + 512], AX.X), 1)
                    v.wait_ge(s_dve.h, s_dve.n)
                    s_dve.inc(v.tensor_max(smax[:, 1:2], smax[:, 1:2], lncol[:, 0:1]), 1)
                    s_t = hf * 64 + jc - 4
                    if s_t >= 0:
                        v.wait_ge(s_pe.h, mtr[s_t])
                        if s_t >= 2:
                            v.wait_ge(spubM[s_t % 2].h, mpub(s_t))
                        v.wait_ge(s_dve.h, s_dve.n)
                        s_dve.inc(v.tensor_copy(
                            mstg[:, (s_t % 2) * 512:(s_t % 2 + 1) * 512],
                            MSL(s_t % 4)), 1)
                    v.wait_ge(s_act.h, mb[s])
                    if s >= 8:
                        v.wait_ge(s_pe.h, mtr[s - 8])
                    s_dve.inc(v.tensor_scalar(
                        q8u[:, (s % 2) * 512:(s % 2 + 1) * 512],
                        xq32[:, (s % 4) * 512:(s % 4 + 1) * 512],
                        240.0, -240.0, ALU.min, ALU.max), 1)
                    s_dve.inc(v.tensor_scalar(
                        q8n[:, (s % 2) * 512:(s % 2 + 1) * 512],
                        q8p[:, (s % 4) * 512:(s % 4 + 1) * 512],
                        120.0, -120.0, ALU.min, ALU.max), 1)
                    v.wait_ge(s_dve.h, s_dve.n)
                    s_dve.inc(v.tensor_tensor(
                        wd6, q8p[:, (s % 4) * 512:(s % 4 + 1) * 512],
                        q8n[:, (s % 2) * 512:(s % 2 + 1) * 512], ALU.subtract), 1)
                    v.wait_ge(s_dve.h, s_dve.n)
                    s_dve.inc(v.scalar_tensor_tensor(
                        wpq6[:, (s % 8) * 512:(s % 8 + 1) * 512],
                        wd6, 2.0, q8u[:, (s % 2) * 512:(s % 2 + 1) * 512],
                        ALU.mult, ALU.add), 1)
                if hf == 1:
                    for s_t in range(124, 128):
                        v.wait_ge(s_pe.h, mtr[s_t])
                        v.wait_ge(spubM[s_t % 2].h, mpub(s_t))
                        v.wait_ge(s_dve.h, s_dve.n)
                        s_dve.inc(v.tensor_copy(
                            mstg[:, (s_t % 2) * 512:(s_t % 2 + 1) * 512],
                            MSL(s_t % 4)), 1)

            @block.gpsimd
            def _(g):
                for jc in range(NJ):
                    sl4 = jc % 4
                    g.wait_ge(s_act.h, act_sq[jc])
                    ssp[sl4].inc(g.dma_start(
                        out=h_spill3[jc, :, t0:t0 + 512],
                        in_=h_sb[:, sl4 * 1024: sl4 * 1024 + 512]), 16)

        if N_PHASES >= 6:
            phase_mm1(0)
            phase_mm1(1)

        if N_PHASES >= 7:
            esE.close()
            h8 = SB(esF, "h8", [128, NJ * 512], BF16)
            rb = SB(esF, "rb", [128, 2 * 512], F32)
            r8u = SB(esF, "r8u", [128, 512], FP8)
            r8b = SB(esF, "r8b", [128, 512], FP8)
            rc = SB(esF, "rc", [128, 512], BF16)
            wpj = SB(esF, "wpj", [128, 2 * 64 * 128], BF16)
            o2T = SB(esF, "o2T", [128, 2 * 512], F32)
            sqd = SB(esF, "sqd", [128, E], F32)
            h83 = h8[:].rearrange("p (a t) -> p a t", t=512)
            wpj3 = wpj[:].rearrange("p (s a j) -> p s a j", s=2, a=64)

            def hbar(g):
                g.wait_ge(s_wt.h, s_wt.n)
                for s4 in range(4):
                    g.wait_ge(ssp[s4].h, ssp[s4].n)

            row_from_col(1, 1, 2)
            scales_from_row(2, 1, 2, cc_h, pre=hbar)

            @block.vector
            def _(v):
                v.wait_ge(s_dve.h, s_dve.n)
                s_dve.inc(v.tensor_tensor(scl[:, 7:8], scl[:, 2:3], scl[:, 6:7],
                                          ALU.mult), 1)
                v.wait_ge(s_dve.h, s_dve.n)
                s_dve.inc(v.reciprocal(scl[:, 7:8], scl[:, 7:8]), 1)

        def phase_h8(hf):
            dve8, pl8 = s_dve.n, s_pl.n
            rb_t0 = [srb[0].n, srb[1].n]
            t0 = hf * 512

            @block.sync
            def _(sync):
                sync.wait_ge(s_cc.h, s_cc.n)
                sync.wait_ge(s_dve.h, dve8)
                for jc in range(NJ):
                    slot = jc % 2
                    if jc >= 2:
                        sync.wait_ge(s_dve.h, dve8 + (jc - 2) * 4 + 2)
                    srb[slot].inc(sync.dma_start(
                        out=rb[:, slot * 512:(slot + 1) * 512],
                        in_=h_spill3[jc, :, t0:t0 + 512]), 16)

            @block.vector
            def _(v):
                for jc in range(NJ):
                    slot = jc % 2
                    v.wait_ge(srb[slot].h, rb_t0[slot] + (jc // 2 + 1) * 16)
                    v.wait_ge(s_dve.h, s_dve.n)
                    s_dve.inc(v.tensor_scalar(r8u[:], rb[:, slot * 512:(slot + 1) * 512],
                                              scl[:, 2:3], 240.0, ALU.mult, ALU.min), 1)
                    s_dve.inc(v.tensor_scalar(r8b[:], rb[:, slot * 512:(slot + 1) * 512],
                                              scl[:, 3:4], 120.0, ALU.mult, ALU.max), 1)
                    v.wait_ge(s_dve.h, s_dve.n)
                    s_dve.inc(v.tensor_scalar(rc[:], r8b[:], 120.0, 2.0,
                                              ALU.subtract, ALU.mult), 1)
                    v.wait_ge(s_dve.h, s_dve.n)
                    s_dve.inc(v.tensor_tensor(h83[:, jc, :], r8u[:], rc[:], ALU.add), 1)

        def phase_mm2(hf):
            pe9, act9, dve9, pl9 = s_pe.n, s_act.n, s_dve.n, s_pl.n
            wq_t0 = [s_h0.n, s_h1.n]
            t0 = hf * 512

            @block.sync
            def _(sync):
                for ec in range(NE):
                    slot = ec % 2
                    if ec >= 2:
                        sync.wait_ge(s_pe.h, pe9 + 5 * (ec - 2) + 1)
                    else:
                        sync.wait_ge(s_pe.h, pe9)
                    shw[slot].inc(sync.dma_start(
                        out=wpj3[:, slot, :, :],
                        in_=wprojT_q[ec]), 16)

            def psT2(slot):
                return (psA if slot == 0 else psB)[:, 1536:1664]

            @block.tensor
            def _(t):
                t.wait_ge(s_dve.h, dve9)
                for ec in range(NE):
                    par = ec % 2
                    psO2 = (psA if par == 0 else psB)[:, 0:512]
                    if ec >= 1:
                        t.wait_ge(s_dve.h, dve9 + 5 * (ec - 2) + 1 if ec >= 2 else dve9)
                    slot = ec % 2
                    t.wait_ge(shw[slot].h, wq_t0[slot] + (ec // 2 + 1) * 16)
                    for jc in range(NJ):
                        last = t.matmul(psO2, wpj3[:, slot, jc, :],
                                        h83[:, jc, :],
                                        start=(jc == 0), stop=(jc == NJ - 1))
                    s_pe.inc(last, 1)
                    t.wait_ge(s_dve.h, dve9 + 5 * ec + 1)
                    for tt in range(4):
                        slot = tt % 2
                        if tt >= 2:
                            t.wait_ge(s_dve.h, dve9 + 5 * ec + 2 + (tt - 2))
                        elif ec >= 1:
                            t.wait_ge(s_dve.h, dve9 + 5 * (ec - 1) + 2 + (tt + 2))
                        s_pe.inc(t.transpose(
                            psT2(slot),
                            o2T[:, (ec % 2) * 512 + tt * 128: (ec % 2) * 512 + tt * 128 + 128],
                            identf[:]), 1)

            @block.vector
            def _(v):
                for ec in range(NE):
                    par = ec % 2
                    psO2 = (psA if par == 0 else psB)[:, 0:512]
                    v.wait_ge(s_pe.h, pe9 + 5 * ec + 1)
                    s_dve.inc(v.tensor_copy(o2T[:, par * 512:(par + 1) * 512], psO2), 1)
                    for tt in range(4):
                        slot = tt % 2
                        v.wait_ge(s_pe.h, pe9 + 5 * ec + 2 + tt)
                        s_dve.inc(v.scalar_tensor_tensor(
                            xres3[:, hf * 4 + tt, ec * 128:(ec + 1) * 128],
                            psT2(slot), scl[:, 7:8],
                            xres3[:, hf * 4 + tt, ec * 128:(ec + 1) * 128],
                            ALU.mult, ALU.add), 1)

            actL, dveL = s_act.n, s_dve.n

            @block.scalar
            def _(sc):
                sc.wait_ge(s_dve.h, dveL)
                for tt4 in range(4):
                    tt = hf * 4 + tt4
                    sc.wait_ge(s_act.h, s_act.n)
                    s_act.inc(sc.activation(sqd[:], xres3[:, tt, :], AF.Square,
                                            accum_out=lncol[:, 0:1]), 1)
                    sc.wait_ge(s_dve.h, dveL + 7 * tt4 + 4)
                    s_act.inc(sc.activation(lncol[:, 5:6], lncol[:, 4:5],
                                            AF.Sqrt, bias=cns[:, 1:2]), 1)

            @block.vector
            def _(v):
                def dr():
                    v.wait_ge(s_dve.h, s_dve.n)
                for tt4 in range(4):
                    tt = hf * 4 + tt4
                    v.wait_ge(s_act.h, actL + 2 * tt4 + 1)
                    dr()
                    s_dve.inc(v.reduce_sum(lncol[:, 1:2], xres3[:, tt, :], AX.X), 1)
                    dr()
                    s_dve.inc(v.tensor_scalar(lncol[:, 2:3], lncol[:, 1:2], 1.0 / E,
                                              None, ALU.mult), 1)
                    dr()
                    s_dve.inc(v.tensor_tensor(lncol[:, 3:4], lncol[:, 2:3],
                                              lncol[:, 2:3], ALU.mult), 1)
                    dr()
                    s_dve.inc(v.scalar_tensor_tensor(lncol[:, 4:5], lncol[:, 0:1],
                                                     1.0 / E, lncol[:, 3:4],
                                                     ALU.mult, ALU.subtract), 1)
                    v.wait_ge(s_act.h, actL + 2 * tt4 + 2)
                    s_dve.inc(v.reciprocal(lncol[:, 6:7], lncol[:, 5:6]), 1)
                    dr()
                    s_dve.inc(v.tensor_scalar(lncol[:, 7:8], lncol[:, 2:3],
                                              lncol[:, 6:7], -1.0, ALU.mult,
                                              ALU.mult), 1)
                    dr()
                    s_dve.inc(v.tensor_scalar(xres3[:, tt, :], xres3[:, tt, :],
                                              lncol[:, 6:7], lncol[:, 7:8],
                                              ALU.mult, ALU.add), 1)
                    dr()

            @block.sync
            def _(sync):
                sync.wait_ge(s_dve.h, s_dve.n)
                s_st.inc(sync.dma_start(
                    out=y.rearrange("(t p) e -> p t e", p=128)[:, hf * 4:(hf + 1) * 4, :],
                    in_=xres3[:, hf * 4:(hf + 1) * 4, :]), 16)

        if N_PHASES >= 7:
            phase_h8(0)
        if N_PHASES >= 8:
            phase_mm2(0)
            phase_h8(1)
            phase_mm2(1)
            esF.close()

        if DEBUG and N_PHASES >= 2:
            @block.sync
            def _(sync):
                sync.wait_ge(s_dve.h, s_dve.n)
                sync.wait_ge(s_act.h, s_act.n)
                s_st.inc(sync.dma_start(
                    out=dbg["dbg_a"],
                    in_=(attO[:] if N_PHASES >= 3 else Qs[:])), 16)
                s_st.inc(sync.dma_start(out=dbg["dbg_b"], in_=Ks[:]), 16)
                s_st.inc(sync.dma_start(out=dbg["dbg_c"], in_=Vs[:]), 16)
                s_st.inc(sync.dma_start(out=dbg["dbg_f"], in_=xres[:]), 16)

        @block.sync
        def _(sync):
            sync.wait_ge(s_ld.h, s_ld.n)
            sync.wait_ge(s_dve.h, s_dve.n)
            sync.wait_ge(s_pe.h, s_pe.n)
            sync.wait_ge(s_act.h, s_act.n)
            sync.wait_ge(s_pl.h, s_pl.n)
            if N_PHASES < 8:
                s_st.inc(sync.dma_start(
                    out=y.rearrange("(t p) e -> p t e", p=128), in_=xres3), 16)
            sync.wait_ge(s_st.h, s_st.n)

        if N_PHASES < 5:
            esD.close()
            esC.close()
            esA.close()
        if N_PHASES < 3:
            esB.close()
        if 5 <= N_PHASES < 7:
            esE.close()
        if N_PHASES == 7:
            esF.close()

    return nc


def kernel(L0_attn_x, L0_attn_Wq, L0_attn_Wk, L0_attn_Wv, L0_attn_Wo,
           L0_attn_bo, L0_ffn_w_fc, L0_ffn_w_proj):
    x = np.ascontiguousarray(np.asarray(L0_attn_x, dtype=np.float32)[0])
    Wq = np.ascontiguousarray(L0_attn_Wq, dtype=np.float32)
    Wk = np.ascontiguousarray(L0_attn_Wk, dtype=np.float32)
    Wv = np.ascontiguousarray(L0_attn_Wv, dtype=np.float32)
    Wo = np.ascontiguousarray(L0_attn_Wo, dtype=np.float32)
    bo = np.ascontiguousarray(L0_attn_bo, dtype=np.float32).reshape(1, M)
    wfc = np.ascontiguousarray(L0_ffn_w_fc, dtype=np.float32)
    wproj = np.ascontiguousarray(L0_ffn_w_proj, dtype=np.float32)

    nc = build_nc()
    in_maps = []
    for c in range(NCORES):
        half = c % 2
        in_maps.append({
            "x_own": x[c * TOWN:(c + 1) * TOWN],
            "x_halo": (np.zeros((TOWN, E), np.float32) if c == 0
                       else np.ascontiguousarray(x[(c - 1) * TOWN:c * TOWN])),
            "Wq": Wq, "Wk": Wk, "Wv": Wv, "Wo": Wo, "bo": bo,
            "wfc_half": np.ascontiguousarray(wfc[half * (FF // 2):(half + 1) * (FF // 2)]),
            "wproj_half": np.ascontiguousarray(wproj[:, half * (FF // 2):(half + 1) * (FF // 2)]),
            "wfc_8": np.ascontiguousarray(wfc[c * (FF // 8):(c + 1) * (FF // 8)]),
            "wproj_8": np.ascontiguousarray(wproj[:, c * (FF // 8):(c + 1) * (FF // 8)]),
            "masks": _mask_host(c),
        })
    res = run_bass_kernel_spmd(nc, in_maps, core_ids=list(range(NCORES)))
    out = np.concatenate([res.results[c]["y"] for c in range(NCORES)], axis=0)
    return out.reshape(1, T, E).astype(np.float32)


# version 11
# speedup vs baseline: 1.0210x; 1.0210x over previous
import math
import os
import numpy as np
import ml_dtypes
import concourse.bass as bass
import concourse.mybir as mybir
from concourse.bass_utils import run_bass_kernel_spmd

F32 = mybir.dt.float32
BF16 = mybir.dt.bfloat16
FP8 = mybir.dt.float8e4
AF = mybir.ActivationFunctionType
ALU = mybir.AluOpType
AX = mybir.AxisListType

NCORES = 8
T, TOWN, E, M, KV, FF = 8192, 1024, 2048, 2048, 512, 8192
HEADS, GROUPS, D = 16, 4, 128
NT, NE, NJ = TOWN // 128, E // 128, FF // 128
SCALE_QK = 1.0 / math.sqrt(D)
EXP_BIAS = -30.0
LN_EPS = 1e-5

N_PHASES = int(os.environ.get("KPHASES", "99"))
DEBUG = os.environ.get("KDEBUG", "0") == "1"


class Sem:
    def __init__(self, nc, name):
        self.h = nc.alloc_semaphore(name)
        self.n = 0

    def inc(self, inst, k=1):
        inst.then_inc(self.h, k)
        self.n += k
        return inst


def _mask_host(core):
    out = np.zeros((NT, 128, 9 * 128), dtype=ml_dtypes.bfloat16)
    kr = np.arange(128)[:, None]
    qc = np.arange(128)[None, :]
    for m in range(NT):
        gm = NT * core + m
        for kt in range(9):
            jt = gm - 8 + kt
            if jt < 0:
                continue
            delta = (8 - kt) * 128
            allowed = (kr <= qc + delta) & (kr >= qc + delta - 1023)
            out[m, :, kt * 128:(kt + 1) * 128] = allowed.astype(ml_dtypes.bfloat16)
    return out


def build_nc():
    nc = bass.Bass("TRN2", target_bir_lowering=False, num_devices=NCORES)

    x_own = nc.dram_tensor("x_own", [TOWN, E], F32, kind="ExternalInput").ap()
    x_halo = nc.dram_tensor("x_halo", [TOWN, E], F32, kind="ExternalInput").ap()
    Wq_d = nc.dram_tensor("Wq", [E, M], F32, kind="ExternalInput").ap()
    Wk_d = nc.dram_tensor("Wk", [E, KV], F32, kind="ExternalInput").ap()
    Wv_d = nc.dram_tensor("Wv", [E, KV], F32, kind="ExternalInput").ap()
    Wo_d = nc.dram_tensor("Wo", [M, M], F32, kind="ExternalInput").ap()
    bo_d = nc.dram_tensor("bo", [1, M], F32, kind="ExternalInput").ap()
    wfc_half = nc.dram_tensor("wfc_half", [FF // 2, E], F32, kind="ExternalInput").ap()
    wproj_half = nc.dram_tensor("wproj_half", [E, FF // 2], F32, kind="ExternalInput").ap()
    wfc_8 = nc.dram_tensor("wfc_8", [FF // 8, E], F32, kind="ExternalInput").ap()
    wproj_8 = nc.dram_tensor("wproj_8", [E, FF // 8], F32, kind="ExternalInput").ap()
    masks_d = nc.dram_tensor("masks", [NT, 128, 9 * 128], BF16, kind="ExternalInput").ap()
    y = nc.dram_tensor("y", [TOWN, E], F32, kind="ExternalOutput").ap()

    h_spill = nc.dram_tensor("h_spill", [NJ, 128, TOWN], F32, kind="Internal").ap()
    wfcT_q = nc.dram_tensor("wfcT_q", [NE, 128, FF], BF16, kind="Internal",
                            addr_space="Shared").ap()
    wprojT_q = nc.dram_tensor("wprojT_q", [NE, 128, 64, 128], BF16, kind="Internal",
                              addr_space="Shared").ap()
    wfcT_loc = nc.dram_tensor("wfcT_loc", [NE, 128, 32, 128], BF16, kind="Internal").ap()
    wprojT_loc = nc.dram_tensor("wprojT_loc", [NE, 128, 32, 128], BF16, kind="Internal").ap()
    cc_w = nc.dram_tensor("cc_w", [1, 16], F32, kind="Internal").ap()
    cc_x = nc.dram_tensor("cc_x", [1, 16], F32, kind="Internal").ap()
    cc_h = nc.dram_tensor("cc_h", [1, 16], F32, kind="Internal").ap()
    rg = [list(range(NCORES))]

    from contextlib import ExitStack
    es = ExitStack()
    esA = ExitStack()
    esB = ExitStack()
    esC = ExitStack()
    esD = ExitStack()
    esE = ExitStack()
    esF = ExitStack()
    SB = lambda stk, name, shape, dt: stk.enter_context(nc.sbuf_tensor(name, shape, dt))

    xres = SB(es, "xres", [128, NT * E], F32)
    ident = SB(es, "ident", [128, 128], BF16)
    identf = SB(es, "identf", [128, 128], F32)
    ones = SB(es, "ones", [128, 128], BF16)
    scl = SB(es, "scl", [128, 8], F32)
    smax = SB(es, "smax", [128, 4], F32)
    srow = SB(es, "srow", [1, 128], F32)
    cns = SB(es, "cns", [128, 2], F32)
    lncol = SB(es, "lncol", [128, 8], F32)
    ccb = SB(es, "ccb", [1, 16], F32)

    psA = es.enter_context(nc.psum_tensor("psA", [128, 2048], F32))
    psB = es.enter_context(nc.psum_tensor("psB", [128, 2048], F32))

    s_ld = Sem(nc, "s_ld")
    s_const = Sem(nc, "s_const")
    s_x0 = Sem(nc, "s_x0")
    s_x1 = Sem(nc, "s_x1")
    s_wkv = Sem(nc, "s_wkv")
    s_wq0 = Sem(nc, "s_wq0")
    s_wq1 = Sem(nc, "s_wq1")
    s_pe = Sem(nc, "s_pe")
    s_act = Sem(nc, "s_act")
    s_dve = Sem(nc, "s_dve")
    s_pl = Sem(nc, "s_pl")
    s_st = Sem(nc, "s_st")
    s_wp = Sem(nc, "s_wp")
    s_wd = Sem(nc, "s_wd")
    s_wt = Sem(nc, "s_wt")
    s_cc = Sem(nc, "s_cc")
    s_mk = Sem(nc, "s_mk")
    s_sp = Sem(nc, "s_sp")
    s_rb = Sem(nc, "s_rb")
    s_wd1 = Sem(nc, "s_wd1")
    s_txs = [[Sem(nc, "s_tx%d_%d" % (p, k)) for k in range(4)] for p in range(2)]
    s_wos = [Sem(nc, "s_wo%d" % k) for k in range(4)]
    s_ccd = Sem(nc, "s_ccd")
    s_rb1 = Sem(nc, "s_rb1")
    s_sp1 = Sem(nc, "s_sp1")
    s_sp2 = Sem(nc, "s_sp2")
    s_sp3 = Sem(nc, "s_sp3")
    s_h0 = Sem(nc, "s_h0")
    s_h1 = Sem(nc, "s_h1")
    s_wf0 = Sem(nc, "s_wf0")
    s_wf1 = Sem(nc, "s_wf1")
    swf = [s_wf0, s_wf1]
    shw = [s_h0, s_h1]
    sx = [s_x0, s_x1]
    swq = [s_wq0, s_wq1]
    swd = [s_wd, s_wd1]
    ssp = [s_sp, s_sp1, s_sp2, s_sp3]
    srb = [s_rb, s_rb1]

    identt = nc.inline_tensor(np.eye(128, dtype=ml_dtypes.bfloat16).view(np.uint16), "idbf")
    identtf = nc.inline_tensor(np.eye(128, dtype=np.float32), "idf32")

    dbg = {}
    if DEBUG:
        for nm, shape in [("dbg_a", [128, HEADS * TOWN]), ("dbg_b", [128, GROUPS * 2048]),
                          ("dbg_c", [128, 16 * KV])]:
            dbg[nm] = nc.dram_tensor(nm, shape, BF16, kind="ExternalOutput").ap()
        dbg["dbg_f"] = nc.dram_tensor("dbg_f", [128, NT * E], F32, kind="ExternalOutput").ap()

    with es, nc.Block() as block:
        Qs = SB(esA, "Qs", [128, HEADS * TOWN], BF16)
        Ks = SB(esA, "Ks", [128, GROUPS * 2 * TOWN], BF16)
        Vs = SB(esA, "Vs", [128, 16 * KV], BF16)
        wb = SB(esA, "wb", [128, 2 * 2048], BF16)
        WKV = SB(esB, "WKV", [128, NE * 512], BF16)
        xstg = SB(esB, "xstg", [128, 2 * 1024], BF16)
        xT = SB(esB, "xT", [128, NE * TOWN], BF16)

        WKV3 = WKV[:].rearrange("p (ec c) -> p ec c", c=512)
        xT3 = xT[:].rearrange("p (ec t) -> p ec t", t=TOWN)
        Qs3 = Qs[:].rearrange("p (h t) -> p h t", t=TOWN)
        Ks3 = Ks[:].rearrange("p (g t) -> p g t", t=2 * TOWN)
        Vs3 = Vs[:].rearrange("p (l c) -> p l c", c=KV)
        xres3 = xres[:].rearrange("p (t e) -> p t e", e=E)

        @block.sync
        def _(sync):
            s_ld.inc(sync.dma_start(
                out=xres3, in_=x_own.rearrange("(t p) e -> p t e", p=128)), 16)

        @block.gpsimd
        def _(g):
            s_const.inc(g.dma_start(out=ident[:], in_=identt.ap().bitcast(BF16)), 16)
            s_const.inc(g.dma_start(out=identf[:], in_=identtf.ap()), 16)

        @block.vector
        def _(v):
            s_dve.inc(v.memset(ones[:], 1.0), 1)
            s_dve.inc(v.memset(smax[:], 0.0), 1)
            s_dve.inc(v.memset(cns[:, 0:1], EXP_BIAS), 1)
            s_dve.inc(v.memset(cns[:, 1:2], LN_EPS), 1)
            s_dve.inc(v.memset(srow[:], 0.0), 1)
            s_dve.inc(v.memset(scl[:], 0.0), 1)
            s_dve.inc(v.memset(lncol[:], 0.0), 1)

        def phase_proj(src_dram, half):
            import os
            SUB = int(os.environ.get("KP1SUB", "9"))
            pe0, dve0 = s_pe.n, s_dve.n
            x_tot0 = [s_x0.n, s_x1.n]

            @block.gpsimd
            def _(g):
                g.wait_ge(s_pe.h, pe0)
                s_wkv.inc(g.dma_start(
                    out=WKV3, in_=Wk_d.rearrange("(ec e) k -> e ec k", e=128)), 16)
                for u in range(2 * NT):
                    tt, gi = divmod(u, 2)
                    slot = u % 2
                    if u >= 2:
                        g.wait_ge(s_pe.h, pe0 + u - 1)
                    sx[slot].inc(g.dma_start(
                        out=xstg[:, slot * 1024:(slot + 1) * 1024],
                        in_=src_dram.rearrange("(t p) e -> p t e", p=128)[
                            :, tt, gi * 1024:(gi + 1) * 1024]), 16)
                if SUB >= 3:
                    g.wait_ge(s_pe.h, pe0 + 2 * NT + 8)
                    s_wkv.inc(g.dma_start(
                        out=WKV3, in_=Wv_d.rearrange("(ec e) k -> e ec k", e=128)), 16)

            @block.tensor
            def _(t):
                t.wait_ge(s_const.h, 32)
                t.wait_ge(s_dve.h, dve0)
                for u in range(2 * NT):
                    tt, gi = divmod(u, 2)
                    slot = u % 2
                    t.wait_ge(sx[slot].h, x_tot0[slot] + (u // 2 + 1) * 16)
                    ps = (psA if u % 2 == 0 else psB).bitcast(BF16)
                    if u >= 2:
                        t.wait_ge(s_dve.h, dve0 + u - 1)
                    last = None
                    for j in range(8):
                        last = t.transpose(
                            ps[:, j * 128:(j + 1) * 128],
                            xstg[:, slot * 1024 + j * 128: slot * 1024 + j * 128 + 128],
                            ident[:])
                    s_pe.inc(last, 1)

            @block.vector
            def _(v):
                for u in range(2 * NT):
                    tt, gi = divmod(u, 2)
                    ps = (psA if u % 2 == 0 else psB).bitcast(BF16)
                    v.wait_ge(s_pe.h, pe0 + u + 1)
                    s_dve.inc(v.tensor_copy(
                        xT3[:, gi * 8:(gi + 1) * 8, tt * 128:(tt + 1) * 128],
                        ps[:, 0:1024].rearrange("p (j c) -> p j c", c=128)), 1)

            pe1, dve1 = s_pe.n, s_dve.n

            wk_tot = s_wkv.n - (16 if SUB >= 3 else 0)
            wv_tot = s_wkv.n
            wq_tot0 = [s_wq0.n, s_wq1.n]
            if half == 1:
                @block.gpsimd
                def _(g):
                    for mc in range(HEADS):
                        slot = mc % 2
                        if mc >= 2:
                            g.wait_ge(s_pe.h, pe1 + 16 + mc - 1)
                        swq[slot].inc(g.dma_start(
                            out=wb[:, slot * 2048: slot * 2048 + 2048].rearrange(
                                "p (ec c) -> p ec c", c=128),
                            in_=Wq_d.rearrange("(ec e) m -> e ec m", e=128)[
                                :, :, mc * 128:(mc + 1) * 128]), 16)

            @block.tensor
            def _(t):
                t.wait_ge(s_dve.h, dve1)
                t.wait_ge(s_wkv.h, wk_tot)
                for gidx in range(8 if SUB >= 2 else 0):
                    kd, kq = divmod(gidx, 2)
                    ps = (psA if gidx % 2 == 0 else psB)[:, 0:512]
                    if gidx >= 2:
                        t.wait_ge(s_dve.h, dve1 + gidx - 1)
                    last = None
                    for ec in range(NE):
                        last = t.matmul(ps, WKV3[:, ec, kd * 128:(kd + 1) * 128],
                                        xT3[:, ec, kq * 512:(kq + 1) * 512],
                                        start=(ec == 0), stop=(ec == NE - 1))
                    s_pe.inc(last, 1)
                if SUB >= 3:
                    t.wait_ge(s_wkv.h, wv_tot)
                for tt in range(NT if SUB >= 3 else 0):
                    gidx = (8 if SUB >= 2 else 0) + tt
                    ps = (psA if gidx % 2 == 0 else psB)[:, 0:512]
                    t.wait_ge(s_dve.h, dve1 + gidx - 1)
                    last = None
                    for ec in range(NE):
                        last = t.matmul(ps, xT3[:, ec, tt * 128:(tt + 1) * 128],
                                        WKV3[:, ec, 0:512],
                                        start=(ec == 0), stop=(ec == NE - 1))
                    s_pe.inc(last, 1)
                if half == 1:
                    t.wait_ge(s_dve.h, dve1 + 16)
                    for mc in range(HEADS):
                        slot = mc % 2
                        psq = (psA if mc % 2 == 0 else psB)
                        t.wait_ge(swq[mc % 2].h, wq_tot0[mc % 2] + (mc // 2 + 1) * 16)
                        if mc >= 2:
                            t.wait_ge(s_dve.h, dve1 + 16 + (mc - 2) + 1)
                        wv = wb[:, slot * 2048: slot * 2048 + 2048].rearrange(
                            "p (ec c) -> p ec c", c=128)
                        last = None
                        for ec in range(NE):
                            for nq in range(2):
                                last = t.matmul(
                                    psq[:, nq * 512:(nq + 1) * 512],
                                    wv[:, ec, :],
                                    xT3[:, ec, nq * 512:(nq + 1) * 512],
                                    start=(ec == 0), stop=(ec == NE - 1))
                        s_pe.inc(last, 1)

            @block.vector
            def _(v):
                nk = 8 if SUB >= 2 else 0
                for gidx in range(nk):
                    kd, kq = divmod(gidx, 2)
                    ps = (psA if gidx % 2 == 0 else psB)[:, 0:512]
                    v.wait_ge(s_pe.h, pe1 + gidx + 1)
                    s_dve.inc(v.tensor_copy(
                        Ks3[:, kd, half * TOWN + kq * 512: half * TOWN + (kq + 1) * 512], ps), 1)
                for tt in range(NT if SUB >= 3 else 0):
                    gidx = nk + tt
                    ps = (psA if gidx % 2 == 0 else psB)[:, 0:512]
                    v.wait_ge(s_pe.h, pe1 + gidx + 1)
                    s_dve.inc(v.tensor_copy(Vs3[:, half * 8 + tt, :], ps), 1)
                if half == 1:
                    for mc in range(HEADS):
                        psq = (psA if mc % 2 == 0 else psB)
                        v.wait_ge(s_pe.h, pe1 + 16 + mc + 1)
                        s_dve.inc(v.tensor_copy(Qs3[:, mc, :], psq[:, 0:1024]), 1)

        if N_PHASES >= 1:
            phase_proj(x_halo, 0)
        if N_PHASES >= 2:
            phase_proj(x_own, 1)

        attO = None
        P3T = {}
        if N_PHASES >= 3:
            esB.close()
            attO = SB(esC, "attO", [128, HEADS * TOWN], BF16)
            wpq = SB(esC, "wpq", [128, 8 * 512], BF16)
            wpin = Ks[:].bitcast(F32)[:, 0:3 * 512]
            wpt32 = Qs[:].bitcast(F32)[:, 0:4 * 512]
            wpb8 = Qs[:].bitcast(FP8)[:, 8192:8192 + 4 * 512]
            from contextlib import ExitStack as _ES
            esC2 = _ES()
            mskb = SB(esC2, "mskb", [128, 2 * 1152], BF16)
            PTb = SB(esC2, "PTb", [128, 2 * 1152], BF16)
            Onrm = SB(esC2, "Onrm", [128, 2 * 128], BF16)
            rcp = SB(esC2, "rcp", [128, 2], F32)
            wmx = wpq[:].bitcast(F32)
            attO3 = attO[:].rearrange("p (h t) -> p h t", t=TOWN)
            s_mk0, s_mk1 = Sem(nc, "s_mk0"), Sem(nc, "s_mk1")
            smk = [s_mk0, s_mk1]
            smx = [s_txs[0][0], s_txs[0][1]]
            mx_t0 = [s.n for s in smx]
            pe3, act3, dve3 = s_pe.n, s_act.n, s_dve.n
            ccd0 = s_ccd.n
            PREP3 = N_PHASES >= 5
            KT1, KV1, KT2, KV2, KCH, KBC, KEV = 64, 65, 66, 67, 81, 82, 83

            def peoff(k):
                if not PREP3:
                    return 0
                return (1 if k > KT1 else 0) + (1 if k > KT2 else 0) + \
                       (1 if k > KBC else 0)

            PET1 = pe3 + 3 * (KT1 + 1) + 1
            PET2 = pe3 + 3 * (KT2 + 1) + 2
            PEBC = pe3 + 3 * (KBC + 1) + 3

            def psS(par):
                return (psA if par == 0 else psB)[:, 0:1152]

            def psO(par):
                return (psA if par == 0 else psB)[:, 1536:1664]

            def psSum(par):
                return (psA if par == 0 else psB)[:, 1152:1153]

            def psOT(par):
                return (psA if par == 0 else psB).bitcast(BF16)[:, 3456:3584]

            dve_mask = {}
            dve_recip = {}
            dve_pair = {}

            @block.vector
            def _(v):
                def dr():
                    v.wait_ge(s_dve.h, s_dve.n)
                for m in range(NT):
                    for h in range(HEADS):
                        k = m * HEADS + h
                        par = k % 2
                        if h == 0:
                            v.wait_ge(smk[m % 2].h, (m // 2 + 1) * 16)
                        v.wait_ge(s_act.h, act3 + 2 * k + 1)
                        s_dve.inc(v.tensor_tensor(
                            PTb[:, par * 1152: par * 1152 + 1152],
                            PTb[:, par * 1152: par * 1152 + 1152],
                            mskb[:, (m % 2) * 1152:(m % 2) * 1152 + 1152],
                            ALU.mult), 1)
                        dve_mask[k] = s_dve.n
                        v.wait_ge(s_pe.h, pe3 + 3 * k + peoff(k) + 2)
                        s_dve.inc(v.reciprocal(rcp[:, par:par + 1], psSum(par)), 1)
                        dve_recip[k] = s_dve.n
                        v.wait_ge(s_pe.h, pe3 + 3 * k + peoff(k) + 3)
                        s_dve.inc(v.tensor_copy(
                            attO3[:, h, m * 128:(m + 1) * 128], psOT(par)), 1)
                        if not PREP3:
                            continue
                        if k < 64:
                            i = k
                            v.wait_ge(smx[i % 2].h, mx_t0[i % 2] + (i // 2 + 1) * 16)
                            dr()
                            s_dve.inc(v.reduce_max(
                                lncol[:, 0:1],
                                wmx[:, (i % 2) * 512:(i % 2) * 512 + 512],
                                AX.X, apply_absolute_value=True), 1)
                            dr()
                            s_dve.inc(v.tensor_max(
                                smax[:, 2 + i // 32:3 + i // 32],
                                smax[:, 2 + i // 32:3 + i // 32],
                                lncol[:, 0:1]), 1)
                            dve_pair[i] = s_dve.n
                        elif k == KV1:
                            v.wait_ge(s_pe.h, PET1)
                            dr()
                            s_dve.inc(v.reduce_max(lncol[0:2, 0:1],
                                                   psA[0:2, 1280:1408], AX.X), 1)
                            P3T["v1"] = s_dve.n
                        elif k == KV2:
                            v.wait_ge(s_pe.h, PET2 + 2)
                            dr()
                            s_dve.inc(v.tensor_copy(srow[0:1, 4:6],
                                                    psB[0:1, 1280:1282]), 1)
                            P3T["v2"] = s_dve.n
                        elif k == KCH:
                            v.wait_ge(s_ccd.h, ccd0 + 32)
                            for i2 in range(2):
                                dr()
                                s_dve.inc(v.tensor_scalar(
                                    srow[0:1, 32 + 2 * i2:33 + 2 * i2],
                                    srow[0:1, 20 + i2:21 + i2],
                                    1.0 / 448.0, 1e-6, ALU.mult, ALU.max), 1)
                                dr()
                                s_dve.inc(v.reciprocal(
                                    srow[0:1, 32 + 2 * i2:33 + 2 * i2],
                                    srow[0:1, 32 + 2 * i2:33 + 2 * i2]), 1)
                                dr()
                                s_dve.inc(v.tensor_scalar(
                                    srow[0:1, 33 + 2 * i2:34 + 2 * i2],
                                    srow[0:1, 32 + 2 * i2:33 + 2 * i2],
                                    0.5, None, ALU.mult), 1)
                            P3T["chain"] = s_dve.n
                        elif k == KEV:
                            v.wait_ge(s_pe.h, PEBC + 2)
                            dr()
                            s_dve.inc(v.tensor_copy(scl[:, 4:8],
                                                    psB[:, 1280:1284]), 1)
                            P3T["scl"] = s_dve.n

            if PREP3:
                @block.gpsimd
                def _(g):
                    g.wait_ge(s_pe.h, pe3)
                    for i in range(64):
                        wi, ch = divmod(i, 32)
                        w8 = wfc_8 if wi == 0 else wproj_8
                        cn = w8.shape[1] // 512
                        a, c2 = divmod(ch, cn)
                        if i >= 2:
                            g.wait_ge(s_dve.h, dve_pair[i - 2])
                        smx[i % 2].inc(g.dma_start(
                            out=wmx[:, (i % 2) * 512:(i % 2) * 512 + 512],
                            in_=w8.rearrange("(a p) (c q) -> p a c q",
                                             p=128, q=512)[:, a, c2, :]), 16)
                    g.wait_ge(s_dve.h, P3T["v2"])
                    s_ccd.inc(g.dma_start(out=cc_w, in_=srow[0:1, 0:16]), 16)
                    g.wait_ge(s_ccd.h, ccd0 + 16)
                    s_cc.inc(g.collective_compute(
                        "AllReduce", ALU.max, rg, [cc_w], [cc_w],
                        cc_dim="Partition", unique_tensors="No"), 1)
                    g.wait_ge(s_cc.h, s_cc.n)
                    s_ccd.inc(g.dma_start(out=srow[0:1, 16:32], in_=cc_w), 16)

            @block.sync
            def _(sync):
                sync.wait_ge(s_pe.h, pe3)
                for m in range(NT):
                    if m >= 2:
                        klast = (m - 2) * HEADS + (HEADS - 1)
                        sync.wait_ge(s_dve.h, dve_mask[klast])
                    smk[m % 2].inc(sync.dma_start(
                        out=mskb[:, (m % 2) * 1152:(m % 2) * 1152 + 1152],
                        in_=masks_d[m]), 16)

            @block.tensor
            def _(t):
                t.wait_ge(s_dve.h, dve3)
                for m in range(NT):
                    for h in range(HEADS):
                        k = m * HEADS + h
                        par = k % 2
                        g = h // 4
                        if k >= 2:
                            t.wait_ge(s_act.h, act3 + 2 * (k - 2) + 1)
                            t.wait_ge(s_dve.h, dve_recip[k - 2])
                        if PREP3 and k == KT1 + 2:
                            t.wait_ge(s_dve.h, P3T["v1"])
                        elif PREP3 and k == KV2 + 2:
                            t.wait_ge(s_dve.h, P3T["v2"])
                        elif PREP3 and k == KEV + 2:
                            t.wait_ge(s_dve.h, P3T["scl"])
                        last = None
                        for kt in range(9):
                            last = t.matmul(
                                psS(par)[:, kt * 128:(kt + 1) * 128],
                                Ks3[:, g, (m + kt) * 128:(m + kt + 1) * 128],
                                Qs3[:, h, m * 128:(m + 1) * 128],
                                start=True, stop=True)
                        s_pe.inc(last, 1)
                        t.wait_ge(s_dve.h, dve_mask[k])
                        for kt in range(9):
                            pt = PTb[:, par * 1152 + kt * 128: par * 1152 + (kt + 1) * 128]
                            t.matmul(psO(par), pt,
                                     Vs3[:, m + kt, g * 128:(g + 1) * 128],
                                     start=(kt == 0), stop=(kt == 8))
                            last = t.matmul(psSum(par), pt, ones[:, 0:1],
                                            start=(kt == 0), stop=(kt == 8))
                        s_pe.inc(last, 1)
                        t.wait_ge(s_act.h, act3 + 2 * k + 2)
                        s_pe.inc(t.transpose(
                            psOT(par), Onrm[:, par * 128:(par + 1) * 128], ident[:]), 1)
                        if PREP3:
                            if k == KT1:
                                t.wait_ge(s_dve.h, dve_pair[63])
                                t.wait_ge(s_dve.h, dve_recip[KT1])
                                s_pe.inc(t.transpose(psA[0:2, 1280:1408],
                                                     smax[:, 2:4], identf[:]), 1)
                            elif k == KT2:
                                t.wait_ge(s_dve.h, P3T["v1"])
                                t.wait_ge(s_dve.h, dve_recip[KT2 - 1])
                                s_pe.inc(t.transpose(psB[0:1, 1280:1282],
                                                     lncol[0:2, 0:1],
                                                     identf[0:2, 0:2]), 1)
                            elif k == KBC:
                                t.wait_ge(s_dve.h, P3T["chain"])
                                s_pe.inc(t.matmul(
                                    psB[:, 1280:1284],
                                    identf[0:1, 0:1].broadcast_to((1, 128)),
                                    srow[0:1, 32:36],
                                    start=True, stop=True), 1)

            @block.scalar
            def _(sc):
                for m in range(NT):
                    for h in range(HEADS):
                        k = m * HEADS + h
                        par = k % 2
                        sc.wait_ge(s_pe.h, pe3 + 3 * k + peoff(k) + 1)
                        s_act.inc(sc.activation(
                            PTb[:, par * 1152: par * 1152 + 1152],
                            psS(par), AF.Exp, bias=cns[:, 0:1], scale=SCALE_QK), 1)
                        sc.wait_ge(s_dve.h, dve_recip[k])
                        s_act.inc(sc.activation(
                            Onrm[:, par * 128:(par + 1) * 128], psO(par),
                            AF.Copy, scale=rcp[:, par:par + 1]), 1)

            esC2.close()

        if N_PHASES >= 4:
            o_sb = SB(esD, "o_sb", [128, 2 * 1024], F32)
            wstg = Vs[:, 0:2 * 1024]
            wpu8 = Vs[:].bitcast(FP8)[:, 8192:8192 + 2 * 512]
            wpcl = Vs[:].bitcast(FP8)[:, 9216:9216 + 2 * 512]
            wpd = Vs[:, 5120:5120 + 512]
            PREP4 = N_PHASES >= 5
            SL = 8
            pe4, act4, dve4 = s_pe.n, s_act.n, s_dve.n
            wo_t0 = [s_wq0.n, s_wq1.n]
            sldw = s_txs[1]
            sld4_0 = [s.n for s in sldw]
            spub4 = [s_wos[0], s_wos[2]]
            pub4_0 = [s.n for s in spub4]

            peW = {}; peT = {}; peSB = {}
            c = pe4 + 4
            for mo in range(16):
                c += 1; peW[mo] = c
                for b in range(2):
                    c += 1; peT[(mo, b)] = c
                if PREP4 and mo >= 1:
                    for b in range(4):
                        c += 1; peSB[4 * (mo - 1) + b] = c
            if PREP4:
                for b in range(4):
                    c += 1; peSB[60 + b] = c
            dveE = {}; dveA = {}; dveEV = {}; dveU = {}; dveQ = {}
            c = dve4 + 32
            for mo in range(16):
                c += 1; dveE[mo] = c
                for b in range(2):
                    c += 1; dveA[(mo, b)] = c
                if PREP4:
                    if mo >= 1:
                        for b in range(4):
                            c += 1; dveEV[4 * (mo - 1) + b] = c
                    for i in range(SL):
                        s = mo * SL + i
                        c += 1; dveU[s] = c
                        c += 3; dveQ[s] = c
            if PREP4:
                for b in range(4):
                    c += 1; dveEV[60 + b] = c
            actT = {}; actB = {}
            c = act4
            if PREP4:
                for mo in range(16):
                    for i in range(SL):
                        s = mo * SL + i
                        c += 1; actT[s] = c
                        c += 1; actB[s] = c

            def pubSB(g):
                return pub4_0[g % 2] + 32 * (g // 2)

            PSLB = [(psB, 3072), (psA, 3072)]

            def PSL(sl):
                pt, off = PSLB[sl]
                return pt.bitcast(BF16)[:, off:off + 1024]

            @block.sync
            def _(sync):
                sync.wait_ge(s_pe.h, pe4)
                sync.wait_ge(s_dve.h, dve4)
                sync.wait_ge(s_act.h, act4)
                s_ld.inc(sync.dma_start(out=o_sb[0:1, 0:E], in_=bo_d), 16)
                if PREP4:
                    wfc_r = wfc_half.rearrange("(a p) (c q) -> p a c q",
                                               p=128, q=512)
                    par32 = nc.snap(nc.partition_id([mybir.EngineType.SP]) % 2 * 32,
                                    engines=[mybir.EngineType.SP],
                                    min_val=0, max_val=32)

                    def wfc_copy(a4):
                        sync.wait_ge(spub4[0].h, spub4[0].n)
                        sync.wait_ge(spub4[1].h, spub4[1].n)
                        sync.wait_ge(s_wt.h, s_wt.n)
                        s_wt.inc(sync.dma_start(
                            out=wfcT_q.rearrange("a p (t j) -> a p t j", j=128)[
                                a4 * 4:(a4 + 1) * 4, :, bass.ds(par32, 32), :],
                            in_=wfcT_loc[a4 * 4:(a4 + 1) * 4]), 16)

                    def pubs(mo_sl):
                        for b in range(4):
                            g = 4 * mo_sl + b
                            st = 2 * mo_sl + b // 2
                            cq, r0 = divmod(st * 4, 32)
                            sync.wait_ge(s_dve.h, dveEV[g])
                            for t2 in range(2):
                                dc = cq * 4 + 2 * (b % 2) + t2
                                spub4[g % 2].inc(sync.dma_start(
                                    out=wfcT_loc[dc][:, r0:r0 + 4, :],
                                    in_=wstg[:, (g % 2) * 1024 + t2 * 512:
                                             (g % 2) * 1024 + (t2 + 1) * 512].rearrange(
                                        "p (i j) -> p i j", j=128)), 16)

                    for mo in range(16):
                        if mo >= 1:
                            pubs(mo - 1)
                            if mo % 4 == 0:
                                wfc_copy(mo // 4 - 1)
                        for i in range(SL):
                            s = mo * SL + i
                            cq, rb = divmod(s, 32)
                            if s >= 3:
                                sync.wait_ge(s_act.h, actT[s - 3])
                            sldw[s % 3].inc(sync.dma_start(
                                out=wpin[:, (s % 3) * 512:(s % 3 + 1) * 512],
                                in_=wfc_r[:, rb, cq, :]), 16)
                    pubs(15)
                    wfc_copy(3)

            @block.tensor
            def _(t):
                t.wait_ge(s_ld.h, s_ld.n)
                t.wait_ge(s_dve.h, dve4)
                for j in range(4):
                    if j >= 1:
                        t.wait_ge(s_dve.h, dve4 + 8 * j)
                    s_pe.inc(t.matmul(psA[:, 1024:1536],
                                      identf[0:1, 0:1].broadcast_to((1, 128)),
                                      o_sb[0:1, j * 512:(j + 1) * 512],
                                      start=True, stop=True), 1)

            @block.vector
            def _(v):
                for j in range(4):
                    v.wait_ge(s_pe.h, pe4 + j + 1)
                    for tt in range(NT):
                        s_dve.inc(v.tensor_tensor(
                            xres3[:, tt, j * 512:(j + 1) * 512],
                            xres3[:, tt, j * 512:(j + 1) * 512],
                            psA[:, 1024:1536], ALU.add), 1)

            @block.gpsimd
            def _(g):
                g.wait_ge(s_pe.h, pe4)
                for mo in range(HEADS):
                    slot = mo % 2
                    if mo >= 2:
                        g.wait_ge(s_pe.h, peW[mo - 2])
                    swq[slot].inc(g.dma_start(
                        out=wb[:, slot * 2048: slot * 2048 + 2048].rearrange(
                            "p (mi c) -> p mi c", c=128),
                        in_=Wo_d.rearrange("(mi e) m -> e mi m", e=128)[
                            :, :, mo * 128:(mo + 1) * 128]), 16)

            def psT(slot):
                return (psA if slot == 0 else psB)[:, 1024:1536]

            def emit_sb(t, mo_sl):
                for b in range(4):
                    g = 4 * mo_sl + b
                    st = 2 * mo_sl + b // 2
                    s0 = st * 4
                    t.wait_ge(s_dve.h, dveQ[s0 + 3])
                    if g >= 2:
                        t.wait_ge(s_dve.h, dveEV[g - 2])
                    sl = PSL(g % 2)
                    last = None
                    for t2 in range(2):
                        tq = 2 * (b % 2) + t2
                        for i in range(4):
                            s = s0 + i
                            last = t.transpose(
                                sl[:, (t2 * 4 + i) * 128:(t2 * 4 + i + 1) * 128],
                                wpq[:, (s % 8) * 512 + tq * 128:
                                    (s % 8) * 512 + (tq + 1) * 128],
                                ident[:])
                    s_pe.inc(last, 1)

            @block.tensor
            def _(t):
                for mo in range(HEADS):
                    par = mo % 2
                    psW = (psA if par == 0 else psB)[:, 0:1024]
                    t.wait_ge(swq[par].h, wo_t0[par] + (mo // 2 + 1) * 16)
                    if mo >= 2:
                        t.wait_ge(s_dve.h, dveE[mo - 2])
                    last = None
                    for mi in range(HEADS):
                        for nq in range(2):
                            last = t.matmul(
                                psW[:, nq * 512:(nq + 1) * 512],
                                wb[:, par * 2048 + mi * 128: par * 2048 + (mi + 1) * 128],
                                attO3[:, mi, nq * 512:(nq + 1) * 512],
                                start=(mi == 0), stop=(mi == HEADS - 1))
                    s_pe.inc(last, 1)
                    t.wait_ge(s_dve.h, dveE[mo])
                    for b in range(2):
                        if mo >= 1:
                            t.wait_ge(s_dve.h, dveA[(mo - 1, b)])
                        last = None
                        for i in range(4):
                            tt = b * 4 + i
                            last = t.transpose(
                                psT(b)[:, i * 128:(i + 1) * 128],
                                o_sb[:, par * 1024 + tt * 128: par * 1024 + tt * 128 + 128],
                                identf[:])
                        s_pe.inc(last, 1)
                    if PREP4 and mo >= 1:
                        emit_sb(t, mo - 1)
                if PREP4:
                    emit_sb(t, 15)

            @block.vector
            def _(v):
                v.wait_ge(s_dve.h, dve4 + 32)
                for mo in range(HEADS):
                    par = mo % 2
                    psW = (psA if par == 0 else psB)[:, 0:1024]
                    v.wait_ge(s_pe.h, peW[mo])
                    s_dve.inc(v.tensor_copy(o_sb[:, par * 1024:(par + 1) * 1024], psW), 1)
                    for b in range(2):
                        v.wait_ge(s_pe.h, peT[(mo, b)])
                        s_dve.inc(v.tensor_tensor(
                            xres3[:, b * 4:(b + 1) * 4, mo * 128:(mo + 1) * 128],
                            xres3[:, b * 4:(b + 1) * 4, mo * 128:(mo + 1) * 128],
                            psT(b).rearrange("p (t e) -> p t e", e=128), ALU.add), 1)
                    if PREP4:
                        if mo >= 1:
                            for b in range(4):
                                g = 4 * (mo - 1) + b
                                v.wait_ge(s_pe.h, peSB[g])
                                if g >= 2:
                                    v.wait_ge(spub4[g % 2].h, pubSB(g))
                                v.wait_ge(s_dve.h, s_dve.n)
                                s_dve.inc(v.tensor_copy(
                                    wstg[:, (g % 2) * 1024:(g % 2 + 1) * 1024],
                                    PSL(g % 2)), 1)
                        for i in range(SL):
                            s = mo * SL + i
                            v.wait_ge(s_act.h, actB[s])
                            if s >= 8:
                                v.wait_ge(s_pe.h, peSB[2 * ((s - 8) // 4) + 1])
                            s_dve.inc(v.tensor_scalar(
                                wpu8[:, (s % 2) * 512:(s % 2 + 1) * 512],
                                wpt32[:, (s % 4) * 512:(s % 4 + 1) * 512],
                                240.0, -240.0, ALU.min, ALU.max), 1)
                            s_dve.inc(v.tensor_scalar(
                                wpcl[:, (s % 2) * 512:(s % 2 + 1) * 512],
                                wpb8[:, (s % 4) * 512:(s % 4 + 1) * 512],
                                120.0, -120.0, ALU.min, ALU.max), 1)
                            v.wait_ge(s_dve.h, s_dve.n)
                            s_dve.inc(v.tensor_tensor(
                                wpd[:, 0:512], wpb8[:, (s % 4) * 512:(s % 4 + 1) * 512],
                                wpcl[:, (s % 2) * 512:(s % 2 + 1) * 512], ALU.subtract), 1)
                            v.wait_ge(s_dve.h, s_dve.n)
                            s_dve.inc(v.scalar_tensor_tensor(
                                wpq[:, (s % 8) * 512:(s % 8 + 1) * 512],
                                wpd[:, 0:512], 2.0,
                                wpu8[:, (s % 2) * 512:(s % 2 + 1) * 512],
                                ALU.mult, ALU.add), 1)
                if PREP4:
                    for b in range(4):
                        g = 60 + b
                        v.wait_ge(s_pe.h, peSB[g])
                        v.wait_ge(spub4[g % 2].h, pubSB(g))
                        v.wait_ge(s_dve.h, s_dve.n)
                        s_dve.inc(v.tensor_copy(
                            wstg[:, (g % 2) * 1024:(g % 2 + 1) * 1024],
                            PSL(g % 2)), 1)

            if PREP4:
                @block.scalar
                def _(sc):
                    for mo in range(16):
                        for i in range(SL):
                            s = mo * SL + i
                            sc.wait_ge(sldw[s % 3].h,
                                       sld4_0[s % 3] + (s // 3 + 1) * 16)
                            if s == 0:
                                sc.wait_ge(s_dve.h, P3T["scl"])
                            if s >= 4:
                                sc.wait_ge(s_dve.h, dveU[s - 4])
                            s_act.inc(sc.activation(
                                wpt32[:, (s % 4) * 512:(s % 4 + 1) * 512],
                                wpin[:, (s % 3) * 512:(s % 3 + 1) * 512],
                                AF.Identity, bias=0.0, scale=scl[:, 4:5]), 1)
                            if s >= 4:
                                sc.wait_ge(s_dve.h, dveU[s - 4] + 2)
                            sc.wait_ge(s_act.h, s_act.n)
                            s_act.inc(sc.activation(
                                wpb8[:, (s % 4) * 512:(s % 4 + 1) * 512],
                                wpt32[:, (s % 4) * 512:(s % 4 + 1) * 512],
                                AF.Copy, scale=0.5), 1)

            actL0, dveL0 = s_act.n, s_dve.n
            INV = 1.0 / E

            @block.scalar
            def _(sc):
                sc.wait_ge(s_dve.h, dveA[(15, 1)])
                for tt in range(NT):
                    s_act.inc(sc.activation(o_sb[:, 0:E], xres3[:, tt, :], AF.Square,
                                            accum_out=lncol[:, 0:1]), 1)
                    sc.wait_ge(s_dve.h, dveL0 + 9 * tt + 4)
                    s_act.inc(sc.activation(lncol[:, 5:6], lncol[:, 4:5],
                                            AF.Sqrt, bias=cns[:, 1:2]), 1)

            @block.vector
            def _(v):
                def dr():
                    v.wait_ge(s_dve.h, s_dve.n)
                for tt in range(NT):
                    v.wait_ge(s_act.h, actL0 + 2 * tt + 1)
                    dr()
                    s_dve.inc(v.reduce_sum(lncol[:, 1:2], xres3[:, tt, :], AX.X), 1)
                    dr()
                    s_dve.inc(v.tensor_scalar(lncol[:, 2:3], lncol[:, 1:2], INV,
                                              None, ALU.mult), 1)
                    dr()
                    s_dve.inc(v.tensor_tensor(lncol[:, 3:4], lncol[:, 2:3],
                                              lncol[:, 2:3], ALU.mult), 1)
                    dr()
                    s_dve.inc(v.scalar_tensor_tensor(lncol[:, 4:5], lncol[:, 0:1], INV,
                                                     lncol[:, 3:4], ALU.mult,
                                                     ALU.subtract), 1)
                    v.wait_ge(s_act.h, actL0 + 2 * tt + 2)
                    s_dve.inc(v.reciprocal(lncol[:, 6:7], lncol[:, 5:6]), 1)
                    dr()
                    s_dve.inc(v.tensor_scalar(lncol[:, 7:8], lncol[:, 2:3],
                                              lncol[:, 6:7], -1.0, ALU.mult,
                                              ALU.mult), 1)
                    dr()
                    s_dve.inc(v.tensor_scalar(xres3[:, tt, :], xres3[:, tt, :],
                                              lncol[:, 6:7], lncol[:, 7:8],
                                              ALU.mult, ALU.add), 1)
                    dr()
                    s_dve.inc(v.reduce_max(lncol[:, 3:4], xres3[:, tt, :], AX.X,
                                           apply_absolute_value=True), 1)
                    dr()
                    s_dve.inc(v.tensor_max(smax[:, 0:1], smax[:, 0:1],
                                           lncol[:, 3:4]), 1)


        if N_PHASES >= 5:
            def row_from_col(cols_lo, cols_n, dst_lo):
                @block.tensor
                def _(t):
                    t.wait_ge(s_dve.h, s_dve.n)
                    s_pe.inc(t.transpose(psA[0:cols_n, 0:128],
                                         smax[:, cols_lo:cols_lo + cols_n], identf[:]), 1)
                @block.vector
                def _(v):
                    v.wait_ge(s_pe.h, s_pe.n)
                    s_dve.inc(v.reduce_max(lncol[0:cols_n, 0:1],
                                           psA[0:cols_n, 0:128], AX.X), 1)
                @block.tensor
                def _(t):
                    t.wait_ge(s_dve.h, s_dve.n)
                    s_pe.inc(t.transpose(psB[0:1, 0:cols_n],
                                         lncol[0:cols_n, 0:1],
                                         identf[0:cols_n, 0:cols_n]), 1)
                @block.vector
                def _(v):
                    v.wait_ge(s_pe.h, s_pe.n)
                    v.wait_ge(s_dve.h, s_dve.n)
                    s_dve.inc(v.tensor_copy(srow[0:1, dst_lo:dst_lo + cols_n],
                                            psB[0:1, 0:cols_n]), 1)

            def scales_from_row(src_lo, n, scl_lo, cc, pre=None):
                @block.gpsimd
                def _(g):
                    if pre is not None:
                        pre(g)
                    g.wait_ge(s_dve.h, s_dve.n)
                    s_ccd.inc(g.dma_start(out=cc, in_=srow[0:1, 0:16]), 16)
                    g.wait_ge(s_ccd.h, s_ccd.n)
                    s_cc.inc(g.collective_compute(
                        "AllReduce", ALU.max, rg, [cc], [cc],
                        cc_dim="Partition", unique_tensors="No"), 1)
                    g.wait_ge(s_cc.h, s_cc.n)
                    s_ccd.inc(g.dma_start(out=srow[0:1, 16:32], in_=cc), 16)
                @block.vector
                def _(v):
                    v.wait_ge(s_ccd.h, s_ccd.n)
                    for i in range(n):
                        v.wait_ge(s_dve.h, s_dve.n)
                        s_dve.inc(v.tensor_scalar(srow[0:1, 32 + 2 * i:33 + 2 * i],
                                                  srow[0:1, 16 + src_lo + i:17 + src_lo + i],
                                                  1.0 / 448.0, 1e-6, ALU.mult, ALU.max), 1)
                        v.wait_ge(s_dve.h, s_dve.n)
                        s_dve.inc(v.reciprocal(srow[0:1, 32 + 2 * i:33 + 2 * i],
                                               srow[0:1, 32 + 2 * i:33 + 2 * i]), 1)
                        v.wait_ge(s_dve.h, s_dve.n)
                        s_dve.inc(v.tensor_scalar(srow[0:1, 33 + 2 * i:34 + 2 * i],
                                                  srow[0:1, 32 + 2 * i:33 + 2 * i],
                                                  0.5, None, ALU.mult), 1)
                @block.tensor
                def _(t):
                    t.wait_ge(s_dve.h, s_dve.n)
                    s_pe.inc(t.matmul(psB[:, 256:256 + 2 * n],
                                      identf[0:1, 0:1].broadcast_to((1, 128)),
                                      srow[0:1, 32:32 + 2 * n],
                                      start=True, stop=True), 1)
                @block.vector
                def _(v):
                    v.wait_ge(s_pe.h, s_pe.n)
                    s_dve.inc(v.tensor_copy(scl[:, scl_lo:scl_lo + 2 * n],
                                            psB[:, 256:256 + 2 * n]), 1)

        if N_PHASES >= 5:
            esD.close()
            esC.close()
            esA.close()
            x8tok = SB(esE, "x8tok", [128, NT * E], BF16)
            x8f = SB(esE, "x8f", [128, NE * TOWN], BF16)
            xq32 = SB(esE, "xq32", [128, E], F32)
            q8u = SB(esE, "q8u", [128, E], FP8)
            q8p = SB(esE, "q8p", [128, E], FP8)
            q8n = SB(esE, "q8n", [128, E], FP8)
            s1f = SB(esE, "s1f", [128, E], F32)
            wfl = SB(esE, "wfl", [128, 2 * 16 * 512], BF16)
            rl = SB(esE, "rl", [128, 2 * 1024], F32)
            h_sb = SB(esE, "h_sb", [128, 4 * 1024], F32)
            x8t3 = x8tok[:].rearrange("p (t e) -> p t e", e=E)
            x8f3 = x8f[:].rearrange("p (ec t) -> p ec t", t=TOWN)
            wfl3 = wfl[:].rearrange("p (s a j) -> p s a j", s=2, a=NE)

            row_from_col(0, 1, 0)
            scales_from_row(0, 1, 0, cc_x,
                            pre=lambda g: g.wait_ge(s_wt.h, s_wt.n))

            @block.vector
            def _(v):
                v.wait_ge(s_dve.h, s_dve.n)
                s_dve.inc(v.tensor_tensor(scl[:, 5:6], scl[:, 0:1], scl[:, 4:5],
                                          ALU.mult), 1)
                v.wait_ge(s_dve.h, s_dve.n)
                s_dve.inc(v.reciprocal(scl[:, 5:6], scl[:, 5:6]), 1)

            act6, dve6 = s_act.n, s_dve.n

            @block.scalar
            def _(sc):
                sc.wait_ge(s_dve.h, dve6)
                for tt in range(NT):
                    if tt >= 1:
                        sc.wait_ge(s_dve.h, dve6 + 5 * tt)
                    sc.wait_ge(s_act.h, s_act.n)
                    s_act.inc(sc.activation(xq32[:], xres3[:, tt, :], AF.Identity,
                                            bias=0.0, scale=scl[:, 0:1]), 1)

            @block.vector
            def _(v):
                v.wait_ge(s_dve.h, dve6)
                for tt in range(NT):
                    v.wait_ge(s_act.h, act6 + tt + 1)
                    s_dve.inc(v.tensor_scalar(q8u[:], xq32[:], 240.0, -240.0,
                                              ALU.min, ALU.max), 1)
                    s_dve.inc(v.tensor_scalar(q8p[:], xq32[:], 0.5, 120.0,
                                              ALU.mult, ALU.max), 1)
                    s_dve.inc(v.tensor_scalar(q8n[:], xq32[:], 0.5, -120.0,
                                              ALU.mult, ALU.min), 1)
                    v.wait_ge(s_dve.h, s_dve.n)
                    s_dve.inc(v.tensor_tensor(s1f[:], q8p[:], q8n[:], ALU.add), 1)
                    v.wait_ge(s_dve.h, s_dve.n)
                    s_dve.inc(v.scalar_tensor_tensor(x8t3[:, tt, :], s1f[:], 2.0,
                                                     q8u[:], ALU.mult, ALU.add), 1)

            pe6, dve6b = s_pe.n, s_dve.n

            @block.tensor
            def _(t):
                t.wait_ge(s_dve.h, dve6b)
                for u in range(2 * NT):
                    tt, gi = divmod(u, 2)
                    ps = (psA if u % 2 == 0 else psB).bitcast(BF16)
                    if u >= 2:
                        t.wait_ge(s_dve.h, dve6b + u - 1)
                    last = None
                    for j in range(8):
                        ec = gi * 8 + j
                        last = t.transpose(ps[:, j * 128:(j + 1) * 128],
                                           x8t3[:, tt, ec * 128:(ec + 1) * 128],
                                           ident[:])
                    s_pe.inc(last, 1)

            @block.vector
            def _(v):
                for u in range(2 * NT):
                    tt, gi = divmod(u, 2)
                    ps = (psA if u % 2 == 0 else psB).bitcast(BF16)
                    v.wait_ge(s_pe.h, pe6 + u + 1)
                    s_dve.inc(v.tensor_copy(
                        x8f3[:, gi * 8:(gi + 1) * 8, tt * 128:(tt + 1) * 128],
                        ps[:, 0:1024].rearrange("p (j c) -> p j c", c=128)), 1)

        h_spill3 = h_spill.rearrange("a p t -> a p t")

        MG = {}
        MSLB = [(0, 3072), (1, 3072), (0, 3584), (1, 3584)]

        def MSL(sl):
            pt, off = MSLB[sl]
            return (psA if pt == 0 else psB).bitcast(BF16)[:, off:off + 512]

        def PSH(sl):
            pt, off = [(0, 0), (1, 0), (0, 512), (1, 512)][sl]
            return (psA if pt == 0 else psB)[:, off:off + 512]

        def phase_mm1(hf):
            pe7, act7, dve7 = s_pe.n, s_act.n, s_dve.n
            wq_t0 = [s_wf0.n, s_wf1.n]
            sp_t0 = [s.n for s in ssp]
            t0 = hf * 512
            sldm = [s_txs[0][2], s_txs[0][3], smk[0], smk[1]]
            if hf == 0:
                MG["ld0"] = [s.n for s in sldm]
                MG["pub0"] = [s_wos[1].n, s_wos[3].n]
                MG["t"] = {}; MG["b"] = {}; MG["u"] = {}; MG["q"] = {}
                MG["tr"] = {}; MG["ev"] = {}
            mt, mb, mu, mq = MG["t"], MG["b"], MG["u"], MG["q"]
            mtr, mev = MG["tr"], MG["ev"]
            wpq6 = x8tok[:, 512:512 + 8 * 512]
            wd6 = x8tok[:, 0:512]
            mstg = x8tok[:, 4608:4608 + 2 * 512]

            pe_mm = {}; act_rl = {}
            c = pe7
            for jc in range(NJ):
                c += 1; pe_mm[jc] = c
                s_t = hf * 64 + jc - 4
                if s_t >= 0:
                    c += 1; mtr[s_t] = c
            if hf == 1:
                for s_t in range(124, 128):
                    c += 1; mtr[s_t] = c
            dve_rm = {}
            c = dve7
            for jc in range(NJ):
                s = hf * 64 + jc
                c += 1; dve_rm[jc] = c
                c += 1
                s_t = hf * 64 + jc - 4
                if s_t >= 0:
                    c += 1; mev[s_t] = c
                c += 1; mu[s] = c
                c += 3; mq[s] = c
            if hf == 1:
                for s_t in range(124, 128):
                    c += 1; mev[s_t] = c
            act_sq = {}
            c = act7
            for jc in range(NJ):
                s = hf * 64 + jc
                c += 1; act_rl[jc] = c
                c += 1; act_sq[jc] = c
                c += 1; mt[s] = c
                c += 1; mb[s] = c

            spubM = [s_wos[1], s_wos[3]]

            def mpub(s_t):
                return MG["pub0"][s_t % 2] + 16 * (s_t // 2)

            @block.sync
            def _(sync):
                sync.wait_ge(s_cc.h, s_cc.n)
                sync.wait_ge(s_dve.h, dve7)
                wpr = wproj_half.rearrange("(a p) (c q) -> p a c q", p=128, q=512)
                par32b = nc.snap(nc.partition_id([mybir.EngineType.SP]) % 2 * 32,
                                 engines=[mybir.EngineType.SP],
                                 min_val=0, max_val=32)

                def wproj_copy(a4):
                    sync.wait_ge(spubM[0].h, spubM[0].n)
                    sync.wait_ge(spubM[1].h, spubM[1].n)
                    sync.wait_ge(s_wt.h, s_wt.n)
                    s_wt.inc(sync.dma_start(
                        out=wprojT_q[a4 * 4:(a4 + 1) * 4, :,
                                     bass.ds(par32b, 32), :],
                        in_=wprojT_loc[a4 * 4:(a4 + 1) * 4]), 16)

                def pub(s_t):
                    ebp, jqp = divmod(s_t, 8)
                    sync.wait_ge(s_dve.h, mev[s_t])
                    spubM[s_t % 2].inc(sync.dma_start(
                        out=wprojT_loc[ebp][:, jqp * 4:(jqp + 1) * 4, :],
                        in_=mstg[:, (s_t % 2) * 512:(s_t % 2 + 1) * 512].rearrange(
                            "p (i j) -> p i j", j=128)), 16)

                for jc in range(NJ):
                    s = hf * 64 + jc
                    if s >= 4:
                        sync.wait_ge(s_act.h, mt[s - 4])
                    eb, jq = divmod(s, 8)
                    sldm[s % 4].inc(sync.dma_start(
                        out=s1f[:, (s % 4) * 512:(s % 4 + 1) * 512],
                        in_=wpr[:, eb, jq, :]), 16)
                    s_t = hf * 64 + jc - 4
                    if s_t >= 0:
                        pub(s_t)
                        if s_t % 32 == 31:
                            wproj_copy(s_t // 32)
                if hf == 1:
                    for s_t in range(124, 128):
                        pub(s_t)
                    wproj_copy(3)

            @block.tensor
            def _(t):
                t.wait_ge(s_dve.h, dve7)

                def tr(s_t):
                    t.wait_ge(s_dve.h, mq[s_t])
                    if s_t >= 2:
                        t.wait_ge(s_dve.h, mev[s_t - 2])
                    sl = MSL(s_t % 4)
                    last = None
                    for tq in range(4):
                        last = t.transpose(
                            sl[:, tq * 128:(tq + 1) * 128],
                            wpq6[:, (s_t % 8) * 512 + tq * 128:
                                 (s_t % 8) * 512 + (tq + 1) * 128],
                            ident[:])
                    s_pe.inc(last, 1)

                for jc in range(NJ):
                    slot = (jc // 4) % 2
                    psH = PSH(jc % 4)
                    t.wait_ge(swf[slot].h, wq_t0[slot] + (jc // 8 + 1) * 16)
                    if jc >= 4:
                        t.wait_ge(s_act.h, act_rl[jc - 4])
                    last = None
                    for dc in range(NE):
                        last = t.matmul(psH,
                                        wfl3[:, slot, dc,
                                             (jc % 4) * 128:(jc % 4) * 128 + 128],
                                        x8f3[:, dc, t0:t0 + 512],
                                        start=(dc == 0), stop=(dc == NE - 1))
                    s_pe.inc(last, 1)
                    s_t = hf * 64 + jc - 4
                    if s_t >= 0:
                        tr(s_t)
                if hf == 1:
                    for s_t in range(124, 128):
                        tr(s_t)

            @block.scalar
            def _(sc):
                for jc in range(NJ):
                    s = hf * 64 + jc
                    sl4 = jc % 4
                    sc.wait_ge(s_pe.h, pe_mm[jc])
                    s_act.inc(sc.activation(rl[:, (jc % 4) * 512:(jc % 4) * 512 + 512],
                                            PSH(jc % 4),
                                            AF.Relu, scale=scl[:, 5:6]), 1)
                    sc.wait_ge(ssp[sl4].h, sp_t0[sl4] + (jc // 4) * 16)
                    if jc >= 4:
                        sc.wait_ge(s_dve.h, dve_rm[jc - 4])
                    sc.wait_ge(s_act.h, s_act.n)
                    s_act.inc(sc.activation(h_sb[:, sl4 * 1024: sl4 * 1024 + 512],
                                            rl[:, (jc % 4) * 512:(jc % 4) * 512 + 512],
                                            AF.Square), 1)
                    sc.wait_ge(sldm[s % 4].h, MG["ld0"][s % 4] + (s // 4 + 1) * 16)
                    if s >= 4:
                        sc.wait_ge(s_dve.h, mu[s - 4])
                    s_act.inc(sc.activation(
                        xq32[:, (s % 4) * 512:(s % 4 + 1) * 512],
                        s1f[:, (s % 4) * 512:(s % 4 + 1) * 512],
                        AF.Identity, bias=0.0, scale=scl[:, 6:7]), 1)
                    if s >= 4:
                        sc.wait_ge(s_dve.h, mu[s - 4] + 2)
                    sc.wait_ge(s_act.h, s_act.n)
                    s_act.inc(sc.activation(
                        q8p[:, (s % 4) * 512:(s % 4 + 1) * 512],
                        xq32[:, (s % 4) * 512:(s % 4 + 1) * 512],
                        AF.Copy, scale=0.5), 1)

            @block.vector
            def _(v):
                for jc in range(NJ):
                    s = hf * 64 + jc
                    sl4 = jc % 4
                    v.wait_ge(s_act.h, act_sq[jc])
                    v.wait_ge(s_dve.h, s_dve.n)
                    s_dve.inc(v.reduce_max(lncol[:, 0:1],
                                           h_sb[:, sl4 * 1024: sl4 * 1024 + 512], AX.X), 1)
                    v.wait_ge(s_dve.h, s_dve.n)
                    s_dve.inc(v.tensor_max(smax[:, 1:2], smax[:, 1:2], lncol[:, 0:1]), 1)
                    s_t = hf * 64 + jc - 4
                    if s_t >= 0:
                        v.wait_ge(s_pe.h, mtr[s_t])
                        if s_t >= 2:
                            v.wait_ge(spubM[s_t % 2].h, mpub(s_t))
                        v.wait_ge(s_dve.h, s_dve.n)
                        s_dve.inc(v.tensor_copy(
                            mstg[:, (s_t % 2) * 512:(s_t % 2 + 1) * 512],
                            MSL(s_t % 4)), 1)
                    v.wait_ge(s_act.h, mb[s])
                    if s >= 8:
                        v.wait_ge(s_pe.h, mtr[s - 8])
                    s_dve.inc(v.tensor_scalar(
                        q8u[:, (s % 2) * 512:(s % 2 + 1) * 512],
                        xq32[:, (s % 4) * 512:(s % 4 + 1) * 512],
                        240.0, -240.0, ALU.min, ALU.max), 1)
                    s_dve.inc(v.tensor_scalar(
                        q8n[:, (s % 2) * 512:(s % 2 + 1) * 512],
                        q8p[:, (s % 4) * 512:(s % 4 + 1) * 512],
                        120.0, -120.0, ALU.min, ALU.max), 1)
                    v.wait_ge(s_dve.h, s_dve.n)
                    s_dve.inc(v.tensor_tensor(
                        wd6, q8p[:, (s % 4) * 512:(s % 4 + 1) * 512],
                        q8n[:, (s % 2) * 512:(s % 2 + 1) * 512], ALU.subtract), 1)
                    v.wait_ge(s_dve.h, s_dve.n)
                    s_dve.inc(v.scalar_tensor_tensor(
                        wpq6[:, (s % 8) * 512:(s % 8 + 1) * 512],
                        wd6, 2.0, q8u[:, (s % 2) * 512:(s % 2 + 1) * 512],
                        ALU.mult, ALU.add), 1)
                if hf == 1:
                    for s_t in range(124, 128):
                        v.wait_ge(s_pe.h, mtr[s_t])
                        v.wait_ge(spubM[s_t % 2].h, mpub(s_t))
                        v.wait_ge(s_dve.h, s_dve.n)
                        s_dve.inc(v.tensor_copy(
                            mstg[:, (s_t % 2) * 512:(s_t % 2 + 1) * 512],
                            MSL(s_t % 4)), 1)

            @block.gpsimd
            def _(g):
                g.wait_ge(s_cc.h, s_cc.n)
                for jc in range(NJ):
                    sl4 = jc % 4
                    if jc % 4 == 0:
                        jq4 = jc // 4
                        slot = jq4 % 2
                        if jq4 >= 2:
                            g.wait_ge(s_pe.h, pe_mm[4 * jq4 - 5])
                        swf[slot].inc(g.dma_start(
                            out=wfl3[:, slot, :, :],
                            in_=wfcT_q[:, :, jq4 * 512:(jq4 + 1) * 512].rearrange(
                                "a p j -> p a j")), 16)
                    g.wait_ge(s_act.h, act_sq[jc])
                    ssp[sl4].inc(g.dma_start(
                        out=h_spill3[jc, :, t0:t0 + 512],
                        in_=h_sb[:, sl4 * 1024: sl4 * 1024 + 512]), 16)

        if N_PHASES >= 6:
            phase_mm1(0)
            phase_mm1(1)

        if N_PHASES >= 7:
            esE.close()
            h8 = SB(esF, "h8", [128, NJ * 512], BF16)
            rb = SB(esF, "rb", [128, 2 * 512], F32)
            r8u = SB(esF, "r8u", [128, 512], FP8)
            r8b = SB(esF, "r8b", [128, 512], FP8)
            rc = SB(esF, "rc", [128, 512], BF16)
            wpj = SB(esF, "wpj", [128, 2 * 64 * 128], BF16)
            o2T = SB(esF, "o2T", [128, 2 * 512], F32)
            sqd = SB(esF, "sqd", [128, E], F32)
            h83 = h8[:].rearrange("p (a t) -> p a t", t=512)
            wpj3 = wpj[:].rearrange("p (s a j) -> p s a j", s=2, a=64)

            def hbar(g):
                g.wait_ge(s_wt.h, s_wt.n)
                for s4 in range(4):
                    g.wait_ge(ssp[s4].h, ssp[s4].n)

            row_from_col(1, 1, 2)
            scales_from_row(2, 1, 2, cc_h, pre=hbar)

            @block.vector
            def _(v):
                v.wait_ge(s_dve.h, s_dve.n)
                s_dve.inc(v.tensor_tensor(scl[:, 7:8], scl[:, 2:3], scl[:, 6:7],
                                          ALU.mult), 1)
                v.wait_ge(s_dve.h, s_dve.n)
                s_dve.inc(v.reciprocal(scl[:, 7:8], scl[:, 7:8]), 1)

        def phase_h8(hf):
            dve8, pl8 = s_dve.n, s_pl.n
            rb_t0 = [srb[0].n, srb[1].n]
            t0 = hf * 512

            @block.sync
            def _(sync):
                sync.wait_ge(s_cc.h, s_cc.n)
                sync.wait_ge(s_dve.h, dve8)
                for jc in range(NJ):
                    slot = jc % 2
                    if jc >= 2:
                        sync.wait_ge(s_dve.h, dve8 + (jc - 2) * 4 + 2)
                    srb[slot].inc(sync.dma_start(
                        out=rb[:, slot * 512:(slot + 1) * 512],
                        in_=h_spill3[jc, :, t0:t0 + 512]), 16)

            @block.vector
            def _(v):
                for jc in range(NJ):
                    slot = jc % 2
                    v.wait_ge(srb[slot].h, rb_t0[slot] + (jc // 2 + 1) * 16)
                    v.wait_ge(s_dve.h, s_dve.n)
                    s_dve.inc(v.tensor_scalar(r8u[:], rb[:, slot * 512:(slot + 1) * 512],
                                              scl[:, 2:3], 240.0, ALU.mult, ALU.min), 1)
                    s_dve.inc(v.tensor_scalar(r8b[:], rb[:, slot * 512:(slot + 1) * 512],
                                              scl[:, 3:4], 120.0, ALU.mult, ALU.max), 1)
                    v.wait_ge(s_dve.h, s_dve.n)
                    s_dve.inc(v.tensor_scalar(rc[:], r8b[:], 120.0, 2.0,
                                              ALU.subtract, ALU.mult), 1)
                    v.wait_ge(s_dve.h, s_dve.n)
                    s_dve.inc(v.tensor_tensor(h83[:, jc, :], r8u[:], rc[:], ALU.add), 1)

        def phase_mm2(hf):
            pe9, act9, dve9, pl9 = s_pe.n, s_act.n, s_dve.n, s_pl.n
            wq_t0 = [s_h0.n, s_h1.n]
            t0 = hf * 512

            @block.sync
            def _(sync):
                for ec in range(NE):
                    slot = ec % 2
                    if ec >= 2:
                        sync.wait_ge(s_pe.h, pe9 + 5 * (ec - 2) + 1)
                    else:
                        sync.wait_ge(s_pe.h, pe9)
                    shw[slot].inc(sync.dma_start(
                        out=wpj3[:, slot, :, :],
                        in_=wprojT_q[ec]), 16)

            def psT2(slot):
                return (psA if slot == 0 else psB)[:, 1536:1664]

            @block.tensor
            def _(t):
                t.wait_ge(s_dve.h, dve9)
                for ec in range(NE):
                    par = ec % 2
                    psO2 = (psA if par == 0 else psB)[:, 0:512]
                    if ec >= 1:
                        t.wait_ge(s_dve.h, dve9 + 5 * (ec - 2) + 1 if ec >= 2 else dve9)
                    slot = ec % 2
                    t.wait_ge(shw[slot].h, wq_t0[slot] + (ec // 2 + 1) * 16)
                    for jc in range(NJ):
                        last = t.matmul(psO2, wpj3[:, slot, jc, :],
                                        h83[:, jc, :],
                                        start=(jc == 0), stop=(jc == NJ - 1))
                    s_pe.inc(last, 1)
                    t.wait_ge(s_dve.h, dve9 + 5 * ec + 1)
                    for tt in range(4):
                        slot = tt % 2
                        if tt >= 2:
                            t.wait_ge(s_dve.h, dve9 + 5 * ec + 2 + (tt - 2))
                        elif ec >= 1:
                            t.wait_ge(s_dve.h, dve9 + 5 * (ec - 1) + 2 + (tt + 2))
                        s_pe.inc(t.transpose(
                            psT2(slot),
                            o2T[:, (ec % 2) * 512 + tt * 128: (ec % 2) * 512 + tt * 128 + 128],
                            identf[:]), 1)

            @block.vector
            def _(v):
                for ec in range(NE):
                    par = ec % 2
                    psO2 = (psA if par == 0 else psB)[:, 0:512]
                    v.wait_ge(s_pe.h, pe9 + 5 * ec + 1)
                    s_dve.inc(v.tensor_copy(o2T[:, par * 512:(par + 1) * 512], psO2), 1)
                    for tt in range(4):
                        slot = tt % 2
                        v.wait_ge(s_pe.h, pe9 + 5 * ec + 2 + tt)
                        s_dve.inc(v.scalar_tensor_tensor(
                            xres3[:, hf * 4 + tt, ec * 128:(ec + 1) * 128],
                            psT2(slot), scl[:, 7:8],
                            xres3[:, hf * 4 + tt, ec * 128:(ec + 1) * 128],
                            ALU.mult, ALU.add), 1)

            actL, dveL = s_act.n, s_dve.n

            @block.scalar
            def _(sc):
                sc.wait_ge(s_dve.h, dveL)
                for tt4 in range(4):
                    tt = hf * 4 + tt4
                    sc.wait_ge(s_act.h, s_act.n)
                    s_act.inc(sc.activation(sqd[:], xres3[:, tt, :], AF.Square,
                                            accum_out=lncol[:, 0:1]), 1)
                    sc.wait_ge(s_dve.h, dveL + 7 * tt4 + 4)
                    s_act.inc(sc.activation(lncol[:, 5:6], lncol[:, 4:5],
                                            AF.Sqrt, bias=cns[:, 1:2]), 1)

            @block.vector
            def _(v):
                def dr():
                    v.wait_ge(s_dve.h, s_dve.n)
                for tt4 in range(4):
                    tt = hf * 4 + tt4
                    v.wait_ge(s_act.h, actL + 2 * tt4 + 1)
                    dr()
                    s_dve.inc(v.reduce_sum(lncol[:, 1:2], xres3[:, tt, :], AX.X), 1)
                    dr()
                    s_dve.inc(v.tensor_scalar(lncol[:, 2:3], lncol[:, 1:2], 1.0 / E,
                                              None, ALU.mult), 1)
                    dr()
                    s_dve.inc(v.tensor_tensor(lncol[:, 3:4], lncol[:, 2:3],
                                              lncol[:, 2:3], ALU.mult), 1)
                    dr()
                    s_dve.inc(v.scalar_tensor_tensor(lncol[:, 4:5], lncol[:, 0:1],
                                                     1.0 / E, lncol[:, 3:4],
                                                     ALU.mult, ALU.subtract), 1)
                    v.wait_ge(s_act.h, actL + 2 * tt4 + 2)
                    s_dve.inc(v.reciprocal(lncol[:, 6:7], lncol[:, 5:6]), 1)
                    dr()
                    s_dve.inc(v.tensor_scalar(lncol[:, 7:8], lncol[:, 2:3],
                                              lncol[:, 6:7], -1.0, ALU.mult,
                                              ALU.mult), 1)
                    dr()
                    s_dve.inc(v.tensor_scalar(xres3[:, tt, :], xres3[:, tt, :],
                                              lncol[:, 6:7], lncol[:, 7:8],
                                              ALU.mult, ALU.add), 1)
                    dr()

            @block.sync
            def _(sync):
                sync.wait_ge(s_dve.h, s_dve.n)
                s_st.inc(sync.dma_start(
                    out=y.rearrange("(t p) e -> p t e", p=128)[:, hf * 4:(hf + 1) * 4, :],
                    in_=xres3[:, hf * 4:(hf + 1) * 4, :]), 16)

        if N_PHASES >= 7:
            phase_h8(0)
        if N_PHASES >= 8:
            phase_mm2(0)
            phase_h8(1)
            phase_mm2(1)
            esF.close()

        if DEBUG and N_PHASES >= 2:
            @block.sync
            def _(sync):
                sync.wait_ge(s_dve.h, s_dve.n)
                sync.wait_ge(s_act.h, s_act.n)
                s_st.inc(sync.dma_start(
                    out=dbg["dbg_a"],
                    in_=(attO[:] if N_PHASES >= 3 else Qs[:])), 16)
                s_st.inc(sync.dma_start(out=dbg["dbg_b"], in_=Ks[:]), 16)
                s_st.inc(sync.dma_start(out=dbg["dbg_c"], in_=Vs[:]), 16)
                s_st.inc(sync.dma_start(out=dbg["dbg_f"], in_=xres[:]), 16)

        @block.sync
        def _(sync):
            sync.wait_ge(s_ld.h, s_ld.n)
            sync.wait_ge(s_dve.h, s_dve.n)
            sync.wait_ge(s_pe.h, s_pe.n)
            sync.wait_ge(s_act.h, s_act.n)
            sync.wait_ge(s_pl.h, s_pl.n)
            if N_PHASES < 8:
                s_st.inc(sync.dma_start(
                    out=y.rearrange("(t p) e -> p t e", p=128), in_=xres3), 16)
            sync.wait_ge(s_st.h, s_st.n)

        if N_PHASES < 5:
            esD.close()
            esC.close()
            esA.close()
        if N_PHASES < 3:
            esB.close()
        if 5 <= N_PHASES < 7:
            esE.close()
        if N_PHASES == 7:
            esF.close()

    return nc


def kernel(L0_attn_x, L0_attn_Wq, L0_attn_Wk, L0_attn_Wv, L0_attn_Wo,
           L0_attn_bo, L0_ffn_w_fc, L0_ffn_w_proj):
    x = np.ascontiguousarray(np.asarray(L0_attn_x, dtype=np.float32)[0])
    Wq = np.ascontiguousarray(L0_attn_Wq, dtype=np.float32)
    Wk = np.ascontiguousarray(L0_attn_Wk, dtype=np.float32)
    Wv = np.ascontiguousarray(L0_attn_Wv, dtype=np.float32)
    Wo = np.ascontiguousarray(L0_attn_Wo, dtype=np.float32)
    bo = np.ascontiguousarray(L0_attn_bo, dtype=np.float32).reshape(1, M)
    wfc = np.ascontiguousarray(L0_ffn_w_fc, dtype=np.float32)
    wproj = np.ascontiguousarray(L0_ffn_w_proj, dtype=np.float32)

    nc = build_nc()
    in_maps = []
    for c in range(NCORES):
        half = c % 2
        in_maps.append({
            "x_own": x[c * TOWN:(c + 1) * TOWN],
            "x_halo": (np.zeros((TOWN, E), np.float32) if c == 0
                       else np.ascontiguousarray(x[(c - 1) * TOWN:c * TOWN])),
            "Wq": Wq, "Wk": Wk, "Wv": Wv, "Wo": Wo, "bo": bo,
            "wfc_half": np.ascontiguousarray(wfc[half * (FF // 2):(half + 1) * (FF // 2)]),
            "wproj_half": np.ascontiguousarray(wproj[:, half * (FF // 2):(half + 1) * (FF // 2)]),
            "wfc_8": np.ascontiguousarray(wfc[c * (FF // 8):(c + 1) * (FF // 8)]),
            "wproj_8": np.ascontiguousarray(wproj[:, c * (FF // 8):(c + 1) * (FF // 8)]),
            "masks": _mask_host(c),
        })
    res = run_bass_kernel_spmd(nc, in_maps, core_ids=list(range(NCORES)))
    out = np.concatenate([res.results[c]["y"] for c in range(NCORES)], axis=0)
    return out.reshape(1, T, E).astype(np.float32)


# version 12
# speedup vs baseline: 1.0479x; 1.0264x over previous
import math
import os
import numpy as np
import ml_dtypes
import concourse.bass as bass
import concourse.mybir as mybir
from concourse.bass_utils import run_bass_kernel_spmd

F32 = mybir.dt.float32
BF16 = mybir.dt.bfloat16
FP8 = mybir.dt.float8e4
AF = mybir.ActivationFunctionType
ALU = mybir.AluOpType
AX = mybir.AxisListType

NCORES = 8
T, TOWN, E, M, KV, FF = 8192, 1024, 2048, 2048, 512, 8192
HEADS, GROUPS, D = 16, 4, 128
NT, NE, NJ = TOWN // 128, E // 128, FF // 128
SCALE_QK = 1.0 / math.sqrt(D)
EXP_BIAS = -30.0
LN_EPS = 1e-5

N_PHASES = int(os.environ.get("KPHASES", "99"))
DEBUG = os.environ.get("KDEBUG", "0") == "1"


class Sem:
    def __init__(self, nc, name):
        self.h = nc.alloc_semaphore(name)
        self.n = 0

    def inc(self, inst, k=1):
        inst.then_inc(self.h, k)
        self.n += k
        return inst


def _mask_host(core):
    out = np.zeros((NT, 128, 9 * 128), dtype=ml_dtypes.bfloat16)
    kr = np.arange(128)[:, None]
    qc = np.arange(128)[None, :]
    for m in range(NT):
        gm = NT * core + m
        for kt in range(9):
            jt = gm - 8 + kt
            if jt < 0:
                continue
            delta = (8 - kt) * 128
            allowed = (kr <= qc + delta) & (kr >= qc + delta - 1023)
            out[m, :, kt * 128:(kt + 1) * 128] = allowed.astype(ml_dtypes.bfloat16)
    return out


def build_nc():
    nc = bass.Bass("TRN2", target_bir_lowering=False, num_devices=NCORES)

    x_own = nc.dram_tensor("x_own", [TOWN, E], F32, kind="ExternalInput").ap()
    x_halo = nc.dram_tensor("x_halo", [TOWN, E], F32, kind="ExternalInput").ap()
    Wq_d = nc.dram_tensor("Wq", [E, M], F32, kind="ExternalInput").ap()
    Wk_d = nc.dram_tensor("Wk", [E, KV], F32, kind="ExternalInput").ap()
    Wv_d = nc.dram_tensor("Wv", [E, KV], F32, kind="ExternalInput").ap()
    Wo_d = nc.dram_tensor("Wo", [M, M], F32, kind="ExternalInput").ap()
    bo_d = nc.dram_tensor("bo", [1, M], F32, kind="ExternalInput").ap()
    wfc_half = nc.dram_tensor("wfc_half", [FF // 2, E], F32, kind="ExternalInput").ap()
    wproj_half = nc.dram_tensor("wproj_half", [E, FF // 2], F32, kind="ExternalInput").ap()
    wfc_8 = nc.dram_tensor("wfc_8", [FF // 8, E], F32, kind="ExternalInput").ap()
    wproj_8 = nc.dram_tensor("wproj_8", [E, FF // 8], F32, kind="ExternalInput").ap()
    masks_d = nc.dram_tensor("masks", [NT, 128, 9 * 128], BF16, kind="ExternalInput").ap()
    y = nc.dram_tensor("y", [TOWN, E], F32, kind="ExternalOutput").ap()

    h_spill = nc.dram_tensor("h_spill", [NJ, 128, TOWN], F32, kind="Internal").ap()
    wfcT_q = nc.dram_tensor("wfcT_q", [2, 8, 128, NE * 512], BF16, kind="Internal",
                            addr_space="Shared").ap()
    wprojT_q = nc.dram_tensor("wprojT_q", [NE, 128, 64, 128], BF16, kind="Internal",
                              addr_space="Shared").ap()
    wfcT_loc = nc.dram_tensor("wfcT_loc", [8, 128, NE * 512], BF16, kind="Internal").ap()
    wprojT_loc = nc.dram_tensor("wprojT_loc", [NE, 128, 32, 128], BF16, kind="Internal").ap()
    cc_w = nc.dram_tensor("cc_w", [1, 16], F32, kind="Internal").ap()
    cc_x = nc.dram_tensor("cc_x", [1, 16], F32, kind="Internal").ap()
    cc_h = nc.dram_tensor("cc_h", [1, 16], F32, kind="Internal").ap()
    rg = [list(range(NCORES))]

    from contextlib import ExitStack
    es = ExitStack()
    esA = ExitStack()
    esB = ExitStack()
    esC = ExitStack()
    esD = ExitStack()
    esE = ExitStack()
    esF = ExitStack()
    SB = lambda stk, name, shape, dt: stk.enter_context(nc.sbuf_tensor(name, shape, dt))

    xres = SB(es, "xres", [128, NT * E], F32)
    ident = SB(es, "ident", [128, 128], BF16)
    identf = SB(es, "identf", [128, 128], F32)
    ones = SB(es, "ones", [128, 128], BF16)
    scl = SB(es, "scl", [128, 8], F32)
    smax = SB(es, "smax", [128, 4], F32)
    srow = SB(es, "srow", [1, 128], F32)
    cns = SB(es, "cns", [128, 2], F32)
    lncol = SB(es, "lncol", [128, 8], F32)
    ccb = SB(es, "ccb", [1, 16], F32)

    psA = es.enter_context(nc.psum_tensor("psA", [128, 2048], F32))
    psB = es.enter_context(nc.psum_tensor("psB", [128, 2048], F32))

    s_ld = Sem(nc, "s_ld")
    s_const = Sem(nc, "s_const")
    s_x0 = Sem(nc, "s_x0")
    s_x1 = Sem(nc, "s_x1")
    s_wkv = Sem(nc, "s_wkv")
    s_wq0 = Sem(nc, "s_wq0")
    s_wq1 = Sem(nc, "s_wq1")
    s_pe = Sem(nc, "s_pe")
    s_act = Sem(nc, "s_act")
    s_dve = Sem(nc, "s_dve")
    s_pl = Sem(nc, "s_pl")
    s_st = Sem(nc, "s_st")
    s_wp = Sem(nc, "s_wp")
    s_wd = Sem(nc, "s_wd")
    s_wt = Sem(nc, "s_wt")
    s_cc = Sem(nc, "s_cc")
    s_mk = Sem(nc, "s_mk")
    s_sp = Sem(nc, "s_sp")
    s_rb = Sem(nc, "s_rb")
    s_wd1 = Sem(nc, "s_wd1")
    s_txs = [[Sem(nc, "s_tx%d_%d" % (p, k)) for k in range(4)] for p in range(2)]
    s_wos = [Sem(nc, "s_wo%d" % k) for k in range(4)]
    s_ccd = Sem(nc, "s_ccd")
    s_rb1 = Sem(nc, "s_rb1")
    s_sp1 = Sem(nc, "s_sp1")
    s_sp2 = Sem(nc, "s_sp2")
    s_sp3 = Sem(nc, "s_sp3")
    s_h0 = Sem(nc, "s_h0")
    s_h1 = Sem(nc, "s_h1")
    s_wf0 = Sem(nc, "s_wf0")
    s_wf1 = Sem(nc, "s_wf1")
    swf = [s_wf0, s_wf1]
    shw = [s_h0, s_h1]
    sx = [s_x0, s_x1]
    swq = [s_wq0, s_wq1]
    swd = [s_wd, s_wd1]
    ssp = [s_sp, s_sp1, s_sp2, s_sp3]
    srb = [s_rb, s_rb1]

    identt = nc.inline_tensor(np.eye(128, dtype=ml_dtypes.bfloat16).view(np.uint16), "idbf")
    identtf = nc.inline_tensor(np.eye(128, dtype=np.float32), "idf32")

    dbg = {}
    if DEBUG:
        for nm, shape in [("dbg_a", [128, HEADS * TOWN]), ("dbg_b", [128, GROUPS * 2048]),
                          ("dbg_c", [128, 16 * KV])]:
            dbg[nm] = nc.dram_tensor(nm, shape, BF16, kind="ExternalOutput").ap()
        dbg["dbg_f"] = nc.dram_tensor("dbg_f", [128, NT * E], F32, kind="ExternalOutput").ap()

    with es, nc.Block() as block:
        Qs = SB(esA, "Qs", [128, HEADS * TOWN], BF16)
        Ks = SB(esA, "Ks", [128, GROUPS * 2 * TOWN], BF16)
        Vs = SB(esA, "Vs", [128, 16 * KV], BF16)
        wb = SB(esA, "wb", [128, 2 * 2048], BF16)
        WKV = SB(esB, "WKV", [128, NE * 512], BF16)
        xstg = SB(esB, "xstg", [128, 2 * 1024], BF16)
        xT = SB(esB, "xT", [128, NE * TOWN], BF16)

        WKV3 = WKV[:].rearrange("p (ec c) -> p ec c", c=512)
        xT3 = xT[:].rearrange("p (ec t) -> p ec t", t=TOWN)
        Qs3 = Qs[:].rearrange("p (h t) -> p h t", t=TOWN)
        Ks3 = Ks[:].rearrange("p (g t) -> p g t", t=2 * TOWN)
        Vs3 = Vs[:].rearrange("p (l c) -> p l c", c=KV)
        xres3 = xres[:].rearrange("p (t e) -> p t e", e=E)

        @block.sync
        def _(sync):
            s_ld.inc(sync.dma_start(
                out=xres3, in_=x_own.rearrange("(t p) e -> p t e", p=128)), 16)

        @block.gpsimd
        def _(g):
            s_const.inc(g.dma_start(out=ident[:], in_=identt.ap().bitcast(BF16)), 16)
            s_const.inc(g.dma_start(out=identf[:], in_=identtf.ap()), 16)

        @block.vector
        def _(v):
            s_dve.inc(v.memset(ones[:], 1.0), 1)
            s_dve.inc(v.memset(smax[:], 0.0), 1)
            s_dve.inc(v.memset(cns[:, 0:1], EXP_BIAS), 1)
            s_dve.inc(v.memset(cns[:, 1:2], LN_EPS), 1)
            s_dve.inc(v.memset(srow[:], 0.0), 1)
            s_dve.inc(v.memset(scl[:], 0.0), 1)
            s_dve.inc(v.memset(lncol[:], 0.0), 1)

        def phase_proj(src_dram, half):
            import os
            SUB = int(os.environ.get("KP1SUB", "9"))
            pe0, dve0 = s_pe.n, s_dve.n
            x_tot0 = [s_x0.n, s_x1.n]

            @block.gpsimd
            def _(g):
                g.wait_ge(s_pe.h, pe0)
                s_wkv.inc(g.dma_start(
                    out=WKV3, in_=Wk_d.rearrange("(ec e) k -> e ec k", e=128)), 16)
                for u in range(2 * NT):
                    tt, gi = divmod(u, 2)
                    slot = u % 2
                    if u >= 2:
                        g.wait_ge(s_pe.h, pe0 + u - 1)
                    sx[slot].inc(g.dma_start(
                        out=xstg[:, slot * 1024:(slot + 1) * 1024],
                        in_=src_dram.rearrange("(t p) e -> p t e", p=128)[
                            :, tt, gi * 1024:(gi + 1) * 1024]), 16)
                if SUB >= 3:
                    g.wait_ge(s_pe.h, pe0 + 2 * NT + 8)
                    s_wkv.inc(g.dma_start(
                        out=WKV3, in_=Wv_d.rearrange("(ec e) k -> e ec k", e=128)), 16)

            @block.tensor
            def _(t):
                t.wait_ge(s_const.h, 32)
                t.wait_ge(s_dve.h, dve0)
                for u in range(2 * NT):
                    tt, gi = divmod(u, 2)
                    slot = u % 2
                    t.wait_ge(sx[slot].h, x_tot0[slot] + (u // 2 + 1) * 16)
                    ps = (psA if u % 2 == 0 else psB).bitcast(BF16)
                    if u >= 2:
                        t.wait_ge(s_dve.h, dve0 + u - 1)
                    last = None
                    for j in range(8):
                        last = t.transpose(
                            ps[:, j * 128:(j + 1) * 128],
                            xstg[:, slot * 1024 + j * 128: slot * 1024 + j * 128 + 128],
                            ident[:])
                    s_pe.inc(last, 1)

            @block.vector
            def _(v):
                for u in range(2 * NT):
                    tt, gi = divmod(u, 2)
                    ps = (psA if u % 2 == 0 else psB).bitcast(BF16)
                    v.wait_ge(s_pe.h, pe0 + u + 1)
                    s_dve.inc(v.tensor_copy(
                        xT3[:, gi * 8:(gi + 1) * 8, tt * 128:(tt + 1) * 128],
                        ps[:, 0:1024].rearrange("p (j c) -> p j c", c=128)), 1)

            pe1, dve1 = s_pe.n, s_dve.n

            wk_tot = s_wkv.n - (16 if SUB >= 3 else 0)
            wv_tot = s_wkv.n
            wq_tot0 = [s_wq0.n, s_wq1.n]
            if half == 1:
                @block.gpsimd
                def _(g):
                    for mc in range(HEADS):
                        slot = mc % 2
                        if mc >= 2:
                            g.wait_ge(s_pe.h, pe1 + 16 + mc - 1)
                        swq[slot].inc(g.dma_start(
                            out=wb[:, slot * 2048: slot * 2048 + 2048].rearrange(
                                "p (ec c) -> p ec c", c=128),
                            in_=Wq_d.rearrange("(ec e) m -> e ec m", e=128)[
                                :, :, mc * 128:(mc + 1) * 128]), 16)

            @block.tensor
            def _(t):
                t.wait_ge(s_dve.h, dve1)
                t.wait_ge(s_wkv.h, wk_tot)
                for gidx in range(8 if SUB >= 2 else 0):
                    kd, kq = divmod(gidx, 2)
                    ps = (psA if gidx % 2 == 0 else psB)[:, 0:512]
                    if gidx >= 2:
                        t.wait_ge(s_dve.h, dve1 + gidx - 1)
                    last = None
                    for ec in range(NE):
                        last = t.matmul(ps, WKV3[:, ec, kd * 128:(kd + 1) * 128],
                                        xT3[:, ec, kq * 512:(kq + 1) * 512],
                                        start=(ec == 0), stop=(ec == NE - 1))
                    s_pe.inc(last, 1)
                if SUB >= 3:
                    t.wait_ge(s_wkv.h, wv_tot)
                for tt in range(NT if SUB >= 3 else 0):
                    gidx = (8 if SUB >= 2 else 0) + tt
                    ps = (psA if gidx % 2 == 0 else psB)[:, 0:512]
                    t.wait_ge(s_dve.h, dve1 + gidx - 1)
                    last = None
                    for ec in range(NE):
                        last = t.matmul(ps, xT3[:, ec, tt * 128:(tt + 1) * 128],
                                        WKV3[:, ec, 0:512],
                                        start=(ec == 0), stop=(ec == NE - 1))
                    s_pe.inc(last, 1)
                if half == 1:
                    t.wait_ge(s_dve.h, dve1 + 16)
                    for mc in range(HEADS):
                        slot = mc % 2
                        psq = (psA if mc % 2 == 0 else psB)
                        t.wait_ge(swq[mc % 2].h, wq_tot0[mc % 2] + (mc // 2 + 1) * 16)
                        if mc >= 2:
                            t.wait_ge(s_dve.h, dve1 + 16 + (mc - 2) + 1)
                        wv = wb[:, slot * 2048: slot * 2048 + 2048].rearrange(
                            "p (ec c) -> p ec c", c=128)
                        last = None
                        for ec in range(NE):
                            for nq in range(2):
                                last = t.matmul(
                                    psq[:, nq * 512:(nq + 1) * 512],
                                    wv[:, ec, :],
                                    xT3[:, ec, nq * 512:(nq + 1) * 512],
                                    start=(ec == 0), stop=(ec == NE - 1))
                        s_pe.inc(last, 1)

            @block.vector
            def _(v):
                nk = 8 if SUB >= 2 else 0
                for gidx in range(nk):
                    kd, kq = divmod(gidx, 2)
                    ps = (psA if gidx % 2 == 0 else psB)[:, 0:512]
                    v.wait_ge(s_pe.h, pe1 + gidx + 1)
                    s_dve.inc(v.tensor_copy(
                        Ks3[:, kd, half * TOWN + kq * 512: half * TOWN + (kq + 1) * 512], ps), 1)
                for tt in range(NT if SUB >= 3 else 0):
                    gidx = nk + tt
                    ps = (psA if gidx % 2 == 0 else psB)[:, 0:512]
                    v.wait_ge(s_pe.h, pe1 + gidx + 1)
                    s_dve.inc(v.tensor_copy(Vs3[:, half * 8 + tt, :], ps), 1)
                if half == 1:
                    for mc in range(HEADS):
                        psq = (psA if mc % 2 == 0 else psB)
                        v.wait_ge(s_pe.h, pe1 + 16 + mc + 1)
                        s_dve.inc(v.tensor_copy(Qs3[:, mc, :], psq[:, 0:1024]), 1)

        if N_PHASES >= 1:
            phase_proj(x_halo, 0)
        if N_PHASES >= 2:
            phase_proj(x_own, 1)

        attO = None
        P3T = {}
        if N_PHASES >= 3:
            esB.close()
            attO = SB(esC, "attO", [128, HEADS * TOWN], BF16)
            wpq = SB(esC, "wpq", [128, 8 * 512], BF16)
            wpin = Ks[:].bitcast(F32)[:, 0:3 * 512]
            wpt32 = Qs[:].bitcast(F32)[:, 0:4 * 512]
            wpb8 = Qs[:].bitcast(FP8)[:, 8192:8192 + 4 * 512]
            from contextlib import ExitStack as _ES
            esC2 = _ES()
            mskb = SB(esC2, "mskb", [128, 2 * 1152], BF16)
            PTb = SB(esC2, "PTb", [128, 2 * 1152], BF16)
            Onrm = SB(esC2, "Onrm", [128, 2 * 128], BF16)
            rcp = SB(esC2, "rcp", [128, 2], F32)
            wmx = wpq[:].bitcast(F32)
            attO3 = attO[:].rearrange("p (h t) -> p h t", t=TOWN)
            s_mk0, s_mk1 = Sem(nc, "s_mk0"), Sem(nc, "s_mk1")
            smk = [s_mk0, s_mk1]
            smx = [s_txs[0][0], s_txs[0][1]]
            mx_t0 = [s.n for s in smx]
            pe3, act3, dve3 = s_pe.n, s_act.n, s_dve.n
            ccd0 = s_ccd.n
            PREP3 = N_PHASES >= 5
            KT1, KV1, KT2, KV2, KCH, KBC, KEV = 64, 65, 66, 67, 81, 82, 83

            def peoff(k):
                if not PREP3:
                    return 0
                return (1 if k > KT1 else 0) + (1 if k > KT2 else 0) + \
                       (1 if k > KBC else 0)

            PET1 = pe3 + 3 * (KT1 + 1) + 1
            PET2 = pe3 + 3 * (KT2 + 1) + 2
            PEBC = pe3 + 3 * (KBC + 1) + 3

            def psS(par):
                return (psA if par == 0 else psB)[:, 0:1152]

            def psO(par):
                return (psA if par == 0 else psB)[:, 1536:1664]

            def psSum(par):
                return (psA if par == 0 else psB)[:, 1152:1153]

            def psOT(par):
                return (psA if par == 0 else psB).bitcast(BF16)[:, 3456:3584]

            dve_mask = {}
            dve_recip = {}
            dve_pair = {}

            @block.vector
            def _(v):
                def dr():
                    v.wait_ge(s_dve.h, s_dve.n)
                for m in range(NT):
                    for h in range(HEADS):
                        k = m * HEADS + h
                        par = k % 2
                        if h == 0:
                            v.wait_ge(smk[m % 2].h, (m // 2 + 1) * 16)
                        v.wait_ge(s_act.h, act3 + 2 * k + 1)
                        s_dve.inc(v.tensor_tensor(
                            PTb[:, par * 1152: par * 1152 + 1152],
                            PTb[:, par * 1152: par * 1152 + 1152],
                            mskb[:, (m % 2) * 1152:(m % 2) * 1152 + 1152],
                            ALU.mult), 1)
                        dve_mask[k] = s_dve.n
                        v.wait_ge(s_pe.h, pe3 + 3 * k + peoff(k) + 2)
                        s_dve.inc(v.reciprocal(rcp[:, par:par + 1], psSum(par)), 1)
                        dve_recip[k] = s_dve.n
                        v.wait_ge(s_pe.h, pe3 + 3 * k + peoff(k) + 3)
                        s_dve.inc(v.tensor_copy(
                            attO3[:, h, m * 128:(m + 1) * 128], psOT(par)), 1)
                        if not PREP3:
                            continue
                        if k < 64:
                            i = k
                            v.wait_ge(smx[i % 2].h, mx_t0[i % 2] + (i // 2 + 1) * 16)
                            dr()
                            s_dve.inc(v.reduce_max(
                                lncol[:, 0:1],
                                wmx[:, (i % 2) * 512:(i % 2) * 512 + 512],
                                AX.X, apply_absolute_value=True), 1)
                            dr()
                            s_dve.inc(v.tensor_max(
                                smax[:, 2 + i // 32:3 + i // 32],
                                smax[:, 2 + i // 32:3 + i // 32],
                                lncol[:, 0:1]), 1)
                            dve_pair[i] = s_dve.n
                        elif k == KV1:
                            v.wait_ge(s_pe.h, PET1)
                            dr()
                            s_dve.inc(v.reduce_max(lncol[0:2, 0:1],
                                                   psA[0:2, 1280:1408], AX.X), 1)
                            P3T["v1"] = s_dve.n
                        elif k == KV2:
                            v.wait_ge(s_pe.h, PET2 + 2)
                            dr()
                            s_dve.inc(v.tensor_copy(srow[0:1, 4:6],
                                                    psB[0:1, 1280:1282]), 1)
                            P3T["v2"] = s_dve.n
                        elif k == KCH:
                            v.wait_ge(s_ccd.h, ccd0 + 32)
                            for i2 in range(2):
                                dr()
                                s_dve.inc(v.tensor_scalar(
                                    srow[0:1, 32 + 2 * i2:33 + 2 * i2],
                                    srow[0:1, 20 + i2:21 + i2],
                                    1.0 / 448.0, 1e-6, ALU.mult, ALU.max), 1)
                                dr()
                                s_dve.inc(v.reciprocal(
                                    srow[0:1, 32 + 2 * i2:33 + 2 * i2],
                                    srow[0:1, 32 + 2 * i2:33 + 2 * i2]), 1)
                                dr()
                                s_dve.inc(v.tensor_scalar(
                                    srow[0:1, 33 + 2 * i2:34 + 2 * i2],
                                    srow[0:1, 32 + 2 * i2:33 + 2 * i2],
                                    0.5, None, ALU.mult), 1)
                            P3T["chain"] = s_dve.n
                        elif k == KEV:
                            v.wait_ge(s_pe.h, PEBC + 2)
                            dr()
                            s_dve.inc(v.tensor_copy(scl[:, 4:8],
                                                    psB[:, 1280:1284]), 1)
                            P3T["scl"] = s_dve.n

            if PREP3:
                @block.gpsimd
                def _(g):
                    g.wait_ge(s_pe.h, pe3)
                    for i in range(64):
                        wi, ch = divmod(i, 32)
                        w8 = wfc_8 if wi == 0 else wproj_8
                        cn = w8.shape[1] // 512
                        a, c2 = divmod(ch, cn)
                        if i >= 2:
                            g.wait_ge(s_dve.h, dve_pair[i - 2])
                        smx[i % 2].inc(g.dma_start(
                            out=wmx[:, (i % 2) * 512:(i % 2) * 512 + 512],
                            in_=w8.rearrange("(a p) (c q) -> p a c q",
                                             p=128, q=512)[:, a, c2, :]), 16)
                    g.wait_ge(s_dve.h, P3T["v2"])
                    s_ccd.inc(g.dma_start(out=cc_w, in_=srow[0:1, 0:16]), 16)
                    g.wait_ge(s_ccd.h, ccd0 + 16)
                    s_cc.inc(g.collective_compute(
                        "AllReduce", ALU.max, rg, [cc_w], [cc_w],
                        cc_dim="Partition", unique_tensors="No"), 1)
                    g.wait_ge(s_cc.h, s_cc.n)
                    s_ccd.inc(g.dma_start(out=srow[0:1, 16:32], in_=cc_w), 16)

            @block.sync
            def _(sync):
                sync.wait_ge(s_pe.h, pe3)
                for m in range(NT):
                    if m >= 2:
                        klast = (m - 2) * HEADS + (HEADS - 1)
                        sync.wait_ge(s_dve.h, dve_mask[klast])
                    smk[m % 2].inc(sync.dma_start(
                        out=mskb[:, (m % 2) * 1152:(m % 2) * 1152 + 1152],
                        in_=masks_d[m]), 16)

            @block.tensor
            def _(t):
                t.wait_ge(s_dve.h, dve3)
                for m in range(NT):
                    for h in range(HEADS):
                        k = m * HEADS + h
                        par = k % 2
                        g = h // 4
                        if k >= 2:
                            t.wait_ge(s_act.h, act3 + 2 * (k - 2) + 1)
                            t.wait_ge(s_dve.h, dve_recip[k - 2])
                        if PREP3 and k == KT1 + 2:
                            t.wait_ge(s_dve.h, P3T["v1"])
                        elif PREP3 and k == KV2 + 2:
                            t.wait_ge(s_dve.h, P3T["v2"])
                        elif PREP3 and k == KEV + 2:
                            t.wait_ge(s_dve.h, P3T["scl"])
                        last = None
                        for kt in range(9):
                            last = t.matmul(
                                psS(par)[:, kt * 128:(kt + 1) * 128],
                                Ks3[:, g, (m + kt) * 128:(m + kt + 1) * 128],
                                Qs3[:, h, m * 128:(m + 1) * 128],
                                start=True, stop=True)
                        s_pe.inc(last, 1)
                        t.wait_ge(s_dve.h, dve_mask[k])
                        for kt in range(9):
                            pt = PTb[:, par * 1152 + kt * 128: par * 1152 + (kt + 1) * 128]
                            t.matmul(psO(par), pt,
                                     Vs3[:, m + kt, g * 128:(g + 1) * 128],
                                     start=(kt == 0), stop=(kt == 8))
                            last = t.matmul(psSum(par), pt, ones[:, 0:1],
                                            start=(kt == 0), stop=(kt == 8))
                        s_pe.inc(last, 1)
                        t.wait_ge(s_act.h, act3 + 2 * k + 2)
                        s_pe.inc(t.transpose(
                            psOT(par), Onrm[:, par * 128:(par + 1) * 128], ident[:]), 1)
                        if PREP3:
                            if k == KT1:
                                t.wait_ge(s_dve.h, dve_pair[63])
                                t.wait_ge(s_dve.h, dve_recip[KT1])
                                s_pe.inc(t.transpose(psA[0:2, 1280:1408],
                                                     smax[:, 2:4], identf[:]), 1)
                            elif k == KT2:
                                t.wait_ge(s_dve.h, P3T["v1"])
                                t.wait_ge(s_dve.h, dve_recip[KT2 - 1])
                                s_pe.inc(t.transpose(psB[0:1, 1280:1282],
                                                     lncol[0:2, 0:1],
                                                     identf[0:2, 0:2]), 1)
                            elif k == KBC:
                                t.wait_ge(s_dve.h, P3T["chain"])
                                s_pe.inc(t.matmul(
                                    psB[:, 1280:1284],
                                    identf[0:1, 0:1].broadcast_to((1, 128)),
                                    srow[0:1, 32:36],
                                    start=True, stop=True), 1)

            @block.scalar
            def _(sc):
                for m in range(NT):
                    for h in range(HEADS):
                        k = m * HEADS + h
                        par = k % 2
                        sc.wait_ge(s_pe.h, pe3 + 3 * k + peoff(k) + 1)
                        s_act.inc(sc.activation(
                            PTb[:, par * 1152: par * 1152 + 1152],
                            psS(par), AF.Exp, bias=cns[:, 0:1], scale=SCALE_QK), 1)
                        sc.wait_ge(s_dve.h, dve_recip[k])
                        s_act.inc(sc.activation(
                            Onrm[:, par * 128:(par + 1) * 128], psO(par),
                            AF.Copy, scale=rcp[:, par:par + 1]), 1)

            esC2.close()

        if N_PHASES >= 4:
            o_sb = SB(esD, "o_sb", [128, 2 * 1024], F32)
            wstg = Vs[:, 0:2 * 1024]
            wpu8 = Vs[:].bitcast(FP8)[:, 8192:8192 + 2 * 512]
            wpcl = Vs[:].bitcast(FP8)[:, 9216:9216 + 2 * 512]
            wpd = Vs[:, 5120:5120 + 512]
            PREP4 = N_PHASES >= 5
            SL = 8
            pe4, act4, dve4 = s_pe.n, s_act.n, s_dve.n
            wo_t0 = [s_wq0.n, s_wq1.n]
            sldw = s_txs[1]
            sld4_0 = [s.n for s in sldw]
            spub4 = [s_wos[0], s_wos[2]]
            pub4_0 = [s.n for s in spub4]

            peW = {}; peT = {}; peSB = {}
            c = pe4 + 4
            for mo in range(16):
                c += 1; peW[mo] = c
                for b in range(2):
                    c += 1; peT[(mo, b)] = c
                if PREP4 and mo >= 1:
                    for b in range(4):
                        c += 1; peSB[4 * (mo - 1) + b] = c
            if PREP4:
                for b in range(4):
                    c += 1; peSB[60 + b] = c
            dveE = {}; dveA = {}; dveEV = {}; dveU = {}; dveQ = {}
            c = dve4 + 32
            for mo in range(16):
                c += 1; dveE[mo] = c
                for b in range(2):
                    c += 1; dveA[(mo, b)] = c
                if PREP4:
                    if mo >= 1:
                        for b in range(4):
                            c += 1; dveEV[4 * (mo - 1) + b] = c
                    for i in range(SL):
                        s = mo * SL + i
                        c += 1; dveU[s] = c
                        c += 3; dveQ[s] = c
            if PREP4:
                for b in range(4):
                    c += 1; dveEV[60 + b] = c
            actT = {}; actB = {}
            c = act4
            if PREP4:
                for mo in range(16):
                    for i in range(SL):
                        s = mo * SL + i
                        c += 1; actT[s] = c
                        c += 1; actB[s] = c

            def pubSB(g):
                return pub4_0[g % 2] + 32 * (g // 2)

            PSLB = [(psB, 3072), (psA, 3072)]

            def PSL(sl):
                pt, off = PSLB[sl]
                return pt.bitcast(BF16)[:, off:off + 1024]

            @block.sync
            def _(sync):
                sync.wait_ge(s_pe.h, pe4)
                sync.wait_ge(s_dve.h, dve4)
                sync.wait_ge(s_act.h, act4)
                s_ld.inc(sync.dma_start(out=o_sb[0:1, 0:E], in_=bo_d), 16)
                if PREP4:
                    wfc_r = wfc_half.rearrange("(a p) (c q) -> p a c q",
                                               p=128, q=512)
                    parH = nc.snap(nc.partition_id([mybir.EngineType.SP]) % 2,
                                   engines=[mybir.EngineType.SP],
                                   min_val=0, max_val=1)

                    def wfc_copy(a4):
                        sync.wait_ge(spub4[0].h, spub4[0].n)
                        sync.wait_ge(spub4[1].h, spub4[1].n)
                        sync.wait_ge(s_wt.h, s_wt.n)
                        s_wt.inc(sync.dma_start(
                            out=wfcT_q[bass.ds(parH, 1), a4 * 2:(a4 + 1) * 2, :, :],
                            in_=wfcT_loc[a4 * 2:(a4 + 1) * 2]), 16)

                    def pubs(mo_sl):
                        for b in range(4):
                            g = 4 * mo_sl + b
                            st = 2 * mo_sl + b // 2
                            cq, r0 = divmod(st * 4, 32)
                            sync.wait_ge(s_dve.h, dveEV[g])
                            for t2 in range(2):
                                dc = cq * 4 + 2 * (b % 2) + t2
                                spub4[g % 2].inc(sync.dma_start(
                                    out=wfcT_loc[r0 // 4][:, dc * 512:(dc + 1) * 512],
                                    in_=wstg[:, (g % 2) * 1024 + t2 * 512:
                                             (g % 2) * 1024 + (t2 + 1) * 512]), 16)

                    for mo in range(16):
                        if mo >= 1:
                            pubs(mo - 1)
                            if mo >= 13:
                                wfc_copy(mo - 13)
                        for i in range(SL):
                            s = mo * SL + i
                            cq, rb = divmod(s, 32)
                            if s >= 3:
                                sync.wait_ge(s_act.h, actT[s - 3])
                            sldw[s % 3].inc(sync.dma_start(
                                out=wpin[:, (s % 3) * 512:(s % 3 + 1) * 512],
                                in_=wfc_r[:, rb, cq, :]), 16)
                    pubs(15)
                    wfc_copy(3)

            @block.tensor
            def _(t):
                t.wait_ge(s_ld.h, s_ld.n)
                t.wait_ge(s_dve.h, dve4)
                for j in range(4):
                    if j >= 1:
                        t.wait_ge(s_dve.h, dve4 + 8 * j)
                    s_pe.inc(t.matmul(psA[:, 1024:1536],
                                      identf[0:1, 0:1].broadcast_to((1, 128)),
                                      o_sb[0:1, j * 512:(j + 1) * 512],
                                      start=True, stop=True), 1)

            @block.vector
            def _(v):
                for j in range(4):
                    v.wait_ge(s_pe.h, pe4 + j + 1)
                    for tt in range(NT):
                        s_dve.inc(v.tensor_tensor(
                            xres3[:, tt, j * 512:(j + 1) * 512],
                            xres3[:, tt, j * 512:(j + 1) * 512],
                            psA[:, 1024:1536], ALU.add), 1)

            @block.gpsimd
            def _(g):
                g.wait_ge(s_pe.h, pe4)
                for mo in range(HEADS):
                    slot = mo % 2
                    if mo >= 2:
                        g.wait_ge(s_pe.h, peW[mo - 2])
                    swq[slot].inc(g.dma_start(
                        out=wb[:, slot * 2048: slot * 2048 + 2048].rearrange(
                            "p (mi c) -> p mi c", c=128),
                        in_=Wo_d.rearrange("(mi e) m -> e mi m", e=128)[
                            :, :, mo * 128:(mo + 1) * 128]), 16)

            def psT(slot):
                return (psA if slot == 0 else psB)[:, 1024:1536]

            def emit_sb(t, mo_sl):
                for b in range(4):
                    g = 4 * mo_sl + b
                    st = 2 * mo_sl + b // 2
                    s0 = st * 4
                    t.wait_ge(s_dve.h, dveQ[s0 + 3])
                    if g >= 2:
                        t.wait_ge(s_dve.h, dveEV[g - 2])
                    sl = PSL(g % 2)
                    last = None
                    for t2 in range(2):
                        tq = 2 * (b % 2) + t2
                        for i in range(4):
                            s = s0 + i
                            last = t.transpose(
                                sl[:, (t2 * 4 + i) * 128:(t2 * 4 + i + 1) * 128],
                                wpq[:, (s % 8) * 512 + tq * 128:
                                    (s % 8) * 512 + (tq + 1) * 128],
                                ident[:])
                    s_pe.inc(last, 1)

            @block.tensor
            def _(t):
                for mo in range(HEADS):
                    par = mo % 2
                    psW = (psA if par == 0 else psB)[:, 0:1024]
                    t.wait_ge(swq[par].h, wo_t0[par] + (mo // 2 + 1) * 16)
                    if mo >= 2:
                        t.wait_ge(s_dve.h, dveE[mo - 2])
                    last = None
                    for mi in range(HEADS):
                        for nq in range(2):
                            last = t.matmul(
                                psW[:, nq * 512:(nq + 1) * 512],
                                wb[:, par * 2048 + mi * 128: par * 2048 + (mi + 1) * 128],
                                attO3[:, mi, nq * 512:(nq + 1) * 512],
                                start=(mi == 0), stop=(mi == HEADS - 1))
                    s_pe.inc(last, 1)
                    t.wait_ge(s_dve.h, dveE[mo])
                    for b in range(2):
                        if mo >= 1:
                            t.wait_ge(s_dve.h, dveA[(mo - 1, b)])
                        last = None
                        for i in range(4):
                            tt = b * 4 + i
                            last = t.transpose(
                                psT(b)[:, i * 128:(i + 1) * 128],
                                o_sb[:, par * 1024 + tt * 128: par * 1024 + tt * 128 + 128],
                                identf[:])
                        s_pe.inc(last, 1)
                    if PREP4 and mo >= 1:
                        emit_sb(t, mo - 1)
                if PREP4:
                    emit_sb(t, 15)

            @block.vector
            def _(v):
                v.wait_ge(s_dve.h, dve4 + 32)
                for mo in range(HEADS):
                    par = mo % 2
                    psW = (psA if par == 0 else psB)[:, 0:1024]
                    v.wait_ge(s_pe.h, peW[mo])
                    s_dve.inc(v.tensor_copy(o_sb[:, par * 1024:(par + 1) * 1024], psW), 1)
                    for b in range(2):
                        v.wait_ge(s_pe.h, peT[(mo, b)])
                        s_dve.inc(v.tensor_tensor(
                            xres3[:, b * 4:(b + 1) * 4, mo * 128:(mo + 1) * 128],
                            xres3[:, b * 4:(b + 1) * 4, mo * 128:(mo + 1) * 128],
                            psT(b).rearrange("p (t e) -> p t e", e=128), ALU.add), 1)
                    if PREP4:
                        if mo >= 1:
                            for b in range(4):
                                g = 4 * (mo - 1) + b
                                v.wait_ge(s_pe.h, peSB[g])
                                if g >= 2:
                                    v.wait_ge(spub4[g % 2].h, pubSB(g))
                                v.wait_ge(s_dve.h, s_dve.n)
                                s_dve.inc(v.tensor_copy(
                                    wstg[:, (g % 2) * 1024:(g % 2 + 1) * 1024],
                                    PSL(g % 2)), 1)
                        for i in range(SL):
                            s = mo * SL + i
                            v.wait_ge(s_act.h, actB[s])
                            if s >= 8:
                                v.wait_ge(s_pe.h, peSB[2 * ((s - 8) // 4) + 1])
                            s_dve.inc(v.tensor_scalar(
                                wpu8[:, (s % 2) * 512:(s % 2 + 1) * 512],
                                wpt32[:, (s % 4) * 512:(s % 4 + 1) * 512],
                                240.0, -240.0, ALU.min, ALU.max), 1)
                            s_dve.inc(v.tensor_scalar(
                                wpcl[:, (s % 2) * 512:(s % 2 + 1) * 512],
                                wpb8[:, (s % 4) * 512:(s % 4 + 1) * 512],
                                120.0, -120.0, ALU.min, ALU.max), 1)
                            v.wait_ge(s_dve.h, s_dve.n)
                            s_dve.inc(v.tensor_tensor(
                                wpd[:, 0:512], wpb8[:, (s % 4) * 512:(s % 4 + 1) * 512],
                                wpcl[:, (s % 2) * 512:(s % 2 + 1) * 512], ALU.subtract), 1)
                            v.wait_ge(s_dve.h, s_dve.n)
                            s_dve.inc(v.scalar_tensor_tensor(
                                wpq[:, (s % 8) * 512:(s % 8 + 1) * 512],
                                wpd[:, 0:512], 2.0,
                                wpu8[:, (s % 2) * 512:(s % 2 + 1) * 512],
                                ALU.mult, ALU.add), 1)
                if PREP4:
                    for b in range(4):
                        g = 60 + b
                        v.wait_ge(s_pe.h, peSB[g])
                        v.wait_ge(spub4[g % 2].h, pubSB(g))
                        v.wait_ge(s_dve.h, s_dve.n)
                        s_dve.inc(v.tensor_copy(
                            wstg[:, (g % 2) * 1024:(g % 2 + 1) * 1024],
                            PSL(g % 2)), 1)

            if PREP4:
                @block.scalar
                def _(sc):
                    for mo in range(16):
                        for i in range(SL):
                            s = mo * SL + i
                            sc.wait_ge(sldw[s % 3].h,
                                       sld4_0[s % 3] + (s // 3 + 1) * 16)
                            if s == 0:
                                sc.wait_ge(s_dve.h, P3T["scl"])
                            if s >= 4:
                                sc.wait_ge(s_dve.h, dveU[s - 4])
                            s_act.inc(sc.activation(
                                wpt32[:, (s % 4) * 512:(s % 4 + 1) * 512],
                                wpin[:, (s % 3) * 512:(s % 3 + 1) * 512],
                                AF.Identity, bias=0.0, scale=scl[:, 4:5]), 1)
                            if s >= 4:
                                sc.wait_ge(s_dve.h, dveU[s - 4] + 2)
                            sc.wait_ge(s_act.h, s_act.n)
                            s_act.inc(sc.activation(
                                wpb8[:, (s % 4) * 512:(s % 4 + 1) * 512],
                                wpt32[:, (s % 4) * 512:(s % 4 + 1) * 512],
                                AF.Copy, scale=0.5), 1)

            actL0, dveL0 = s_act.n, s_dve.n
            INV = 1.0 / E

            @block.scalar
            def _(sc):
                sc.wait_ge(s_dve.h, dveA[(15, 1)])
                for tt in range(NT):
                    s_act.inc(sc.activation(o_sb[:, 0:E], xres3[:, tt, :], AF.Square,
                                            accum_out=lncol[:, 0:1]), 1)
                    sc.wait_ge(s_dve.h, dveL0 + 9 * tt + 4)
                    s_act.inc(sc.activation(lncol[:, 5:6], lncol[:, 4:5],
                                            AF.Sqrt, bias=cns[:, 1:2]), 1)

            @block.vector
            def _(v):
                def dr():
                    v.wait_ge(s_dve.h, s_dve.n)
                for tt in range(NT):
                    v.wait_ge(s_act.h, actL0 + 2 * tt + 1)
                    dr()
                    s_dve.inc(v.reduce_sum(lncol[:, 1:2], xres3[:, tt, :], AX.X), 1)
                    dr()
                    s_dve.inc(v.tensor_scalar(lncol[:, 2:3], lncol[:, 1:2], INV,
                                              None, ALU.mult), 1)
                    dr()
                    s_dve.inc(v.tensor_tensor(lncol[:, 3:4], lncol[:, 2:3],
                                              lncol[:, 2:3], ALU.mult), 1)
                    dr()
                    s_dve.inc(v.scalar_tensor_tensor(lncol[:, 4:5], lncol[:, 0:1], INV,
                                                     lncol[:, 3:4], ALU.mult,
                                                     ALU.subtract), 1)
                    v.wait_ge(s_act.h, actL0 + 2 * tt + 2)
                    s_dve.inc(v.reciprocal(lncol[:, 6:7], lncol[:, 5:6]), 1)
                    dr()
                    s_dve.inc(v.tensor_scalar(lncol[:, 7:8], lncol[:, 2:3],
                                              lncol[:, 6:7], -1.0, ALU.mult,
                                              ALU.mult), 1)
                    dr()
                    s_dve.inc(v.tensor_scalar(xres3[:, tt, :], xres3[:, tt, :],
                                              lncol[:, 6:7], lncol[:, 7:8],
                                              ALU.mult, ALU.add), 1)
                    dr()
                    s_dve.inc(v.reduce_max(lncol[:, 3:4], xres3[:, tt, :], AX.X,
                                           apply_absolute_value=True), 1)
                    dr()
                    s_dve.inc(v.tensor_max(smax[:, 0:1], smax[:, 0:1],
                                           lncol[:, 3:4]), 1)


        if N_PHASES >= 5:
            def row_from_col(cols_lo, cols_n, dst_lo):
                @block.tensor
                def _(t):
                    t.wait_ge(s_dve.h, s_dve.n)
                    s_pe.inc(t.transpose(psA[0:cols_n, 0:128],
                                         smax[:, cols_lo:cols_lo + cols_n], identf[:]), 1)
                @block.vector
                def _(v):
                    v.wait_ge(s_pe.h, s_pe.n)
                    s_dve.inc(v.reduce_max(lncol[0:cols_n, 0:1],
                                           psA[0:cols_n, 0:128], AX.X), 1)
                @block.tensor
                def _(t):
                    t.wait_ge(s_dve.h, s_dve.n)
                    s_pe.inc(t.transpose(psB[0:1, 0:cols_n],
                                         lncol[0:cols_n, 0:1],
                                         identf[0:cols_n, 0:cols_n]), 1)
                @block.vector
                def _(v):
                    v.wait_ge(s_pe.h, s_pe.n)
                    v.wait_ge(s_dve.h, s_dve.n)
                    s_dve.inc(v.tensor_copy(srow[0:1, dst_lo:dst_lo + cols_n],
                                            psB[0:1, 0:cols_n]), 1)

            def scales_from_row(src_lo, n, scl_lo, cc, pre=None):
                @block.gpsimd
                def _(g):
                    if pre is not None:
                        pre(g)
                    g.wait_ge(s_dve.h, s_dve.n)
                    s_ccd.inc(g.dma_start(out=cc, in_=srow[0:1, 0:16]), 16)
                    g.wait_ge(s_ccd.h, s_ccd.n)
                    s_cc.inc(g.collective_compute(
                        "AllReduce", ALU.max, rg, [cc], [cc],
                        cc_dim="Partition", unique_tensors="No"), 1)
                    g.wait_ge(s_cc.h, s_cc.n)
                    s_ccd.inc(g.dma_start(out=srow[0:1, 16:32], in_=cc), 16)
                @block.vector
                def _(v):
                    v.wait_ge(s_ccd.h, s_ccd.n)
                    for i in range(n):
                        v.wait_ge(s_dve.h, s_dve.n)
                        s_dve.inc(v.tensor_scalar(srow[0:1, 32 + 2 * i:33 + 2 * i],
                                                  srow[0:1, 16 + src_lo + i:17 + src_lo + i],
                                                  1.0 / 448.0, 1e-6, ALU.mult, ALU.max), 1)
                        v.wait_ge(s_dve.h, s_dve.n)
                        s_dve.inc(v.reciprocal(srow[0:1, 32 + 2 * i:33 + 2 * i],
                                               srow[0:1, 32 + 2 * i:33 + 2 * i]), 1)
                        v.wait_ge(s_dve.h, s_dve.n)
                        s_dve.inc(v.tensor_scalar(srow[0:1, 33 + 2 * i:34 + 2 * i],
                                                  srow[0:1, 32 + 2 * i:33 + 2 * i],
                                                  0.5, None, ALU.mult), 1)
                @block.tensor
                def _(t):
                    t.wait_ge(s_dve.h, s_dve.n)
                    s_pe.inc(t.matmul(psB[:, 256:256 + 2 * n],
                                      identf[0:1, 0:1].broadcast_to((1, 128)),
                                      srow[0:1, 32:32 + 2 * n],
                                      start=True, stop=True), 1)
                @block.vector
                def _(v):
                    v.wait_ge(s_pe.h, s_pe.n)
                    s_dve.inc(v.tensor_copy(scl[:, scl_lo:scl_lo + 2 * n],
                                            psB[:, 256:256 + 2 * n]), 1)

        if N_PHASES >= 5:
            esD.close()
            esC.close()
            esA.close()
            x8tok = SB(esE, "x8tok", [128, NT * E], BF16)
            x8f = SB(esE, "x8f", [128, NE * TOWN], BF16)
            xq32 = SB(esE, "xq32", [128, E], F32)
            q8u = SB(esE, "q8u", [128, E], FP8)
            q8p = SB(esE, "q8p", [128, E], FP8)
            q8n = SB(esE, "q8n", [128, E], FP8)
            s1f = SB(esE, "s1f", [128, E], F32)
            wfl = SB(esE, "wfl", [128, 2 * 16 * 512], BF16)
            rl = SB(esE, "rl", [128, 2 * 1024], F32)
            h_sb = SB(esE, "h_sb", [128, 4 * 1024], F32)
            x8t3 = x8tok[:].rearrange("p (t e) -> p t e", e=E)
            x8f3 = x8f[:].rearrange("p (ec t) -> p ec t", t=TOWN)
            wfl3 = wfl[:].rearrange("p (s a j) -> p s a j", s=2, a=NE)

            row_from_col(0, 1, 0)
            scales_from_row(0, 1, 0, cc_x,
                            pre=lambda g: g.wait_ge(s_wt.h, s_wt.n))

            @block.vector
            def _(v):
                v.wait_ge(s_dve.h, s_dve.n)
                s_dve.inc(v.tensor_tensor(scl[:, 5:6], scl[:, 0:1], scl[:, 4:5],
                                          ALU.mult), 1)
                v.wait_ge(s_dve.h, s_dve.n)
                s_dve.inc(v.reciprocal(scl[:, 5:6], scl[:, 5:6]), 1)

            act6, dve6 = s_act.n, s_dve.n

            @block.scalar
            def _(sc):
                sc.wait_ge(s_dve.h, dve6)
                for tt in range(NT):
                    if tt >= 1:
                        sc.wait_ge(s_dve.h, dve6 + 5 * tt)
                    sc.wait_ge(s_act.h, s_act.n)
                    s_act.inc(sc.activation(xq32[:], xres3[:, tt, :], AF.Identity,
                                            bias=0.0, scale=scl[:, 0:1]), 1)

            @block.vector
            def _(v):
                v.wait_ge(s_dve.h, dve6)
                for tt in range(NT):
                    v.wait_ge(s_act.h, act6 + tt + 1)
                    s_dve.inc(v.tensor_scalar(q8u[:], xq32[:], 240.0, -240.0,
                                              ALU.min, ALU.max), 1)
                    s_dve.inc(v.tensor_scalar(q8p[:], xq32[:], 0.5, 120.0,
                                              ALU.mult, ALU.max), 1)
                    s_dve.inc(v.tensor_scalar(q8n[:], xq32[:], 0.5, -120.0,
                                              ALU.mult, ALU.min), 1)
                    v.wait_ge(s_dve.h, s_dve.n)
                    s_dve.inc(v.tensor_tensor(s1f[:], q8p[:], q8n[:], ALU.add), 1)
                    v.wait_ge(s_dve.h, s_dve.n)
                    s_dve.inc(v.scalar_tensor_tensor(x8t3[:, tt, :], s1f[:], 2.0,
                                                     q8u[:], ALU.mult, ALU.add), 1)

            pe6, dve6b = s_pe.n, s_dve.n

            @block.tensor
            def _(t):
                t.wait_ge(s_dve.h, dve6b)
                for u in range(2 * NT):
                    tt, gi = divmod(u, 2)
                    ps = (psA if u % 2 == 0 else psB).bitcast(BF16)
                    if u >= 2:
                        t.wait_ge(s_dve.h, dve6b + u - 1)
                    last = None
                    for j in range(8):
                        ec = gi * 8 + j
                        last = t.transpose(ps[:, j * 128:(j + 1) * 128],
                                           x8t3[:, tt, ec * 128:(ec + 1) * 128],
                                           ident[:])
                    s_pe.inc(last, 1)

            @block.vector
            def _(v):
                for u in range(2 * NT):
                    tt, gi = divmod(u, 2)
                    ps = (psA if u % 2 == 0 else psB).bitcast(BF16)
                    v.wait_ge(s_pe.h, pe6 + u + 1)
                    s_dve.inc(v.tensor_copy(
                        x8f3[:, gi * 8:(gi + 1) * 8, tt * 128:(tt + 1) * 128],
                        ps[:, 0:1024].rearrange("p (j c) -> p j c", c=128)), 1)

        h_spill3 = h_spill.rearrange("a p t -> a p t")

        MG = {}
        MSLB = [(0, 3072), (1, 3072), (0, 3584), (1, 3584)]

        def MSL(sl):
            pt, off = MSLB[sl]
            return (psA if pt == 0 else psB).bitcast(BF16)[:, off:off + 512]

        def PSH(sl):
            pt, off = [(0, 0), (1, 0), (0, 512), (1, 512)][sl]
            return (psA if pt == 0 else psB)[:, off:off + 512]

        def phase_mm1(hf):
            pe7, act7, dve7 = s_pe.n, s_act.n, s_dve.n
            wq_t0 = [s_wf0.n, s_wf1.n]
            sp_t0 = [s.n for s in ssp]
            t0 = hf * 512
            sldm = [s_txs[0][2], s_txs[0][3], smk[0], smk[1]]
            if hf == 0:
                MG["ld0"] = [s.n for s in sldm]
                MG["pub0"] = [s_wos[1].n, s_wos[3].n]
                MG["t"] = {}; MG["b"] = {}; MG["u"] = {}; MG["q"] = {}
                MG["tr"] = {}; MG["ev"] = {}
            mt, mb, mu, mq = MG["t"], MG["b"], MG["u"], MG["q"]
            mtr, mev = MG["tr"], MG["ev"]
            wpq6 = x8tok[:, 512:512 + 8 * 512]
            wd6 = x8tok[:, 0:512]
            mstg = x8tok[:, 4608:4608 + 2 * 512]

            pe_mm = {}; act_rl = {}
            c = pe7
            for jc in range(NJ):
                c += 1; pe_mm[jc] = c
                s_t = hf * 64 + jc - 4
                if s_t >= 0:
                    c += 1; mtr[s_t] = c
            if hf == 1:
                for s_t in range(124, 128):
                    c += 1; mtr[s_t] = c
            dve_rm = {}
            c = dve7
            for jc in range(NJ):
                s = hf * 64 + jc
                c += 1; dve_rm[jc] = c
                c += 1
                s_t = hf * 64 + jc - 4
                if s_t >= 0:
                    c += 1; mev[s_t] = c
                c += 1; mu[s] = c
                c += 3; mq[s] = c
            if hf == 1:
                for s_t in range(124, 128):
                    c += 1; mev[s_t] = c
            act_sq = {}
            c = act7
            for jc in range(NJ):
                s = hf * 64 + jc
                c += 1; act_rl[jc] = c
                c += 1; act_sq[jc] = c
                c += 1; mt[s] = c
                c += 1; mb[s] = c

            spubM = [s_wos[1], s_wos[3]]

            def mpub(s_t):
                return MG["pub0"][s_t % 2] + 16 * (s_t // 2)

            @block.sync
            def _(sync):
                sync.wait_ge(s_cc.h, s_cc.n)
                sync.wait_ge(s_dve.h, dve7)
                wpr = wproj_half.rearrange("(a p) (c q) -> p a c q", p=128, q=512)
                par32b = nc.snap(nc.partition_id([mybir.EngineType.SP]) % 2 * 32,
                                 engines=[mybir.EngineType.SP],
                                 min_val=0, max_val=32)

                def wproj_copy(a4):
                    sync.wait_ge(spubM[0].h, spubM[0].n)
                    sync.wait_ge(spubM[1].h, spubM[1].n)
                    sync.wait_ge(s_wt.h, s_wt.n)
                    s_wt.inc(sync.dma_start(
                        out=wprojT_q[a4 * 4:(a4 + 1) * 4, :,
                                     bass.ds(par32b, 32), :],
                        in_=wprojT_loc[a4 * 4:(a4 + 1) * 4]), 16)

                def pub(s_t):
                    ebp, jqp = divmod(s_t, 8)
                    sync.wait_ge(s_dve.h, mev[s_t])
                    spubM[s_t % 2].inc(sync.dma_start(
                        out=wprojT_loc[ebp][:, jqp * 4:(jqp + 1) * 4, :],
                        in_=mstg[:, (s_t % 2) * 512:(s_t % 2 + 1) * 512].rearrange(
                            "p (i j) -> p i j", j=128)), 16)

                for jc in range(NJ):
                    s = hf * 64 + jc
                    if s >= 4:
                        sync.wait_ge(s_act.h, mt[s - 4])
                    eb, jq = divmod(s, 8)
                    sldm[s % 4].inc(sync.dma_start(
                        out=s1f[:, (s % 4) * 512:(s % 4 + 1) * 512],
                        in_=wpr[:, eb, jq, :]), 16)
                    s_t = hf * 64 + jc - 4
                    if s_t >= 0:
                        pub(s_t)
                        if s_t % 32 == 31:
                            wproj_copy(s_t // 32)
                if hf == 1:
                    for s_t in range(124, 128):
                        pub(s_t)
                    wproj_copy(3)

            @block.tensor
            def _(t):
                t.wait_ge(s_dve.h, dve7)

                def tr(s_t):
                    t.wait_ge(s_dve.h, mq[s_t])
                    if s_t >= 2:
                        t.wait_ge(s_dve.h, mev[s_t - 2])
                    sl = MSL(s_t % 4)
                    last = None
                    for tq in range(4):
                        last = t.transpose(
                            sl[:, tq * 128:(tq + 1) * 128],
                            wpq6[:, (s_t % 8) * 512 + tq * 128:
                                 (s_t % 8) * 512 + (tq + 1) * 128],
                            ident[:])
                    s_pe.inc(last, 1)

                for jc in range(NJ):
                    slot = (jc // 4) % 2
                    psH = PSH(jc % 4)
                    t.wait_ge(swf[slot].h, wq_t0[slot] + (jc // 8 + 1) * 16)
                    if jc >= 4:
                        t.wait_ge(s_act.h, act_rl[jc - 4])
                    last = None
                    for dc in range(NE):
                        last = t.matmul(psH,
                                        wfl3[:, slot, dc,
                                             (jc % 4) * 128:(jc % 4) * 128 + 128],
                                        x8f3[:, dc, t0:t0 + 512],
                                        start=(dc == 0), stop=(dc == NE - 1))
                    s_pe.inc(last, 1)
                    s_t = hf * 64 + jc - 4
                    if s_t >= 0:
                        tr(s_t)
                if hf == 1:
                    for s_t in range(124, 128):
                        tr(s_t)

            @block.scalar
            def _(sc):
                for jc in range(NJ):
                    s = hf * 64 + jc
                    sl4 = jc % 4
                    sc.wait_ge(s_pe.h, pe_mm[jc])
                    s_act.inc(sc.activation(rl[:, (jc % 4) * 512:(jc % 4) * 512 + 512],
                                            PSH(jc % 4),
                                            AF.Relu, scale=scl[:, 5:6]), 1)
                    sc.wait_ge(ssp[sl4].h, sp_t0[sl4] + (jc // 4) * 16)
                    if jc >= 4:
                        sc.wait_ge(s_dve.h, dve_rm[jc - 4])
                    sc.wait_ge(s_act.h, s_act.n)
                    s_act.inc(sc.activation(h_sb[:, sl4 * 1024: sl4 * 1024 + 512],
                                            rl[:, (jc % 4) * 512:(jc % 4) * 512 + 512],
                                            AF.Square), 1)
                    sc.wait_ge(sldm[s % 4].h, MG["ld0"][s % 4] + (s // 4 + 1) * 16)
                    if s >= 4:
                        sc.wait_ge(s_dve.h, mu[s - 4])
                    s_act.inc(sc.activation(
                        xq32[:, (s % 4) * 512:(s % 4 + 1) * 512],
                        s1f[:, (s % 4) * 512:(s % 4 + 1) * 512],
                        AF.Identity, bias=0.0, scale=scl[:, 6:7]), 1)
                    if s >= 4:
                        sc.wait_ge(s_dve.h, mu[s - 4] + 2)
                    sc.wait_ge(s_act.h, s_act.n)
                    s_act.inc(sc.activation(
                        q8p[:, (s % 4) * 512:(s % 4 + 1) * 512],
                        xq32[:, (s % 4) * 512:(s % 4 + 1) * 512],
                        AF.Copy, scale=0.5), 1)

            @block.vector
            def _(v):
                for jc in range(NJ):
                    s = hf * 64 + jc
                    sl4 = jc % 4
                    v.wait_ge(s_act.h, act_sq[jc])
                    v.wait_ge(s_dve.h, s_dve.n)
                    s_dve.inc(v.reduce_max(lncol[:, 0:1],
                                           h_sb[:, sl4 * 1024: sl4 * 1024 + 512], AX.X), 1)
                    v.wait_ge(s_dve.h, s_dve.n)
                    s_dve.inc(v.tensor_max(smax[:, 1:2], smax[:, 1:2], lncol[:, 0:1]), 1)
                    s_t = hf * 64 + jc - 4
                    if s_t >= 0:
                        v.wait_ge(s_pe.h, mtr[s_t])
                        if s_t >= 2:
                            v.wait_ge(spubM[s_t % 2].h, mpub(s_t))
                        v.wait_ge(s_dve.h, s_dve.n)
                        s_dve.inc(v.tensor_copy(
                            mstg[:, (s_t % 2) * 512:(s_t % 2 + 1) * 512],
                            MSL(s_t % 4)), 1)
                    v.wait_ge(s_act.h, mb[s])
                    if s >= 8:
                        v.wait_ge(s_pe.h, mtr[s - 8])
                    s_dve.inc(v.tensor_scalar(
                        q8u[:, (s % 2) * 512:(s % 2 + 1) * 512],
                        xq32[:, (s % 4) * 512:(s % 4 + 1) * 512],
                        240.0, -240.0, ALU.min, ALU.max), 1)
                    s_dve.inc(v.tensor_scalar(
                        q8n[:, (s % 2) * 512:(s % 2 + 1) * 512],
                        q8p[:, (s % 4) * 512:(s % 4 + 1) * 512],
                        120.0, -120.0, ALU.min, ALU.max), 1)
                    v.wait_ge(s_dve.h, s_dve.n)
                    s_dve.inc(v.tensor_tensor(
                        wd6, q8p[:, (s % 4) * 512:(s % 4 + 1) * 512],
                        q8n[:, (s % 2) * 512:(s % 2 + 1) * 512], ALU.subtract), 1)
                    v.wait_ge(s_dve.h, s_dve.n)
                    s_dve.inc(v.scalar_tensor_tensor(
                        wpq6[:, (s % 8) * 512:(s % 8 + 1) * 512],
                        wd6, 2.0, q8u[:, (s % 2) * 512:(s % 2 + 1) * 512],
                        ALU.mult, ALU.add), 1)
                if hf == 1:
                    for s_t in range(124, 128):
                        v.wait_ge(s_pe.h, mtr[s_t])
                        v.wait_ge(spubM[s_t % 2].h, mpub(s_t))
                        v.wait_ge(s_dve.h, s_dve.n)
                        s_dve.inc(v.tensor_copy(
                            mstg[:, (s_t % 2) * 512:(s_t % 2 + 1) * 512],
                            MSL(s_t % 4)), 1)

            @block.gpsimd
            def _(g):
                g.wait_ge(s_cc.h, s_cc.n)
                for jc in range(NJ):
                    sl4 = jc % 4
                    if jc % 4 == 0:
                        jq4 = jc // 4
                        slot = jq4 % 2
                        if jq4 >= 2:
                            g.wait_ge(s_pe.h, pe_mm[4 * jq4 - 5])
                        swf[slot].inc(g.dma_start(
                            out=wfl3[:, slot, :, :],
                            in_=wfcT_q[jq4 // 8, jq4 % 8].rearrange(
                                "p (a j) -> p a j", j=512)), 16)
                    g.wait_ge(s_act.h, act_sq[jc])
                    ssp[sl4].inc(g.dma_start(
                        out=h_spill3[jc, :, t0:t0 + 512],
                        in_=h_sb[:, sl4 * 1024: sl4 * 1024 + 512]), 16)

        if N_PHASES >= 6:
            phase_mm1(0)
            phase_mm1(1)

        if N_PHASES >= 7:
            esE.close()
            h8 = SB(esF, "h8", [128, NJ * 512], BF16)
            rb = SB(esF, "rb", [128, 2 * 512], F32)
            r8u = SB(esF, "r8u", [128, 512], FP8)
            r8b = SB(esF, "r8b", [128, 512], FP8)
            rc = SB(esF, "rc", [128, 512], BF16)
            wpj = SB(esF, "wpj", [128, 2 * 64 * 128], BF16)
            o2T = SB(esF, "o2T", [128, 2 * 512], F32)
            sqd = SB(esF, "sqd", [128, E], F32)
            h83 = h8[:].rearrange("p (a t) -> p a t", t=512)
            wpj3 = wpj[:].rearrange("p (s a j) -> p s a j", s=2, a=64)

            def hbar(g):
                g.wait_ge(s_wt.h, s_wt.n)
                for s4 in range(4):
                    g.wait_ge(ssp[s4].h, ssp[s4].n)

            row_from_col(1, 1, 2)
            scales_from_row(2, 1, 2, cc_h, pre=hbar)

            @block.vector
            def _(v):
                v.wait_ge(s_dve.h, s_dve.n)
                s_dve.inc(v.tensor_tensor(scl[:, 7:8], scl[:, 2:3], scl[:, 6:7],
                                          ALU.mult), 1)
                v.wait_ge(s_dve.h, s_dve.n)
                s_dve.inc(v.reciprocal(scl[:, 7:8], scl[:, 7:8]), 1)

        def phase_h8(hf):
            dve8, pl8 = s_dve.n, s_pl.n
            rb_t0 = [srb[0].n, srb[1].n]
            t0 = hf * 512

            @block.sync
            def _(sync):
                sync.wait_ge(s_cc.h, s_cc.n)
                sync.wait_ge(s_dve.h, dve8)
                for jc in range(NJ):
                    slot = jc % 2
                    if jc >= 2:
                        sync.wait_ge(s_dve.h, dve8 + (jc - 2) * 4 + 2)
                    srb[slot].inc(sync.dma_start(
                        out=rb[:, slot * 512:(slot + 1) * 512],
                        in_=h_spill3[jc, :, t0:t0 + 512]), 16)

            @block.vector
            def _(v):
                for jc in range(NJ):
                    slot = jc % 2
                    v.wait_ge(srb[slot].h, rb_t0[slot] + (jc // 2 + 1) * 16)
                    v.wait_ge(s_dve.h, s_dve.n)
                    s_dve.inc(v.tensor_scalar(r8u[:], rb[:, slot * 512:(slot + 1) * 512],
                                              scl[:, 2:3], 240.0, ALU.mult, ALU.min), 1)
                    s_dve.inc(v.tensor_scalar(r8b[:], rb[:, slot * 512:(slot + 1) * 512],
                                              scl[:, 3:4], 120.0, ALU.mult, ALU.max), 1)
                    v.wait_ge(s_dve.h, s_dve.n)
                    s_dve.inc(v.tensor_scalar(rc[:], r8b[:], 120.0, 2.0,
                                              ALU.subtract, ALU.mult), 1)
                    v.wait_ge(s_dve.h, s_dve.n)
                    s_dve.inc(v.tensor_tensor(h83[:, jc, :], r8u[:], rc[:], ALU.add), 1)

        def phase_mm2(hf):
            pe9, act9, dve9, pl9 = s_pe.n, s_act.n, s_dve.n, s_pl.n
            wq_t0 = [s_h0.n, s_h1.n]
            t0 = hf * 512

            @block.sync
            def _(sync):
                for ec in range(NE):
                    slot = ec % 2
                    if ec >= 2:
                        sync.wait_ge(s_pe.h, pe9 + 5 * (ec - 2) + 1)
                    else:
                        sync.wait_ge(s_pe.h, pe9)
                    shw[slot].inc(sync.dma_start(
                        out=wpj3[:, slot, :, :],
                        in_=wprojT_q[ec]), 16)

            def psT2(slot):
                return (psA if slot == 0 else psB)[:, 1536:1664]

            @block.tensor
            def _(t):
                t.wait_ge(s_dve.h, dve9)
                for ec in range(NE):
                    par = ec % 2
                    psO2 = (psA if par == 0 else psB)[:, 0:512]
                    if ec >= 1:
                        t.wait_ge(s_dve.h, dve9 + 5 * (ec - 2) + 1 if ec >= 2 else dve9)
                    slot = ec % 2
                    t.wait_ge(shw[slot].h, wq_t0[slot] + (ec // 2 + 1) * 16)
                    for jc in range(NJ):
                        last = t.matmul(psO2, wpj3[:, slot, jc, :],
                                        h83[:, jc, :],
                                        start=(jc == 0), stop=(jc == NJ - 1))
                    s_pe.inc(last, 1)
                    t.wait_ge(s_dve.h, dve9 + 5 * ec + 1)
                    for tt in range(4):
                        slot = tt % 2
                        if tt >= 2:
                            t.wait_ge(s_dve.h, dve9 + 5 * ec + 2 + (tt - 2))
                        elif ec >= 1:
                            t.wait_ge(s_dve.h, dve9 + 5 * (ec - 1) + 2 + (tt + 2))
                        s_pe.inc(t.transpose(
                            psT2(slot),
                            o2T[:, (ec % 2) * 512 + tt * 128: (ec % 2) * 512 + tt * 128 + 128],
                            identf[:]), 1)

            @block.vector
            def _(v):
                for ec in range(NE):
                    par = ec % 2
                    psO2 = (psA if par == 0 else psB)[:, 0:512]
                    v.wait_ge(s_pe.h, pe9 + 5 * ec + 1)
                    s_dve.inc(v.tensor_copy(o2T[:, par * 512:(par + 1) * 512], psO2), 1)
                    for tt in range(4):
                        slot = tt % 2
                        v.wait_ge(s_pe.h, pe9 + 5 * ec + 2 + tt)
                        s_dve.inc(v.scalar_tensor_tensor(
                            xres3[:, hf * 4 + tt, ec * 128:(ec + 1) * 128],
                            psT2(slot), scl[:, 7:8],
                            xres3[:, hf * 4 + tt, ec * 128:(ec + 1) * 128],
                            ALU.mult, ALU.add), 1)

            actL, dveL = s_act.n, s_dve.n

            @block.scalar
            def _(sc):
                sc.wait_ge(s_dve.h, dveL)
                for tt4 in range(4):
                    tt = hf * 4 + tt4
                    sc.wait_ge(s_act.h, s_act.n)
                    s_act.inc(sc.activation(sqd[:], xres3[:, tt, :], AF.Square,
                                            accum_out=lncol[:, 0:1]), 1)
                    sc.wait_ge(s_dve.h, dveL + 7 * tt4 + 4)
                    s_act.inc(sc.activation(lncol[:, 5:6], lncol[:, 4:5],
                                            AF.Sqrt, bias=cns[:, 1:2]), 1)

            @block.vector
            def _(v):
                def dr():
                    v.wait_ge(s_dve.h, s_dve.n)
                for tt4 in range(4):
                    tt = hf * 4 + tt4
                    v.wait_ge(s_act.h, actL + 2 * tt4 + 1)
                    dr()
                    s_dve.inc(v.reduce_sum(lncol[:, 1:2], xres3[:, tt, :], AX.X), 1)
                    dr()
                    s_dve.inc(v.tensor_scalar(lncol[:, 2:3], lncol[:, 1:2], 1.0 / E,
                                              None, ALU.mult), 1)
                    dr()
                    s_dve.inc(v.tensor_tensor(lncol[:, 3:4], lncol[:, 2:3],
                                              lncol[:, 2:3], ALU.mult), 1)
                    dr()
                    s_dve.inc(v.scalar_tensor_tensor(lncol[:, 4:5], lncol[:, 0:1],
                                                     1.0 / E, lncol[:, 3:4],
                                                     ALU.mult, ALU.subtract), 1)
                    v.wait_ge(s_act.h, actL + 2 * tt4 + 2)
                    s_dve.inc(v.reciprocal(lncol[:, 6:7], lncol[:, 5:6]), 1)
                    dr()
                    s_dve.inc(v.tensor_scalar(lncol[:, 7:8], lncol[:, 2:3],
                                              lncol[:, 6:7], -1.0, ALU.mult,
                                              ALU.mult), 1)
                    dr()
                    s_dve.inc(v.tensor_scalar(xres3[:, tt, :], xres3[:, tt, :],
                                              lncol[:, 6:7], lncol[:, 7:8],
                                              ALU.mult, ALU.add), 1)
                    dr()

            @block.sync
            def _(sync):
                sync.wait_ge(s_dve.h, s_dve.n)
                s_st.inc(sync.dma_start(
                    out=y.rearrange("(t p) e -> p t e", p=128)[:, hf * 4:(hf + 1) * 4, :],
                    in_=xres3[:, hf * 4:(hf + 1) * 4, :]), 16)

        if N_PHASES >= 7:
            phase_h8(0)
        if N_PHASES >= 8:
            phase_mm2(0)
            phase_h8(1)
            phase_mm2(1)
            esF.close()

        if DEBUG and N_PHASES >= 2:
            @block.sync
            def _(sync):
                sync.wait_ge(s_dve.h, s_dve.n)
                sync.wait_ge(s_act.h, s_act.n)
                s_st.inc(sync.dma_start(
                    out=dbg["dbg_a"],
                    in_=(attO[:] if N_PHASES >= 3 else Qs[:])), 16)
                s_st.inc(sync.dma_start(out=dbg["dbg_b"], in_=Ks[:]), 16)
                s_st.inc(sync.dma_start(out=dbg["dbg_c"], in_=Vs[:]), 16)
                s_st.inc(sync.dma_start(out=dbg["dbg_f"], in_=xres[:]), 16)

        @block.sync
        def _(sync):
            sync.wait_ge(s_ld.h, s_ld.n)
            sync.wait_ge(s_dve.h, s_dve.n)
            sync.wait_ge(s_pe.h, s_pe.n)
            sync.wait_ge(s_act.h, s_act.n)
            sync.wait_ge(s_pl.h, s_pl.n)
            if N_PHASES < 8:
                s_st.inc(sync.dma_start(
                    out=y.rearrange("(t p) e -> p t e", p=128), in_=xres3), 16)
            sync.wait_ge(s_st.h, s_st.n)

        if N_PHASES < 5:
            esD.close()
            esC.close()
            esA.close()
        if N_PHASES < 3:
            esB.close()
        if 5 <= N_PHASES < 7:
            esE.close()
        if N_PHASES == 7:
            esF.close()

    return nc


def kernel(L0_attn_x, L0_attn_Wq, L0_attn_Wk, L0_attn_Wv, L0_attn_Wo,
           L0_attn_bo, L0_ffn_w_fc, L0_ffn_w_proj):
    x = np.ascontiguousarray(np.asarray(L0_attn_x, dtype=np.float32)[0])
    Wq = np.ascontiguousarray(L0_attn_Wq, dtype=np.float32)
    Wk = np.ascontiguousarray(L0_attn_Wk, dtype=np.float32)
    Wv = np.ascontiguousarray(L0_attn_Wv, dtype=np.float32)
    Wo = np.ascontiguousarray(L0_attn_Wo, dtype=np.float32)
    bo = np.ascontiguousarray(L0_attn_bo, dtype=np.float32).reshape(1, M)
    wfc = np.ascontiguousarray(L0_ffn_w_fc, dtype=np.float32)
    wproj = np.ascontiguousarray(L0_ffn_w_proj, dtype=np.float32)

    nc = build_nc()
    in_maps = []
    for c in range(NCORES):
        half = c % 2
        in_maps.append({
            "x_own": x[c * TOWN:(c + 1) * TOWN],
            "x_halo": (np.zeros((TOWN, E), np.float32) if c == 0
                       else np.ascontiguousarray(x[(c - 1) * TOWN:c * TOWN])),
            "Wq": Wq, "Wk": Wk, "Wv": Wv, "Wo": Wo, "bo": bo,
            "wfc_half": np.ascontiguousarray(wfc[half * (FF // 2):(half + 1) * (FF // 2)]),
            "wproj_half": np.ascontiguousarray(wproj[:, half * (FF // 2):(half + 1) * (FF // 2)]),
            "wfc_8": np.ascontiguousarray(wfc[c * (FF // 8):(c + 1) * (FF // 8)]),
            "wproj_8": np.ascontiguousarray(wproj[:, c * (FF // 8):(c + 1) * (FF // 8)]),
            "masks": _mask_host(c),
        })
    res = run_bass_kernel_spmd(nc, in_maps, core_ids=list(range(NCORES)))
    out = np.concatenate([res.results[c]["y"] for c in range(NCORES)], axis=0)
    return out.reshape(1, T, E).astype(np.float32)


# version 13 (final)
# speedup vs baseline: 1.1034x; 1.0530x over previous
import math
import os
import numpy as np
import ml_dtypes
import concourse.bass as bass
import concourse.mybir as mybir
from concourse.bass_utils import run_bass_kernel_spmd

F32 = mybir.dt.float32
BF16 = mybir.dt.bfloat16
FP8 = mybir.dt.float8e4
AF = mybir.ActivationFunctionType
ALU = mybir.AluOpType
AX = mybir.AxisListType

NCORES = 8
T, TOWN, E, M, KV, FF = 8192, 1024, 2048, 2048, 512, 8192
HEADS, GROUPS, D = 16, 4, 128
NT, NE, NJ = TOWN // 128, E // 128, FF // 128
SCALE_QK = 1.0 / math.sqrt(D)
EXP_BIAS = -30.0
LN_EPS = 1e-5

N_PHASES = int(os.environ.get("KPHASES", "99"))
DEBUG = os.environ.get("KDEBUG", "0") == "1"


class Sem:
    def __init__(self, nc, name):
        self.h = nc.alloc_semaphore(name)
        self.n = 0

    def inc(self, inst, k=1):
        inst.then_inc(self.h, k)
        self.n += k
        return inst


def _mask_host(core):
    out = np.zeros((NT, 128, 9 * 128), dtype=ml_dtypes.bfloat16)
    kr = np.arange(128)[:, None]
    qc = np.arange(128)[None, :]
    for m in range(NT):
        gm = NT * core + m
        for kt in range(9):
            jt = gm - 8 + kt
            if jt < 0:
                continue
            delta = (8 - kt) * 128
            allowed = (kr <= qc + delta) & (kr >= qc + delta - 1023)
            out[m, :, kt * 128:(kt + 1) * 128] = allowed.astype(ml_dtypes.bfloat16)
    return out


def build_nc():
    nc = bass.Bass("TRN2", target_bir_lowering=False, num_devices=NCORES)

    x_own = nc.dram_tensor("x_own", [TOWN, E], F32, kind="ExternalInput").ap()
    x_halo = nc.dram_tensor("x_halo", [TOWN, E], F32, kind="ExternalInput").ap()
    Wq_d = nc.dram_tensor("Wq", [E, M], F32, kind="ExternalInput").ap()
    Wk_d = nc.dram_tensor("Wk", [E, KV], F32, kind="ExternalInput").ap()
    Wv_d = nc.dram_tensor("Wv", [E, KV], F32, kind="ExternalInput").ap()
    Wo_d = nc.dram_tensor("Wo", [M, M], F32, kind="ExternalInput").ap()
    bo_d = nc.dram_tensor("bo", [1, M], F32, kind="ExternalInput").ap()
    wfc_half = nc.dram_tensor("wfc_half", [FF // 2, E], F32, kind="ExternalInput").ap()
    wproj_half = nc.dram_tensor("wproj_half", [E, FF // 2], F32, kind="ExternalInput").ap()
    wfc_8 = nc.dram_tensor("wfc_8", [FF // 8, E], F32, kind="ExternalInput").ap()
    wproj_8 = nc.dram_tensor("wproj_8", [E, FF // 8], F32, kind="ExternalInput").ap()
    masks_d = nc.dram_tensor("masks", [NT, 128, 9 * 128], BF16, kind="ExternalInput").ap()
    y = nc.dram_tensor("y", [TOWN, E], F32, kind="ExternalOutput").ap()

    h_spill = nc.dram_tensor("h_spill", [NJ, 128, TOWN], F32, kind="Internal").ap()
    wfcT_q = nc.dram_tensor("wfcT_q", [2, 8, 128, NE * 512], BF16, kind="Internal",
                            addr_space="Shared").ap()
    wprojT_q = nc.dram_tensor("wprojT_q", [NE, 128, 64, 128], BF16, kind="Internal",
                              addr_space="Shared").ap()
    wfcT_loc = nc.dram_tensor("wfcT_loc", [8, 128, NE * 512], BF16, kind="Internal").ap()
    wprojT_loc = nc.dram_tensor("wprojT_loc", [NE, 128, 32, 128], BF16, kind="Internal").ap()
    cc_w = nc.dram_tensor("cc_w", [1, 16], F32, kind="Internal").ap()
    cc_x = nc.dram_tensor("cc_x", [1, 16], F32, kind="Internal").ap()
    cc_h = nc.dram_tensor("cc_h", [1, 16], F32, kind="Internal").ap()
    rg = [list(range(NCORES))]

    from contextlib import ExitStack
    es = ExitStack()
    esA = ExitStack()
    esB = ExitStack()
    esC = ExitStack()
    esD = ExitStack()
    esE = ExitStack()
    esF = ExitStack()
    SB = lambda stk, name, shape, dt: stk.enter_context(nc.sbuf_tensor(name, shape, dt))

    xres = SB(es, "xres", [128, NT * E], F32)
    ident = SB(es, "ident", [128, 128], BF16)
    identf = SB(es, "identf", [128, 128], F32)
    ones = SB(es, "ones", [128, 128], BF16)
    scl = SB(es, "scl", [128, 8], F32)
    smax = SB(es, "smax", [128, 4], F32)
    srow = SB(es, "srow", [1, 128], F32)
    cns = SB(es, "cns", [128, 2], F32)
    lncol = SB(es, "lncol", [128, 8], F32)
    ccb = SB(es, "ccb", [1, 16], F32)

    psA = es.enter_context(nc.psum_tensor("psA", [128, 2048], F32))
    psB = es.enter_context(nc.psum_tensor("psB", [128, 2048], F32))

    s_ld = Sem(nc, "s_ld")
    s_const = Sem(nc, "s_const")
    s_x0 = Sem(nc, "s_x0")
    s_x1 = Sem(nc, "s_x1")
    s_wkv = Sem(nc, "s_wkv")
    s_wq0 = Sem(nc, "s_wq0")
    s_wq1 = Sem(nc, "s_wq1")
    s_pe = Sem(nc, "s_pe")
    s_act = Sem(nc, "s_act")
    s_dve = Sem(nc, "s_dve")
    s_pl = Sem(nc, "s_pl")
    s_st = Sem(nc, "s_st")
    s_wp = Sem(nc, "s_wp")
    s_wd = Sem(nc, "s_wd")
    s_wt = Sem(nc, "s_wt")
    s_cc = Sem(nc, "s_cc")
    s_mk = Sem(nc, "s_mk")
    s_sp = Sem(nc, "s_sp")
    s_rb = Sem(nc, "s_rb")
    s_wd1 = Sem(nc, "s_wd1")
    s_txs = [[Sem(nc, "s_tx%d_%d" % (p, k)) for k in range(4)] for p in range(2)]
    s_wos = [Sem(nc, "s_wo%d" % k) for k in range(4)]
    s_ccd = Sem(nc, "s_ccd")
    s_rb1 = Sem(nc, "s_rb1")
    s_sp1 = Sem(nc, "s_sp1")
    s_sp2 = Sem(nc, "s_sp2")
    s_sp3 = Sem(nc, "s_sp3")
    s_h0 = Sem(nc, "s_h0")
    s_h1 = Sem(nc, "s_h1")
    s_wf0 = Sem(nc, "s_wf0")
    s_wf1 = Sem(nc, "s_wf1")
    swf = [s_wf0, s_wf1]
    shw = [s_h0, s_h1]
    sx = [s_x0, s_x1]
    swq = [s_wq0, s_wq1]
    swd = [s_wd, s_wd1]
    ssp = [s_sp, s_sp1, s_sp2, s_sp3]
    srb = [s_rb, s_rb1]

    identt = nc.inline_tensor(np.eye(128, dtype=ml_dtypes.bfloat16).view(np.uint16), "idbf")
    identtf = nc.inline_tensor(np.eye(128, dtype=np.float32), "idf32")

    dbg = {}
    if DEBUG:
        for nm, shape in [("dbg_a", [128, HEADS * TOWN]), ("dbg_b", [128, GROUPS * 2048]),
                          ("dbg_c", [128, 16 * KV])]:
            dbg[nm] = nc.dram_tensor(nm, shape, BF16, kind="ExternalOutput").ap()
        dbg["dbg_f"] = nc.dram_tensor("dbg_f", [128, NT * E], F32, kind="ExternalOutput").ap()

    with es, nc.Block() as block:
        Qs = SB(esA, "Qs", [128, HEADS * TOWN], BF16)
        Ks = SB(esA, "Ks", [128, GROUPS * 2 * TOWN], BF16)
        Vs = SB(esA, "Vs", [128, 16 * KV], BF16)
        wb = SB(esA, "wb", [128, 2 * 2048], BF16)
        WKV = SB(esB, "WKV", [128, NE * 512], BF16)
        xstg = SB(esB, "xstg", [128, 2 * 1024], BF16)
        xT = SB(esB, "xT", [128, NE * TOWN], BF16)

        WKV3 = WKV[:].rearrange("p (ec c) -> p ec c", c=512)
        xT3 = xT[:].rearrange("p (ec t) -> p ec t", t=TOWN)
        Qs3 = Qs[:].rearrange("p (h t) -> p h t", t=TOWN)
        Ks3 = Ks[:].rearrange("p (g t) -> p g t", t=2 * TOWN)
        Vs3 = Vs[:].rearrange("p (l c) -> p l c", c=KV)
        xres3 = xres[:].rearrange("p (t e) -> p t e", e=E)

        @block.sync
        def _(sync):
            s_ld.inc(sync.dma_start(
                out=xres3, in_=x_own.rearrange("(t p) e -> p t e", p=128)), 16)

        @block.gpsimd
        def _(g):
            s_const.inc(g.dma_start(out=ident[:], in_=identt.ap().bitcast(BF16)), 16)
            s_const.inc(g.dma_start(out=identf[:], in_=identtf.ap()), 16)

        @block.vector
        def _(v):
            s_dve.inc(v.memset(ones[:], 1.0), 1)
            s_dve.inc(v.memset(smax[:], 0.0), 1)
            s_dve.inc(v.memset(cns[:, 0:1], EXP_BIAS), 1)
            s_dve.inc(v.memset(cns[:, 1:2], LN_EPS), 1)
            s_dve.inc(v.memset(srow[:], 0.0), 1)
            s_dve.inc(v.memset(scl[:], 0.0), 1)
            s_dve.inc(v.memset(lncol[:], 0.0), 1)

        def phase_proj(src_dram, half):
            import os
            SUB = int(os.environ.get("KP1SUB", "9"))
            pe0, dve0 = s_pe.n, s_dve.n
            x_tot0 = [s_x0.n, s_x1.n]

            @block.gpsimd
            def _(g):
                g.wait_ge(s_pe.h, pe0)
                s_wkv.inc(g.dma_start(
                    out=WKV3, in_=Wk_d.rearrange("(ec e) k -> e ec k", e=128)), 16)
                for u in range(2 * NT):
                    tt, gi = divmod(u, 2)
                    slot = u % 2
                    if u >= 2:
                        g.wait_ge(s_pe.h, pe0 + u - 1)
                    sx[slot].inc(g.dma_start(
                        out=xstg[:, slot * 1024:(slot + 1) * 1024],
                        in_=src_dram.rearrange("(t p) e -> p t e", p=128)[
                            :, tt, gi * 1024:(gi + 1) * 1024]), 16)
                if SUB >= 3:
                    g.wait_ge(s_pe.h, pe0 + 2 * NT + 8)
                    s_wkv.inc(g.dma_start(
                        out=WKV3, in_=Wv_d.rearrange("(ec e) k -> e ec k", e=128)), 16)

            @block.tensor
            def _(t):
                t.wait_ge(s_const.h, 32)
                t.wait_ge(s_dve.h, dve0)
                for u in range(2 * NT):
                    tt, gi = divmod(u, 2)
                    slot = u % 2
                    t.wait_ge(sx[slot].h, x_tot0[slot] + (u // 2 + 1) * 16)
                    ps = (psA if u % 2 == 0 else psB).bitcast(BF16)
                    if u >= 2:
                        t.wait_ge(s_dve.h, dve0 + u - 1)
                    last = None
                    for j in range(8):
                        last = t.transpose(
                            ps[:, j * 128:(j + 1) * 128],
                            xstg[:, slot * 1024 + j * 128: slot * 1024 + j * 128 + 128],
                            ident[:])
                    s_pe.inc(last, 1)

            @block.vector
            def _(v):
                for u in range(2 * NT):
                    tt, gi = divmod(u, 2)
                    ps = (psA if u % 2 == 0 else psB).bitcast(BF16)
                    v.wait_ge(s_pe.h, pe0 + u + 1)
                    s_dve.inc(v.tensor_copy(
                        xT3[:, gi * 8:(gi + 1) * 8, tt * 128:(tt + 1) * 128],
                        ps[:, 0:1024].rearrange("p (j c) -> p j c", c=128)), 1)

            pe1, dve1 = s_pe.n, s_dve.n

            wk_tot = s_wkv.n - (16 if SUB >= 3 else 0)
            wv_tot = s_wkv.n
            wq_tot0 = [s_wq0.n, s_wq1.n]
            if half == 1:
                @block.gpsimd
                def _(g):
                    for mc in range(HEADS):
                        slot = mc % 2
                        if mc >= 2:
                            g.wait_ge(s_pe.h, pe1 + 16 + mc - 1)
                        swq[slot].inc(g.dma_start(
                            out=wb[:, slot * 2048: slot * 2048 + 2048].rearrange(
                                "p (ec c) -> p ec c", c=128),
                            in_=Wq_d.rearrange("(ec e) m -> e ec m", e=128)[
                                :, :, mc * 128:(mc + 1) * 128]), 16)

            @block.tensor
            def _(t):
                t.wait_ge(s_dve.h, dve1)
                t.wait_ge(s_wkv.h, wk_tot)
                for gidx in range(8 if SUB >= 2 else 0):
                    kd, kq = divmod(gidx, 2)
                    ps = (psA if gidx % 2 == 0 else psB)[:, 0:512]
                    if gidx >= 2:
                        t.wait_ge(s_dve.h, dve1 + gidx - 1)
                    last = None
                    for ec in range(NE):
                        last = t.matmul(ps, WKV3[:, ec, kd * 128:(kd + 1) * 128],
                                        xT3[:, ec, kq * 512:(kq + 1) * 512],
                                        start=(ec == 0), stop=(ec == NE - 1))
                    s_pe.inc(last, 1)
                if SUB >= 3:
                    t.wait_ge(s_wkv.h, wv_tot)
                for tt in range(NT if SUB >= 3 else 0):
                    gidx = (8 if SUB >= 2 else 0) + tt
                    ps = (psA if gidx % 2 == 0 else psB)[:, 0:512]
                    t.wait_ge(s_dve.h, dve1 + gidx - 1)
                    last = None
                    for ec in range(NE):
                        last = t.matmul(ps, xT3[:, ec, tt * 128:(tt + 1) * 128],
                                        WKV3[:, ec, 0:512],
                                        start=(ec == 0), stop=(ec == NE - 1))
                    s_pe.inc(last, 1)
                if half == 1:
                    t.wait_ge(s_dve.h, dve1 + 16)
                    for mc in range(HEADS):
                        slot = mc % 2
                        psq = (psA if mc % 2 == 0 else psB)
                        t.wait_ge(swq[mc % 2].h, wq_tot0[mc % 2] + (mc // 2 + 1) * 16)
                        if mc >= 2:
                            t.wait_ge(s_dve.h, dve1 + 16 + (mc - 2) + 1)
                        wv = wb[:, slot * 2048: slot * 2048 + 2048].rearrange(
                            "p (ec c) -> p ec c", c=128)
                        last = None
                        for ec in range(NE):
                            for nq in range(2):
                                last = t.matmul(
                                    psq[:, nq * 512:(nq + 1) * 512],
                                    wv[:, ec, :],
                                    xT3[:, ec, nq * 512:(nq + 1) * 512],
                                    start=(ec == 0), stop=(ec == NE - 1))
                        s_pe.inc(last, 1)

            @block.vector
            def _(v):
                nk = 8 if SUB >= 2 else 0
                for gidx in range(nk):
                    kd, kq = divmod(gidx, 2)
                    ps = (psA if gidx % 2 == 0 else psB)[:, 0:512]
                    v.wait_ge(s_pe.h, pe1 + gidx + 1)
                    s_dve.inc(v.tensor_copy(
                        Ks3[:, kd, half * TOWN + kq * 512: half * TOWN + (kq + 1) * 512], ps), 1)
                for tt in range(NT if SUB >= 3 else 0):
                    gidx = nk + tt
                    ps = (psA if gidx % 2 == 0 else psB)[:, 0:512]
                    v.wait_ge(s_pe.h, pe1 + gidx + 1)
                    s_dve.inc(v.tensor_copy(Vs3[:, half * 8 + tt, :], ps), 1)
                if half == 1:
                    for mc in range(HEADS):
                        psq = (psA if mc % 2 == 0 else psB)
                        v.wait_ge(s_pe.h, pe1 + 16 + mc + 1)
                        s_dve.inc(v.tensor_copy(Qs3[:, mc, :], psq[:, 0:1024]), 1)

        if N_PHASES >= 1:
            phase_proj(x_halo, 0)
        if N_PHASES >= 2:
            phase_proj(x_own, 1)

        attO = None
        P3T = {}
        if N_PHASES >= 3:
            esB.close()
            attO = SB(esC, "attO", [128, HEADS * TOWN], BF16)
            wpq = SB(esC, "wpq", [128, 8 * 512], BF16)
            wpin = Ks[:].bitcast(F32)[:, 0:3 * 512]
            wpt32 = Qs[:].bitcast(F32)[:, 0:4 * 512]
            wpb8 = Qs[:].bitcast(FP8)[:, 8192:8192 + 4 * 512]
            from contextlib import ExitStack as _ES
            esC2 = _ES()
            mskb = SB(esC2, "mskb", [128, 2 * 1152], BF16)
            PTb = SB(esC2, "PTb", [128, 2 * 1152], BF16)
            Onrm = SB(esC2, "Onrm", [128, 2 * 128], BF16)
            rcp = SB(esC2, "rcp", [128, 2], F32)
            wmx = wpq[:].bitcast(F32)
            attO3 = attO[:].rearrange("p (h t) -> p h t", t=TOWN)
            s_mk0, s_mk1 = Sem(nc, "s_mk0"), Sem(nc, "s_mk1")
            smk = [s_mk0, s_mk1]
            smx = [s_txs[0][0], s_txs[0][1]]
            mx_t0 = [s.n for s in smx]
            pe3, act3, dve3 = s_pe.n, s_act.n, s_dve.n
            ccd0 = s_ccd.n
            PREP3 = N_PHASES >= 5
            KT1, KV1, KT2, KV2, KCH, KBC, KEV = 64, 65, 66, 67, 81, 82, 83

            def peoff(k):
                if not PREP3:
                    return 0
                return (1 if k > KT1 else 0) + (1 if k > KT2 else 0) + \
                       (1 if k > KBC else 0)

            def peQK(k):
                return pe3 + (0 if k == 0 else 3 * k - 1) + peoff(k) + 1

            def pePV(k):
                return peQK(k) + (2 if k >= 1 else 1)

            def peOT(km1):
                if km1 == 127:
                    return pe3 + 383 + peoff(999) + 1
                return peQK(km1 + 1) + 1

            PET1 = pePV(KT1) + 1
            PET2 = pePV(KT2) + 1
            PEBC = pePV(KBC) + 1

            def psS(par):
                return (psA if par == 0 else psB)[:, 0:1152]

            def psO(par):
                return (psA if par == 0 else psB)[:, 1536:1664]

            def psSum(par):
                return (psA if par == 0 else psB)[:, 1152:1153]

            def psOT(par):
                return (psA if par == 0 else psB).bitcast(BF16)[:, 3456:3584]

            dve_mask = {}
            dve_recip = {}
            dve_onrm = {}
            dve_ot = {}

            def otcopy(v, km1):
                parp = km1 % 2
                mp, hp = divmod(km1, HEADS)
                v.wait_ge(s_pe.h, peOT(km1))
                s_dve.inc(v.tensor_copy(
                    attO3[:, hp, mp * 128:(mp + 1) * 128], psOT(parp)), 1)
                dve_ot[km1] = s_dve.n

            dve_pair = {}

            @block.vector
            def _(v):
                def dr():
                    v.wait_ge(s_dve.h, s_dve.n)
                for m in range(NT):
                    for h in range(HEADS):
                        k = m * HEADS + h
                        par = k % 2
                        if h == 0:
                            v.wait_ge(smk[m % 2].h, (m // 2 + 1) * 16)
                        v.wait_ge(s_act.h, act3 + k + 1)
                        s_dve.inc(v.tensor_tensor(
                            PTb[:, par * 1152: par * 1152 + 1152],
                            PTb[:, par * 1152: par * 1152 + 1152],
                            mskb[:, (m % 2) * 1152:(m % 2) * 1152 + 1152],
                            ALU.mult), 1)
                        dve_mask[k] = s_dve.n
                        v.wait_ge(s_pe.h, pePV(k))
                        s_dve.inc(v.reciprocal(rcp[:, par:par + 1], psSum(par)), 1)
                        dve_recip[k] = s_dve.n
                        if k >= 2:
                            v.wait_ge(s_pe.h, peOT(k - 2))
                        dr()
                        s_dve.inc(v.tensor_scalar(
                            Onrm[:, par * 128:(par + 1) * 128], psO(par),
                            rcp[:, par:par + 1], None, ALU.mult), 1)
                        dve_onrm[k] = s_dve.n
                        if k >= 1:
                            otcopy(v, k - 1)
                        if not PREP3:
                            continue
                        if k < 64:
                            i = k
                            v.wait_ge(smx[i % 2].h, mx_t0[i % 2] + (i // 2 + 1) * 16)
                            dr()
                            s_dve.inc(v.reduce_max(
                                lncol[:, 0:1],
                                wmx[:, (i % 2) * 512:(i % 2) * 512 + 512],
                                AX.X, apply_absolute_value=True), 1)
                            dr()
                            s_dve.inc(v.tensor_max(
                                smax[:, 2 + i // 32:3 + i // 32],
                                smax[:, 2 + i // 32:3 + i // 32],
                                lncol[:, 0:1]), 1)
                            dve_pair[i] = s_dve.n
                        elif k == KV1:
                            v.wait_ge(s_pe.h, PET1)
                            dr()
                            s_dve.inc(v.reduce_max(lncol[0:2, 0:1],
                                                   psA[0:2, 1280:1408], AX.X), 1)
                            P3T["v1"] = s_dve.n
                        elif k == KV2:
                            v.wait_ge(s_pe.h, pePV(KV2))
                            dr()
                            s_dve.inc(v.tensor_copy(srow[0:1, 4:6],
                                                    psB[0:1, 1280:1282]), 1)
                            P3T["v2"] = s_dve.n
                        elif k == KCH:
                            v.wait_ge(s_ccd.h, ccd0 + 32)
                            for i2 in range(2):
                                dr()
                                s_dve.inc(v.tensor_scalar(
                                    srow[0:1, 32 + 2 * i2:33 + 2 * i2],
                                    srow[0:1, 20 + i2:21 + i2],
                                    1.0 / 448.0, 1e-6, ALU.mult, ALU.max), 1)
                                dr()
                                s_dve.inc(v.reciprocal(
                                    srow[0:1, 32 + 2 * i2:33 + 2 * i2],
                                    srow[0:1, 32 + 2 * i2:33 + 2 * i2]), 1)
                                dr()
                                s_dve.inc(v.tensor_scalar(
                                    srow[0:1, 33 + 2 * i2:34 + 2 * i2],
                                    srow[0:1, 32 + 2 * i2:33 + 2 * i2],
                                    0.5, None, ALU.mult), 1)
                            P3T["chain"] = s_dve.n
                        elif k == KEV:
                            v.wait_ge(s_pe.h, pePV(KEV))
                            dr()
                            s_dve.inc(v.tensor_copy(scl[:, 4:8],
                                                    psB[:, 1280:1284]), 1)
                            P3T["scl"] = s_dve.n
                otcopy(v, 127)

            if PREP3:
                @block.gpsimd
                def _(g):
                    g.wait_ge(s_pe.h, pe3)
                    for i in range(64):
                        wi, ch = divmod(i, 32)
                        w8 = wfc_8 if wi == 0 else wproj_8
                        cn = w8.shape[1] // 512
                        a, c2 = divmod(ch, cn)
                        if i >= 2:
                            g.wait_ge(s_dve.h, dve_pair[i - 2])
                        smx[i % 2].inc(g.dma_start(
                            out=wmx[:, (i % 2) * 512:(i % 2) * 512 + 512],
                            in_=w8.rearrange("(a p) (c q) -> p a c q",
                                             p=128, q=512)[:, a, c2, :]), 16)
                    g.wait_ge(s_dve.h, P3T["v2"])
                    s_ccd.inc(g.dma_start(out=cc_w, in_=srow[0:1, 0:16]), 16)
                    g.wait_ge(s_ccd.h, ccd0 + 16)
                    s_cc.inc(g.collective_compute(
                        "AllReduce", ALU.max, rg, [cc_w], [cc_w],
                        cc_dim="Partition", unique_tensors="No"), 1)
                    g.wait_ge(s_cc.h, s_cc.n)
                    s_ccd.inc(g.dma_start(out=srow[0:1, 16:32], in_=cc_w), 16)

            @block.sync
            def _(sync):
                sync.wait_ge(s_pe.h, pe3)
                for m in range(NT):
                    if m >= 2:
                        klast = (m - 2) * HEADS + (HEADS - 1)
                        sync.wait_ge(s_dve.h, dve_mask[klast])
                    smk[m % 2].inc(sync.dma_start(
                        out=mskb[:, (m % 2) * 1152:(m % 2) * 1152 + 1152],
                        in_=masks_d[m]), 16)

            @block.tensor
            def _(t):
                t.wait_ge(s_dve.h, dve3)
                for m in range(NT):
                    for h in range(HEADS):
                        k = m * HEADS + h
                        par = k % 2
                        g = h // 4
                        if k >= 2:
                            t.wait_ge(s_act.h, act3 + (k - 2) + 1)
                            t.wait_ge(s_dve.h, dve_recip[k - 2])
                        if PREP3 and k == KT1 + 2:
                            t.wait_ge(s_dve.h, P3T["v1"])
                        elif PREP3 and k == KV2 + 2:
                            t.wait_ge(s_dve.h, P3T["v2"])
                        elif PREP3 and k == KEV + 2:
                            t.wait_ge(s_dve.h, P3T["scl"])
                        last = None
                        for kt in range(9):
                            last = t.matmul(
                                psS(par)[:, kt * 128:(kt + 1) * 128],
                                Ks3[:, g, (m + kt) * 128:(m + kt + 1) * 128],
                                Qs3[:, h, m * 128:(m + 1) * 128],
                                start=True, stop=True)
                        s_pe.inc(last, 1)
                        if k >= 1:
                            parp = (k - 1) % 2
                            t.wait_ge(s_dve.h, dve_onrm[k - 1])
                            s_pe.inc(t.transpose(
                                psOT(parp), Onrm[:, parp * 128:(parp + 1) * 128],
                                ident[:]), 1)
                        t.wait_ge(s_dve.h, dve_mask[k])
                        for kt in range(9):
                            pt = PTb[:, par * 1152 + kt * 128: par * 1152 + (kt + 1) * 128]
                            t.matmul(psO(par), pt,
                                     Vs3[:, m + kt, g * 128:(g + 1) * 128],
                                     start=(kt == 0), stop=(kt == 8))
                            last = t.matmul(psSum(par), pt, ones[:, 0:1],
                                            start=(kt == 0), stop=(kt == 8))
                        s_pe.inc(last, 1)
                        if PREP3:
                            if k == KT1:
                                t.wait_ge(s_dve.h, dve_pair[63])
                                t.wait_ge(s_dve.h, dve_recip[KT1])
                                s_pe.inc(t.transpose(psA[0:2, 1280:1408],
                                                     smax[:, 2:4], identf[:]), 1)
                            elif k == KT2:
                                t.wait_ge(s_dve.h, P3T["v1"])
                                t.wait_ge(s_dve.h, dve_recip[KT2 - 1])
                                s_pe.inc(t.transpose(psB[0:1, 1280:1282],
                                                     lncol[0:2, 0:1],
                                                     identf[0:2, 0:2]), 1)
                            elif k == KBC:
                                t.wait_ge(s_dve.h, P3T["chain"])
                                s_pe.inc(t.matmul(
                                    psB[:, 1280:1284],
                                    identf[0:1, 0:1].broadcast_to((1, 128)),
                                    srow[0:1, 32:36],
                                    start=True, stop=True), 1)
                t.wait_ge(s_dve.h, dve_onrm[127])
                s_pe.inc(t.transpose(
                    psOT(1), Onrm[:, 128:256], ident[:]), 1)

            @block.scalar
            def _(sc):
                for m in range(NT):
                    for h in range(HEADS):
                        k = m * HEADS + h
                        par = k % 2
                        sc.wait_ge(s_pe.h, peQK(k))
                        s_act.inc(sc.activation(
                            PTb[:, par * 1152: par * 1152 + 1152],
                            psS(par), AF.Exp, bias=cns[:, 0:1], scale=SCALE_QK), 1)

            esC2.close()

        if N_PHASES >= 4:
            o_sb = SB(esD, "o_sb", [128, 2 * 1024], F32)
            wstg = Vs[:, 0:2 * 1024]
            wpu8 = Vs[:].bitcast(FP8)[:, 8192:8192 + 2 * 512]
            wpcl = Vs[:].bitcast(FP8)[:, 9216:9216 + 2 * 512]
            wpd = Vs[:, 5120:5120 + 512]
            PREP4 = N_PHASES >= 5
            SL = 8
            pe4, act4, dve4 = s_pe.n, s_act.n, s_dve.n
            wo_t0 = [s_wq0.n, s_wq1.n]
            sldw = s_txs[1]
            sld4_0 = [s.n for s in sldw]
            spub4 = [s_wos[0], s_wos[2]]
            pub4_0 = [s.n for s in spub4]

            peW = {}; peT = {}; peSB = {}
            c = pe4 + 4
            for mo in range(16):
                c += 1; peW[mo] = c
                for b in range(2):
                    c += 1; peT[(mo, b)] = c
                if PREP4 and mo >= 1:
                    for b in range(4):
                        c += 1; peSB[4 * (mo - 1) + b] = c
            if PREP4:
                for b in range(4):
                    c += 1; peSB[60 + b] = c
            dveE = {}; dveA = {}; dveEV = {}; dveU = {}; dveQ = {}
            c = dve4 + 32
            for mo in range(16):
                c += 1; dveE[mo] = c
                for b in range(2):
                    c += 1; dveA[(mo, b)] = c
                if PREP4:
                    if mo >= 1:
                        for b in range(4):
                            c += 1; dveEV[4 * (mo - 1) + b] = c
                    for i in range(SL):
                        s = mo * SL + i
                        c += 1; dveU[s] = c
                        c += 3; dveQ[s] = c
            if PREP4:
                for b in range(4):
                    c += 1; dveEV[60 + b] = c
            actT = {}; actB = {}
            c = act4
            if PREP4:
                for mo in range(16):
                    for i in range(SL):
                        s = mo * SL + i
                        c += 1; actT[s] = c
                        c += 1; actB[s] = c

            def pubSB(g):
                return pub4_0[g % 2] + 32 * (g // 2)

            PSLB = [(psB, 3072), (psA, 3072)]

            def PSL(sl):
                pt, off = PSLB[sl]
                return pt.bitcast(BF16)[:, off:off + 1024]

            @block.sync
            def _(sync):
                sync.wait_ge(s_pe.h, pe4)
                sync.wait_ge(s_dve.h, dve4)
                sync.wait_ge(s_act.h, act4)
                s_ld.inc(sync.dma_start(out=o_sb[0:1, 0:E], in_=bo_d), 16)
                if PREP4:
                    wfc_r = wfc_half.rearrange("(a p) (c q) -> p a c q",
                                               p=128, q=512)
                    parH = nc.snap(nc.partition_id([mybir.EngineType.SP]) % 2,
                                   engines=[mybir.EngineType.SP],
                                   min_val=0, max_val=1)

                    def wfc_copy(a4):
                        sync.wait_ge(spub4[0].h, spub4[0].n)
                        sync.wait_ge(spub4[1].h, spub4[1].n)
                        sync.wait_ge(s_wt.h, s_wt.n)
                        s_wt.inc(sync.dma_start(
                            out=wfcT_q[bass.ds(parH, 1), a4 * 2:(a4 + 1) * 2, :, :],
                            in_=wfcT_loc[a4 * 2:(a4 + 1) * 2]), 16)

                    def pubs(mo_sl):
                        for b in range(4):
                            g = 4 * mo_sl + b
                            st = 2 * mo_sl + b // 2
                            cq, r0 = divmod(st * 4, 32)
                            sync.wait_ge(s_dve.h, dveEV[g])
                            for t2 in range(2):
                                dc = cq * 4 + 2 * (b % 2) + t2
                                spub4[g % 2].inc(sync.dma_start(
                                    out=wfcT_loc[r0 // 4][:, dc * 512:(dc + 1) * 512],
                                    in_=wstg[:, (g % 2) * 1024 + t2 * 512:
                                             (g % 2) * 1024 + (t2 + 1) * 512]), 16)

                    for mo in range(16):
                        if mo >= 1:
                            pubs(mo - 1)
                            if mo >= 13:
                                wfc_copy(mo - 13)
                        for i in range(SL):
                            s = mo * SL + i
                            cq, rb = divmod(s, 32)
                            if s >= 3:
                                sync.wait_ge(s_act.h, actT[s - 3])
                            sldw[s % 3].inc(sync.dma_start(
                                out=wpin[:, (s % 3) * 512:(s % 3 + 1) * 512],
                                in_=wfc_r[:, rb, cq, :]), 16)
                    pubs(15)
                    wfc_copy(3)

            @block.tensor
            def _(t):
                t.wait_ge(s_ld.h, s_ld.n)
                t.wait_ge(s_dve.h, dve4)
                for j in range(4):
                    if j >= 1:
                        t.wait_ge(s_dve.h, dve4 + 8 * j)
                    s_pe.inc(t.matmul(psA[:, 1024:1536],
                                      identf[0:1, 0:1].broadcast_to((1, 128)),
                                      o_sb[0:1, j * 512:(j + 1) * 512],
                                      start=True, stop=True), 1)

            @block.vector
            def _(v):
                for j in range(4):
                    v.wait_ge(s_pe.h, pe4 + j + 1)
                    for tt in range(NT):
                        s_dve.inc(v.tensor_tensor(
                            xres3[:, tt, j * 512:(j + 1) * 512],
                            xres3[:, tt, j * 512:(j + 1) * 512],
                            psA[:, 1024:1536], ALU.add), 1)

            @block.gpsimd
            def _(g):
                g.wait_ge(s_pe.h, pe4)
                for mo in range(HEADS):
                    slot = mo % 2
                    if mo >= 2:
                        g.wait_ge(s_pe.h, peW[mo - 2])
                    swq[slot].inc(g.dma_start(
                        out=wb[:, slot * 2048: slot * 2048 + 2048].rearrange(
                            "p (mi c) -> p mi c", c=128),
                        in_=Wo_d.rearrange("(mi e) m -> e mi m", e=128)[
                            :, :, mo * 128:(mo + 1) * 128]), 16)

            def psT(slot):
                return (psA if slot == 0 else psB)[:, 1024:1536]

            def emit_sb(t, mo_sl):
                for b in range(4):
                    g = 4 * mo_sl + b
                    st = 2 * mo_sl + b // 2
                    s0 = st * 4
                    t.wait_ge(s_dve.h, dveQ[s0 + 3])
                    if g >= 2:
                        t.wait_ge(s_dve.h, dveEV[g - 2])
                    sl = PSL(g % 2)
                    last = None
                    for t2 in range(2):
                        tq = 2 * (b % 2) + t2
                        for i in range(4):
                            s = s0 + i
                            last = t.transpose(
                                sl[:, (t2 * 4 + i) * 128:(t2 * 4 + i + 1) * 128],
                                wpq[:, (s % 8) * 512 + tq * 128:
                                    (s % 8) * 512 + (tq + 1) * 128],
                                ident[:])
                    s_pe.inc(last, 1)

            @block.tensor
            def _(t):
                for mo in range(HEADS):
                    par = mo % 2
                    psW = (psA if par == 0 else psB)[:, 0:1024]
                    t.wait_ge(swq[par].h, wo_t0[par] + (mo // 2 + 1) * 16)
                    if mo >= 2:
                        t.wait_ge(s_dve.h, dveE[mo - 2])
                    last = None
                    for mi in range(HEADS):
                        for nq in range(2):
                            last = t.matmul(
                                psW[:, nq * 512:(nq + 1) * 512],
                                wb[:, par * 2048 + mi * 128: par * 2048 + (mi + 1) * 128],
                                attO3[:, mi, nq * 512:(nq + 1) * 512],
                                start=(mi == 0), stop=(mi == HEADS - 1))
                    s_pe.inc(last, 1)
                    t.wait_ge(s_dve.h, dveE[mo])
                    for b in range(2):
                        if mo >= 1:
                            t.wait_ge(s_dve.h, dveA[(mo - 1, b)])
                        last = None
                        for i in range(4):
                            tt = b * 4 + i
                            last = t.transpose(
                                psT(b)[:, i * 128:(i + 1) * 128],
                                o_sb[:, par * 1024 + tt * 128: par * 1024 + tt * 128 + 128],
                                identf[:])
                        s_pe.inc(last, 1)
                    if PREP4 and mo >= 1:
                        emit_sb(t, mo - 1)
                if PREP4:
                    emit_sb(t, 15)

            @block.vector
            def _(v):
                v.wait_ge(s_dve.h, dve4 + 32)
                for mo in range(HEADS):
                    par = mo % 2
                    psW = (psA if par == 0 else psB)[:, 0:1024]
                    v.wait_ge(s_pe.h, peW[mo])
                    s_dve.inc(v.tensor_copy(o_sb[:, par * 1024:(par + 1) * 1024], psW), 1)
                    for b in range(2):
                        v.wait_ge(s_pe.h, peT[(mo, b)])
                        s_dve.inc(v.tensor_tensor(
                            xres3[:, b * 4:(b + 1) * 4, mo * 128:(mo + 1) * 128],
                            xres3[:, b * 4:(b + 1) * 4, mo * 128:(mo + 1) * 128],
                            psT(b).rearrange("p (t e) -> p t e", e=128), ALU.add), 1)
                    if PREP4:
                        if mo >= 1:
                            for b in range(4):
                                g = 4 * (mo - 1) + b
                                v.wait_ge(s_pe.h, peSB[g])
                                if g >= 2:
                                    v.wait_ge(spub4[g % 2].h, pubSB(g))
                                v.wait_ge(s_dve.h, s_dve.n)
                                s_dve.inc(v.tensor_copy(
                                    wstg[:, (g % 2) * 1024:(g % 2 + 1) * 1024],
                                    PSL(g % 2)), 1)
                        for i in range(SL):
                            s = mo * SL + i
                            v.wait_ge(s_act.h, actB[s])
                            if s >= 8:
                                v.wait_ge(s_pe.h, peSB[2 * ((s - 8) // 4) + 1])
                            s_dve.inc(v.tensor_scalar(
                                wpu8[:, (s % 2) * 512:(s % 2 + 1) * 512],
                                wpt32[:, (s % 4) * 512:(s % 4 + 1) * 512],
                                240.0, -240.0, ALU.min, ALU.max), 1)
                            s_dve.inc(v.tensor_scalar(
                                wpcl[:, (s % 2) * 512:(s % 2 + 1) * 512],
                                wpb8[:, (s % 4) * 512:(s % 4 + 1) * 512],
                                120.0, -120.0, ALU.min, ALU.max), 1)
                            v.wait_ge(s_dve.h, s_dve.n)
                            s_dve.inc(v.tensor_tensor(
                                wpd[:, 0:512], wpb8[:, (s % 4) * 512:(s % 4 + 1) * 512],
                                wpcl[:, (s % 2) * 512:(s % 2 + 1) * 512], ALU.subtract), 1)
                            v.wait_ge(s_dve.h, s_dve.n)
                            s_dve.inc(v.scalar_tensor_tensor(
                                wpq[:, (s % 8) * 512:(s % 8 + 1) * 512],
                                wpd[:, 0:512], 2.0,
                                wpu8[:, (s % 2) * 512:(s % 2 + 1) * 512],
                                ALU.mult, ALU.add), 1)
                if PREP4:
                    for b in range(4):
                        g = 60 + b
                        v.wait_ge(s_pe.h, peSB[g])
                        v.wait_ge(spub4[g % 2].h, pubSB(g))
                        v.wait_ge(s_dve.h, s_dve.n)
                        s_dve.inc(v.tensor_copy(
                            wstg[:, (g % 2) * 1024:(g % 2 + 1) * 1024],
                            PSL(g % 2)), 1)

            if PREP4:
                @block.scalar
                def _(sc):
                    for mo in range(16):
                        for i in range(SL):
                            s = mo * SL + i
                            sc.wait_ge(sldw[s % 3].h,
                                       sld4_0[s % 3] + (s // 3 + 1) * 16)
                            if s == 0:
                                sc.wait_ge(s_dve.h, P3T["scl"])
                            if s >= 4:
                                sc.wait_ge(s_dve.h, dveU[s - 4])
                            s_act.inc(sc.activation(
                                wpt32[:, (s % 4) * 512:(s % 4 + 1) * 512],
                                wpin[:, (s % 3) * 512:(s % 3 + 1) * 512],
                                AF.Identity, bias=0.0, scale=scl[:, 4:5]), 1)
                            if s >= 4:
                                sc.wait_ge(s_dve.h, dveU[s - 4] + 2)
                            sc.wait_ge(s_act.h, s_act.n)
                            s_act.inc(sc.activation(
                                wpb8[:, (s % 4) * 512:(s % 4 + 1) * 512],
                                wpt32[:, (s % 4) * 512:(s % 4 + 1) * 512],
                                AF.Copy, scale=0.5), 1)

            actL0, dveL0 = s_act.n, s_dve.n
            INV = 1.0 / E

            @block.scalar
            def _(sc):
                sc.wait_ge(s_dve.h, dveA[(15, 1)])
                for tt in range(NT):
                    s_act.inc(sc.activation(o_sb[:, 0:E], xres3[:, tt, :], AF.Square,
                                            accum_out=lncol[:, 0:1]), 1)
                    sc.wait_ge(s_dve.h, dveL0 + 9 * tt + 4)
                    s_act.inc(sc.activation(lncol[:, 5:6], lncol[:, 4:5],
                                            AF.Sqrt, bias=cns[:, 1:2]), 1)

            @block.vector
            def _(v):
                def dr():
                    v.wait_ge(s_dve.h, s_dve.n)
                for tt in range(NT):
                    v.wait_ge(s_act.h, actL0 + 2 * tt + 1)
                    dr()
                    s_dve.inc(v.reduce_sum(lncol[:, 1:2], xres3[:, tt, :], AX.X), 1)
                    dr()
                    s_dve.inc(v.tensor_scalar(lncol[:, 2:3], lncol[:, 1:2], INV,
                                              None, ALU.mult), 1)
                    dr()
                    s_dve.inc(v.tensor_tensor(lncol[:, 3:4], lncol[:, 2:3],
                                              lncol[:, 2:3], ALU.mult), 1)
                    dr()
                    s_dve.inc(v.scalar_tensor_tensor(lncol[:, 4:5], lncol[:, 0:1], INV,
                                                     lncol[:, 3:4], ALU.mult,
                                                     ALU.subtract), 1)
                    v.wait_ge(s_act.h, actL0 + 2 * tt + 2)
                    s_dve.inc(v.reciprocal(lncol[:, 6:7], lncol[:, 5:6]), 1)
                    dr()
                    s_dve.inc(v.tensor_scalar(lncol[:, 7:8], lncol[:, 2:3],
                                              lncol[:, 6:7], -1.0, ALU.mult,
                                              ALU.mult), 1)
                    dr()
                    s_dve.inc(v.tensor_scalar(xres3[:, tt, :], xres3[:, tt, :],
                                              lncol[:, 6:7], lncol[:, 7:8],
                                              ALU.mult, ALU.add), 1)
                    dr()
                    s_dve.inc(v.reduce_max(lncol[:, 3:4], xres3[:, tt, :], AX.X,
                                           apply_absolute_value=True), 1)
                    dr()
                    s_dve.inc(v.tensor_max(smax[:, 0:1], smax[:, 0:1],
                                           lncol[:, 3:4]), 1)


        if N_PHASES >= 5:
            def row_from_col(cols_lo, cols_n, dst_lo):
                @block.tensor
                def _(t):
                    t.wait_ge(s_dve.h, s_dve.n)
                    s_pe.inc(t.transpose(psA[0:cols_n, 0:128],
                                         smax[:, cols_lo:cols_lo + cols_n], identf[:]), 1)
                @block.vector
                def _(v):
                    v.wait_ge(s_pe.h, s_pe.n)
                    s_dve.inc(v.reduce_max(lncol[0:cols_n, 0:1],
                                           psA[0:cols_n, 0:128], AX.X), 1)
                @block.tensor
                def _(t):
                    t.wait_ge(s_dve.h, s_dve.n)
                    s_pe.inc(t.transpose(psB[0:1, 0:cols_n],
                                         lncol[0:cols_n, 0:1],
                                         identf[0:cols_n, 0:cols_n]), 1)
                @block.vector
                def _(v):
                    v.wait_ge(s_pe.h, s_pe.n)
                    v.wait_ge(s_dve.h, s_dve.n)
                    s_dve.inc(v.tensor_copy(srow[0:1, dst_lo:dst_lo + cols_n],
                                            psB[0:1, 0:cols_n]), 1)

            def scales_from_row(src_lo, n, scl_lo, cc, pre=None):
                @block.gpsimd
                def _(g):
                    if pre is not None:
                        pre(g)
                    g.wait_ge(s_dve.h, s_dve.n)
                    s_ccd.inc(g.dma_start(out=cc, in_=srow[0:1, 0:16]), 16)
                    g.wait_ge(s_ccd.h, s_ccd.n)
                    s_cc.inc(g.collective_compute(
                        "AllReduce", ALU.max, rg, [cc], [cc],
                        cc_dim="Partition", unique_tensors="No"), 1)
                    g.wait_ge(s_cc.h, s_cc.n)
                    s_ccd.inc(g.dma_start(out=srow[0:1, 16:32], in_=cc), 16)
                @block.vector
                def _(v):
                    v.wait_ge(s_ccd.h, s_ccd.n)
                    for i in range(n):
                        v.wait_ge(s_dve.h, s_dve.n)
                        s_dve.inc(v.tensor_scalar(srow[0:1, 32 + 2 * i:33 + 2 * i],
                                                  srow[0:1, 16 + src_lo + i:17 + src_lo + i],
                                                  1.0 / 448.0, 1e-6, ALU.mult, ALU.max), 1)
                        v.wait_ge(s_dve.h, s_dve.n)
                        s_dve.inc(v.reciprocal(srow[0:1, 32 + 2 * i:33 + 2 * i],
                                               srow[0:1, 32 + 2 * i:33 + 2 * i]), 1)
                        v.wait_ge(s_dve.h, s_dve.n)
                        s_dve.inc(v.tensor_scalar(srow[0:1, 33 + 2 * i:34 + 2 * i],
                                                  srow[0:1, 32 + 2 * i:33 + 2 * i],
                                                  0.5, None, ALU.mult), 1)
                @block.tensor
                def _(t):
                    t.wait_ge(s_dve.h, s_dve.n)
                    s_pe.inc(t.matmul(psB[:, 256:256 + 2 * n],
                                      identf[0:1, 0:1].broadcast_to((1, 128)),
                                      srow[0:1, 32:32 + 2 * n],
                                      start=True, stop=True), 1)
                @block.vector
                def _(v):
                    v.wait_ge(s_pe.h, s_pe.n)
                    s_dve.inc(v.tensor_copy(scl[:, scl_lo:scl_lo + 2 * n],
                                            psB[:, 256:256 + 2 * n]), 1)

        if N_PHASES >= 5:
            esD.close()
            esC.close()
            esA.close()
            x8tok = SB(esE, "x8tok", [128, NT * E], BF16)
            x8f = SB(esE, "x8f", [128, NE * TOWN], BF16)
            xq32 = SB(esE, "xq32", [128, E], F32)
            q8u = SB(esE, "q8u", [128, E], FP8)
            q8p = SB(esE, "q8p", [128, E], FP8)
            q8n = SB(esE, "q8n", [128, E], FP8)
            s1f = SB(esE, "s1f", [128, E], F32)
            wfl = SB(esE, "wfl", [128, 2 * 16 * 512], BF16)
            rl = SB(esE, "rl", [128, 2 * 1024], F32)
            h_sb = SB(esE, "h_sb", [128, 4 * 1024], F32)
            x8t3 = x8tok[:].rearrange("p (t e) -> p t e", e=E)
            x8f3 = x8f[:].rearrange("p (ec t) -> p ec t", t=TOWN)
            wfl3 = wfl[:].rearrange("p (s a j) -> p s a j", s=2, a=NE)

            row_from_col(0, 1, 0)
            scales_from_row(0, 1, 0, cc_x,
                            pre=lambda g: g.wait_ge(s_wt.h, s_wt.n))

            @block.vector
            def _(v):
                v.wait_ge(s_dve.h, s_dve.n)
                s_dve.inc(v.tensor_tensor(scl[:, 5:6], scl[:, 0:1], scl[:, 4:5],
                                          ALU.mult), 1)
                v.wait_ge(s_dve.h, s_dve.n)
                s_dve.inc(v.reciprocal(scl[:, 5:6], scl[:, 5:6]), 1)

            act6, dve6 = s_act.n, s_dve.n

            @block.scalar
            def _(sc):
                sc.wait_ge(s_dve.h, dve6)
                for tt in range(NT):
                    if tt >= 1:
                        sc.wait_ge(s_dve.h, dve6 + 5 * tt)
                    sc.wait_ge(s_act.h, s_act.n)
                    s_act.inc(sc.activation(xq32[:], xres3[:, tt, :], AF.Identity,
                                            bias=0.0, scale=scl[:, 0:1]), 1)

            @block.vector
            def _(v):
                v.wait_ge(s_dve.h, dve6)
                for tt in range(NT):
                    v.wait_ge(s_act.h, act6 + tt + 1)
                    s_dve.inc(v.tensor_scalar(q8u[:], xq32[:], 240.0, -240.0,
                                              ALU.min, ALU.max), 1)
                    s_dve.inc(v.tensor_scalar(q8p[:], xq32[:], 0.5, 120.0,
                                              ALU.mult, ALU.max), 1)
                    s_dve.inc(v.tensor_scalar(q8n[:], xq32[:], 0.5, -120.0,
                                              ALU.mult, ALU.min), 1)
                    v.wait_ge(s_dve.h, s_dve.n)
                    s_dve.inc(v.tensor_tensor(s1f[:], q8p[:], q8n[:], ALU.add), 1)
                    v.wait_ge(s_dve.h, s_dve.n)
                    s_dve.inc(v.scalar_tensor_tensor(x8t3[:, tt, :], s1f[:], 2.0,
                                                     q8u[:], ALU.mult, ALU.add), 1)

            pe6, dve6b = s_pe.n, s_dve.n

            @block.tensor
            def _(t):
                t.wait_ge(s_dve.h, dve6b)
                for u in range(2 * NT):
                    tt, gi = divmod(u, 2)
                    ps = (psA if u % 2 == 0 else psB).bitcast(BF16)
                    if u >= 2:
                        t.wait_ge(s_dve.h, dve6b + u - 1)
                    last = None
                    for j in range(8):
                        ec = gi * 8 + j
                        last = t.transpose(ps[:, j * 128:(j + 1) * 128],
                                           x8t3[:, tt, ec * 128:(ec + 1) * 128],
                                           ident[:])
                    s_pe.inc(last, 1)

            @block.vector
            def _(v):
                for u in range(2 * NT):
                    tt, gi = divmod(u, 2)
                    ps = (psA if u % 2 == 0 else psB).bitcast(BF16)
                    v.wait_ge(s_pe.h, pe6 + u + 1)
                    s_dve.inc(v.tensor_copy(
                        x8f3[:, gi * 8:(gi + 1) * 8, tt * 128:(tt + 1) * 128],
                        ps[:, 0:1024].rearrange("p (j c) -> p j c", c=128)), 1)

        h_spill3 = h_spill.rearrange("a p t -> a p t")

        MG = {}
        MSLB = [(0, 3072), (1, 3072), (0, 3584), (1, 3584)]

        def MSL(sl):
            pt, off = MSLB[sl]
            return (psA if pt == 0 else psB).bitcast(BF16)[:, off:off + 512]

        def PSH(sl):
            pt, off = [(0, 0), (1, 0), (0, 512), (1, 512)][sl]
            return (psA if pt == 0 else psB)[:, off:off + 512]

        def phase_mm1(hf):
            pe7, act7, dve7 = s_pe.n, s_act.n, s_dve.n
            wq_t0 = [s_wf0.n, s_wf1.n]
            sp_t0 = [s.n for s in ssp]
            t0 = hf * 512
            sldm = [s_txs[0][2], s_txs[0][3], smk[0], smk[1]]
            if hf == 0:
                MG["ld0"] = [s.n for s in sldm]
                MG["pub0"] = [s_wos[1].n, s_wos[3].n]
                MG["t"] = {}; MG["b"] = {}; MG["u"] = {}; MG["q"] = {}
                MG["tr"] = {}; MG["ev"] = {}
            mt, mb, mu, mq = MG["t"], MG["b"], MG["u"], MG["q"]
            mtr, mev = MG["tr"], MG["ev"]
            wpq6 = x8tok[:, 512:512 + 8 * 512]
            wd6 = x8tok[:, 0:512]
            mstg = x8tok[:, 4608:4608 + 2 * 512]

            pe_mm = {}; act_rl = {}
            c = pe7
            for jc in range(NJ):
                c += 1; pe_mm[jc] = c
                s_t = hf * 64 + jc - 4
                if s_t >= 0:
                    c += 1; mtr[s_t] = c
            if hf == 1:
                for s_t in range(124, 128):
                    c += 1; mtr[s_t] = c
            dve_rm = {}
            c = dve7
            for jc in range(NJ):
                s = hf * 64 + jc
                c += 1; dve_rm[jc] = c
                c += 1
                s_t = hf * 64 + jc - 4
                if s_t >= 0:
                    c += 1; mev[s_t] = c
                c += 1; mu[s] = c
                c += 3; mq[s] = c
            if hf == 1:
                for s_t in range(124, 128):
                    c += 1; mev[s_t] = c
            act_sq = {}
            c = act7
            for jc in range(NJ):
                s = hf * 64 + jc
                c += 1; act_rl[jc] = c
                c += 1; act_sq[jc] = c
                c += 1; mt[s] = c
                c += 1; mb[s] = c

            spubM = [s_wos[1], s_wos[3]]

            def mpub(s_t):
                return MG["pub0"][s_t % 2] + 16 * (s_t // 2)

            @block.sync
            def _(sync):
                sync.wait_ge(s_cc.h, s_cc.n)
                sync.wait_ge(s_dve.h, dve7)
                wpr = wproj_half.rearrange("(a p) (c q) -> p a c q", p=128, q=512)
                par32b = nc.snap(nc.partition_id([mybir.EngineType.SP]) % 2 * 32,
                                 engines=[mybir.EngineType.SP],
                                 min_val=0, max_val=32)

                def wproj_copy(a4):
                    sync.wait_ge(spubM[0].h, spubM[0].n)
                    sync.wait_ge(spubM[1].h, spubM[1].n)
                    sync.wait_ge(s_wt.h, s_wt.n)
                    s_wt.inc(sync.dma_start(
                        out=wprojT_q[a4 * 4:(a4 + 1) * 4, :,
                                     bass.ds(par32b, 32), :],
                        in_=wprojT_loc[a4 * 4:(a4 + 1) * 4]), 16)

                def pub(s_t):
                    ebp, jqp = divmod(s_t, 8)
                    sync.wait_ge(s_dve.h, mev[s_t])
                    spubM[s_t % 2].inc(sync.dma_start(
                        out=wprojT_loc[ebp][:, jqp * 4:(jqp + 1) * 4, :],
                        in_=mstg[:, (s_t % 2) * 512:(s_t % 2 + 1) * 512].rearrange(
                            "p (i j) -> p i j", j=128)), 16)

                for jc in range(NJ):
                    s = hf * 64 + jc
                    if s >= 4:
                        sync.wait_ge(s_act.h, mt[s - 4])
                    eb, jq = divmod(s, 8)
                    sldm[s % 4].inc(sync.dma_start(
                        out=s1f[:, (s % 4) * 512:(s % 4 + 1) * 512],
                        in_=wpr[:, eb, jq, :]), 16)
                    s_t = hf * 64 + jc - 4
                    if s_t >= 0:
                        pub(s_t)
                        if s_t % 32 == 31:
                            wproj_copy(s_t // 32)
                if hf == 1:
                    for s_t in range(124, 128):
                        pub(s_t)
                    wproj_copy(3)

            @block.tensor
            def _(t):
                t.wait_ge(s_dve.h, dve7)

                def tr(s_t):
                    t.wait_ge(s_dve.h, mq[s_t])
                    if s_t >= 2:
                        t.wait_ge(s_dve.h, mev[s_t - 2])
                    sl = MSL(s_t % 4)
                    last = None
                    for tq in range(4):
                        last = t.transpose(
                            sl[:, tq * 128:(tq + 1) * 128],
                            wpq6[:, (s_t % 8) * 512 + tq * 128:
                                 (s_t % 8) * 512 + (tq + 1) * 128],
                            ident[:])
                    s_pe.inc(last, 1)

                for jc in range(NJ):
                    slot = (jc // 4) % 2
                    psH = PSH(jc % 4)
                    t.wait_ge(swf[slot].h, wq_t0[slot] + (jc // 8 + 1) * 16)
                    if jc >= 4:
                        t.wait_ge(s_act.h, act_rl[jc - 4])
                    last = None
                    for dc in range(NE):
                        last = t.matmul(psH,
                                        wfl3[:, slot, dc,
                                             (jc % 4) * 128:(jc % 4) * 128 + 128],
                                        x8f3[:, dc, t0:t0 + 512],
                                        start=(dc == 0), stop=(dc == NE - 1))
                    s_pe.inc(last, 1)
                    s_t = hf * 64 + jc - 4
                    if s_t >= 0:
                        tr(s_t)
                if hf == 1:
                    for s_t in range(124, 128):
                        tr(s_t)

            @block.scalar
            def _(sc):
                for jc in range(NJ):
                    s = hf * 64 + jc
                    sl4 = jc % 4
                    sc.wait_ge(s_pe.h, pe_mm[jc])
                    s_act.inc(sc.activation(rl[:, (jc % 4) * 512:(jc % 4) * 512 + 512],
                                            PSH(jc % 4),
                                            AF.Relu, scale=scl[:, 5:6]), 1)
                    sc.wait_ge(ssp[sl4].h, sp_t0[sl4] + (jc // 4) * 16)
                    if jc >= 4:
                        sc.wait_ge(s_dve.h, dve_rm[jc - 4])
                    sc.wait_ge(s_act.h, s_act.n)
                    s_act.inc(sc.activation(h_sb[:, sl4 * 1024: sl4 * 1024 + 512],
                                            rl[:, (jc % 4) * 512:(jc % 4) * 512 + 512],
                                            AF.Square), 1)
                    sc.wait_ge(sldm[s % 4].h, MG["ld0"][s % 4] + (s // 4 + 1) * 16)
                    if s >= 4:
                        sc.wait_ge(s_dve.h, mu[s - 4])
                    s_act.inc(sc.activation(
                        xq32[:, (s % 4) * 512:(s % 4 + 1) * 512],
                        s1f[:, (s % 4) * 512:(s % 4 + 1) * 512],
                        AF.Identity, bias=0.0, scale=scl[:, 6:7]), 1)
                    if s >= 4:
                        sc.wait_ge(s_dve.h, mu[s - 4] + 2)
                    sc.wait_ge(s_act.h, s_act.n)
                    s_act.inc(sc.activation(
                        q8p[:, (s % 4) * 512:(s % 4 + 1) * 512],
                        xq32[:, (s % 4) * 512:(s % 4 + 1) * 512],
                        AF.Copy, scale=0.5), 1)

            @block.vector
            def _(v):
                for jc in range(NJ):
                    s = hf * 64 + jc
                    sl4 = jc % 4
                    v.wait_ge(s_act.h, act_sq[jc])
                    v.wait_ge(s_dve.h, s_dve.n)
                    s_dve.inc(v.reduce_max(lncol[:, 0:1],
                                           h_sb[:, sl4 * 1024: sl4 * 1024 + 512], AX.X), 1)
                    v.wait_ge(s_dve.h, s_dve.n)
                    s_dve.inc(v.tensor_max(smax[:, 1:2], smax[:, 1:2], lncol[:, 0:1]), 1)
                    s_t = hf * 64 + jc - 4
                    if s_t >= 0:
                        v.wait_ge(s_pe.h, mtr[s_t])
                        if s_t >= 2:
                            v.wait_ge(spubM[s_t % 2].h, mpub(s_t))
                        v.wait_ge(s_dve.h, s_dve.n)
                        s_dve.inc(v.tensor_copy(
                            mstg[:, (s_t % 2) * 512:(s_t % 2 + 1) * 512],
                            MSL(s_t % 4)), 1)
                    v.wait_ge(s_act.h, mb[s])
                    if s >= 8:
                        v.wait_ge(s_pe.h, mtr[s - 8])
                    s_dve.inc(v.tensor_scalar(
                        q8u[:, (s % 2) * 512:(s % 2 + 1) * 512],
                        xq32[:, (s % 4) * 512:(s % 4 + 1) * 512],
                        240.0, -240.0, ALU.min, ALU.max), 1)
                    s_dve.inc(v.tensor_scalar(
                        q8n[:, (s % 2) * 512:(s % 2 + 1) * 512],
                        q8p[:, (s % 4) * 512:(s % 4 + 1) * 512],
                        120.0, -120.0, ALU.min, ALU.max), 1)
                    v.wait_ge(s_dve.h, s_dve.n)
                    s_dve.inc(v.tensor_tensor(
                        wd6, q8p[:, (s % 4) * 512:(s % 4 + 1) * 512],
                        q8n[:, (s % 2) * 512:(s % 2 + 1) * 512], ALU.subtract), 1)
                    v.wait_ge(s_dve.h, s_dve.n)
                    s_dve.inc(v.scalar_tensor_tensor(
                        wpq6[:, (s % 8) * 512:(s % 8 + 1) * 512],
                        wd6, 2.0, q8u[:, (s % 2) * 512:(s % 2 + 1) * 512],
                        ALU.mult, ALU.add), 1)
                if hf == 1:
                    for s_t in range(124, 128):
                        v.wait_ge(s_pe.h, mtr[s_t])
                        v.wait_ge(spubM[s_t % 2].h, mpub(s_t))
                        v.wait_ge(s_dve.h, s_dve.n)
                        s_dve.inc(v.tensor_copy(
                            mstg[:, (s_t % 2) * 512:(s_t % 2 + 1) * 512],
                            MSL(s_t % 4)), 1)

            @block.gpsimd
            def _(g):
                g.wait_ge(s_cc.h, s_cc.n)
                for jc in range(NJ):
                    sl4 = jc % 4
                    if jc % 4 == 0:
                        jq4 = jc // 4
                        slot = jq4 % 2
                        if jq4 >= 2:
                            g.wait_ge(s_pe.h, pe_mm[4 * jq4 - 5])
                        swf[slot].inc(g.dma_start(
                            out=wfl3[:, slot, :, :],
                            in_=wfcT_q[jq4 // 8, jq4 % 8].rearrange(
                                "p (a j) -> p a j", j=512)), 16)
                    g.wait_ge(s_act.h, act_sq[jc])
                    ssp[sl4].inc(g.dma_start(
                        out=h_spill3[jc, :, t0:t0 + 512],
                        in_=h_sb[:, sl4 * 1024: sl4 * 1024 + 512]), 16)

        if N_PHASES >= 6:
            phase_mm1(0)
            phase_mm1(1)

        if N_PHASES >= 7:
            esE.close()
            h8 = SB(esF, "h8", [128, NJ * 512], BF16)
            rb = SB(esF, "rb", [128, 2 * 512], F32)
            r8u = SB(esF, "r8u", [128, 512], FP8)
            r8b = SB(esF, "r8b", [128, 512], FP8)
            rc = SB(esF, "rc", [128, 512], BF16)
            wpj = SB(esF, "wpj", [128, 2 * 64 * 128], BF16)
            o2T = SB(esF, "o2T", [128, 2 * 512], F32)
            sqd = SB(esF, "sqd", [128, E], F32)
            h83 = h8[:].rearrange("p (a t) -> p a t", t=512)
            wpj3 = wpj[:].rearrange("p (s a j) -> p s a j", s=2, a=64)

            def hbar(g):
                g.wait_ge(s_wt.h, s_wt.n)
                for s4 in range(4):
                    g.wait_ge(ssp[s4].h, ssp[s4].n)

            row_from_col(1, 1, 2)
            scales_from_row(2, 1, 2, cc_h, pre=hbar)

            @block.vector
            def _(v):
                v.wait_ge(s_dve.h, s_dve.n)
                s_dve.inc(v.tensor_tensor(scl[:, 7:8], scl[:, 2:3], scl[:, 6:7],
                                          ALU.mult), 1)
                v.wait_ge(s_dve.h, s_dve.n)
                s_dve.inc(v.reciprocal(scl[:, 7:8], scl[:, 7:8]), 1)

        def phase_h8(hf):
            dve8, pl8 = s_dve.n, s_pl.n
            rb_t0 = [srb[0].n, srb[1].n]
            t0 = hf * 512

            @block.sync
            def _(sync):
                sync.wait_ge(s_cc.h, s_cc.n)
                sync.wait_ge(s_dve.h, dve8)
                for jc in range(NJ):
                    slot = jc % 2
                    if jc >= 2:
                        sync.wait_ge(s_dve.h, dve8 + (jc - 2) * 4 + 2)
                    srb[slot].inc(sync.dma_start(
                        out=rb[:, slot * 512:(slot + 1) * 512],
                        in_=h_spill3[jc, :, t0:t0 + 512]), 16)

            @block.vector
            def _(v):
                for jc in range(NJ):
                    slot = jc % 2
                    v.wait_ge(srb[slot].h, rb_t0[slot] + (jc // 2 + 1) * 16)
                    v.wait_ge(s_dve.h, s_dve.n)
                    s_dve.inc(v.tensor_scalar(r8u[:], rb[:, slot * 512:(slot + 1) * 512],
                                              scl[:, 2:3], 240.0, ALU.mult, ALU.min), 1)
                    s_dve.inc(v.tensor_scalar(r8b[:], rb[:, slot * 512:(slot + 1) * 512],
                                              scl[:, 3:4], 120.0, ALU.mult, ALU.max), 1)
                    v.wait_ge(s_dve.h, s_dve.n)
                    s_dve.inc(v.tensor_scalar(rc[:], r8b[:], 120.0, 2.0,
                                              ALU.subtract, ALU.mult), 1)
                    v.wait_ge(s_dve.h, s_dve.n)
                    s_dve.inc(v.tensor_tensor(h83[:, jc, :], r8u[:], rc[:], ALU.add), 1)

        def phase_mm2(hf):
            pe9, act9, dve9, pl9 = s_pe.n, s_act.n, s_dve.n, s_pl.n
            wq_t0 = [s_h0.n, s_h1.n]
            t0 = hf * 512

            @block.sync
            def _(sync):
                for ec in range(NE):
                    slot = ec % 2
                    if ec >= 2:
                        sync.wait_ge(s_pe.h, pe9 + 5 * (ec - 2) + 1)
                    else:
                        sync.wait_ge(s_pe.h, pe9)
                    shw[slot].inc(sync.dma_start(
                        out=wpj3[:, slot, :, :],
                        in_=wprojT_q[ec]), 16)

            def psT2(slot):
                return (psA if slot == 0 else psB)[:, 1536:1664]

            @block.tensor
            def _(t):
                t.wait_ge(s_dve.h, dve9)
                for ec in range(NE):
                    par = ec % 2
                    psO2 = (psA if par == 0 else psB)[:, 0:512]
                    if ec >= 1:
                        t.wait_ge(s_dve.h, dve9 + 5 * (ec - 2) + 1 if ec >= 2 else dve9)
                    slot = ec % 2
                    t.wait_ge(shw[slot].h, wq_t0[slot] + (ec // 2 + 1) * 16)
                    for jc in range(NJ):
                        last = t.matmul(psO2, wpj3[:, slot, jc, :],
                                        h83[:, jc, :],
                                        start=(jc == 0), stop=(jc == NJ - 1))
                    s_pe.inc(last, 1)
                    t.wait_ge(s_dve.h, dve9 + 5 * ec + 1)
                    for tt in range(4):
                        slot = tt % 2
                        if tt >= 2:
                            t.wait_ge(s_dve.h, dve9 + 5 * ec + 2 + (tt - 2))
                        elif ec >= 1:
                            t.wait_ge(s_dve.h, dve9 + 5 * (ec - 1) + 2 + (tt + 2))
                        s_pe.inc(t.transpose(
                            psT2(slot),
                            o2T[:, (ec % 2) * 512 + tt * 128: (ec % 2) * 512 + tt * 128 + 128],
                            identf[:]), 1)

            @block.vector
            def _(v):
                for ec in range(NE):
                    par = ec % 2
                    psO2 = (psA if par == 0 else psB)[:, 0:512]
                    v.wait_ge(s_pe.h, pe9 + 5 * ec + 1)
                    s_dve.inc(v.tensor_copy(o2T[:, par * 512:(par + 1) * 512], psO2), 1)
                    for tt in range(4):
                        slot = tt % 2
                        v.wait_ge(s_pe.h, pe9 + 5 * ec + 2 + tt)
                        s_dve.inc(v.scalar_tensor_tensor(
                            xres3[:, hf * 4 + tt, ec * 128:(ec + 1) * 128],
                            psT2(slot), scl[:, 7:8],
                            xres3[:, hf * 4 + tt, ec * 128:(ec + 1) * 128],
                            ALU.mult, ALU.add), 1)

            actL, dveL = s_act.n, s_dve.n

            @block.scalar
            def _(sc):
                sc.wait_ge(s_dve.h, dveL)
                for tt4 in range(4):
                    tt = hf * 4 + tt4
                    sc.wait_ge(s_act.h, s_act.n)
                    s_act.inc(sc.activation(sqd[:], xres3[:, tt, :], AF.Square,
                                            accum_out=lncol[:, 0:1]), 1)
                    sc.wait_ge(s_dve.h, dveL + 7 * tt4 + 4)
                    s_act.inc(sc.activation(lncol[:, 5:6], lncol[:, 4:5],
                                            AF.Sqrt, bias=cns[:, 1:2]), 1)

            @block.vector
            def _(v):
                def dr():
                    v.wait_ge(s_dve.h, s_dve.n)
                for tt4 in range(4):
                    tt = hf * 4 + tt4
                    v.wait_ge(s_act.h, actL + 2 * tt4 + 1)
                    dr()
                    s_dve.inc(v.reduce_sum(lncol[:, 1:2], xres3[:, tt, :], AX.X), 1)
                    dr()
                    s_dve.inc(v.tensor_scalar(lncol[:, 2:3], lncol[:, 1:2], 1.0 / E,
                                              None, ALU.mult), 1)
                    dr()
                    s_dve.inc(v.tensor_tensor(lncol[:, 3:4], lncol[:, 2:3],
                                              lncol[:, 2:3], ALU.mult), 1)
                    dr()
                    s_dve.inc(v.scalar_tensor_tensor(lncol[:, 4:5], lncol[:, 0:1],
                                                     1.0 / E, lncol[:, 3:4],
                                                     ALU.mult, ALU.subtract), 1)
                    v.wait_ge(s_act.h, actL + 2 * tt4 + 2)
                    s_dve.inc(v.reciprocal(lncol[:, 6:7], lncol[:, 5:6]), 1)
                    dr()
                    s_dve.inc(v.tensor_scalar(lncol[:, 7:8], lncol[:, 2:3],
                                              lncol[:, 6:7], -1.0, ALU.mult,
                                              ALU.mult), 1)
                    dr()
                    s_dve.inc(v.tensor_scalar(xres3[:, tt, :], xres3[:, tt, :],
                                              lncol[:, 6:7], lncol[:, 7:8],
                                              ALU.mult, ALU.add), 1)
                    dr()

            @block.sync
            def _(sync):
                sync.wait_ge(s_dve.h, s_dve.n)
                s_st.inc(sync.dma_start(
                    out=y.rearrange("(t p) e -> p t e", p=128)[:, hf * 4:(hf + 1) * 4, :],
                    in_=xres3[:, hf * 4:(hf + 1) * 4, :]), 16)

        if N_PHASES >= 7:
            phase_h8(0)
        if N_PHASES >= 8:
            phase_mm2(0)
            phase_h8(1)
            phase_mm2(1)
            esF.close()

        if DEBUG and N_PHASES >= 2:
            @block.sync
            def _(sync):
                sync.wait_ge(s_dve.h, s_dve.n)
                sync.wait_ge(s_act.h, s_act.n)
                s_st.inc(sync.dma_start(
                    out=dbg["dbg_a"],
                    in_=(attO[:] if N_PHASES >= 3 else Qs[:])), 16)
                s_st.inc(sync.dma_start(out=dbg["dbg_b"], in_=Ks[:]), 16)
                s_st.inc(sync.dma_start(out=dbg["dbg_c"], in_=Vs[:]), 16)
                s_st.inc(sync.dma_start(out=dbg["dbg_f"], in_=xres[:]), 16)

        @block.sync
        def _(sync):
            sync.wait_ge(s_ld.h, s_ld.n)
            sync.wait_ge(s_dve.h, s_dve.n)
            sync.wait_ge(s_pe.h, s_pe.n)
            sync.wait_ge(s_act.h, s_act.n)
            sync.wait_ge(s_pl.h, s_pl.n)
            if N_PHASES < 8:
                s_st.inc(sync.dma_start(
                    out=y.rearrange("(t p) e -> p t e", p=128), in_=xres3), 16)
            sync.wait_ge(s_st.h, s_st.n)

        if N_PHASES < 5:
            esD.close()
            esC.close()
            esA.close()
        if N_PHASES < 3:
            esB.close()
        if 5 <= N_PHASES < 7:
            esE.close()
        if N_PHASES == 7:
            esF.close()

    return nc


def kernel(L0_attn_x, L0_attn_Wq, L0_attn_Wk, L0_attn_Wv, L0_attn_Wo,
           L0_attn_bo, L0_ffn_w_fc, L0_ffn_w_proj):
    x = np.ascontiguousarray(np.asarray(L0_attn_x, dtype=np.float32)[0])
    Wq = np.ascontiguousarray(L0_attn_Wq, dtype=np.float32)
    Wk = np.ascontiguousarray(L0_attn_Wk, dtype=np.float32)
    Wv = np.ascontiguousarray(L0_attn_Wv, dtype=np.float32)
    Wo = np.ascontiguousarray(L0_attn_Wo, dtype=np.float32)
    bo = np.ascontiguousarray(L0_attn_bo, dtype=np.float32).reshape(1, M)
    wfc = np.ascontiguousarray(L0_ffn_w_fc, dtype=np.float32)
    wproj = np.ascontiguousarray(L0_ffn_w_proj, dtype=np.float32)

    nc = build_nc()
    in_maps = []
    for c in range(NCORES):
        half = c % 2
        in_maps.append({
            "x_own": x[c * TOWN:(c + 1) * TOWN],
            "x_halo": (np.zeros((TOWN, E), np.float32) if c == 0
                       else np.ascontiguousarray(x[(c - 1) * TOWN:c * TOWN])),
            "Wq": Wq, "Wk": Wk, "Wv": Wv, "Wo": Wo, "bo": bo,
            "wfc_half": np.ascontiguousarray(wfc[half * (FF // 2):(half + 1) * (FF // 2)]),
            "wproj_half": np.ascontiguousarray(wproj[:, half * (FF // 2):(half + 1) * (FF // 2)]),
            "wfc_8": np.ascontiguousarray(wfc[c * (FF // 8):(c + 1) * (FF // 8)]),
            "wproj_8": np.ascontiguousarray(wproj[:, c * (FF // 8):(c + 1) * (FF // 8)]),
            "masks": _mask_host(c),
        })
    res = run_bass_kernel_spmd(nc, in_maps, core_ids=list(range(NCORES)))
    out = np.concatenate([res.results[c]["y"] for c in range(NCORES)], axis=0)
    return out.reshape(1, T, E).astype(np.float32)
